# version 02
# baseline (speedup 1.0000x reference)
import numpy as np
import concourse.bass as bass
import concourse.mybir as mybir
from concourse.alu_op_type import AluOpType as OP
from concourse.bass_utils import run_bass_kernel_spmd
from concourse.bass import IndirectOffsetOnAxis
from contextlib import ExitStack, contextmanager

F32 = mybir.dt.float32
BF16 = mybir.dt.bfloat16
U32 = mybir.dt.uint32
I32 = mybir.dt.int32
U8 = mybir.dt.uint8
AF = mybir.ActivationFunctionType
AX = mybir.AxisListType

NCORE = 8
T = 2048
D = 1024
NT = T // 128
KD = D // 128
LN_EPS = 1e-5
DEBUG_TAPS = False

GUARDED = ("vector", "scalar", "gpsimd")


class Prog:
    ENGS = ("sync", "tensor", "vector", "scalar", "gpsimd")

    def __init__(self):
        self.nc = bass.Bass("TRN2", target_bir_lowering=False)
        self.es = ExitStack()
        self.ops = {e: [] for e in self.ENGS}
        self.dbg = {}
        self.cnt = {}
        self.sems = {}
        self.gsems = {}
        self.prefix = ""
        self.es_stack = [self.es]
        for e in self.ENGS:
            self.sems[e] = self.es.enter_context(self.nc.semaphore(f"pe_{e}"))
            self.cnt[e] = 0

    @contextmanager
    def scope(self, prefix):
        old = self.prefix
        self.prefix = old + prefix
        es = ExitStack()
        self.es_stack.append(es)
        try:
            yield
        finally:
            es.close()
            self.es_stack.pop()
            self.prefix = old

    def sb(self, name, shape, dtype=F32):
        return self.es_stack[-1].enter_context(
            self.nc.sbuf_tensor(self.prefix + name, shape, dtype))

    def ps(self, name, shape, dtype=F32):
        return self.es_stack[-1].enter_context(
            self.nc.psum_tensor(self.prefix + name, shape, dtype))

    def dram(self, name, shape, dtype=F32, **kw):
        if kw.get("kind") in ("ExternalInput", "ExternalOutput"):
            return self.nc.dram_tensor(name, shape, dtype, **kw)
        return self.nc.dram_tensor(self.prefix + name, shape, dtype, **kw)

    def join(self):
        marks = {e: self.mark(e) for e in self.ENGS}
        for a in self.ENGS:
            for b in self.ENGS:
                if a != b:
                    self.wait(a, b, marks[b])
        for gname, g in list(self.gsems.items()):
            for e in self.ENGS:
                self.waitg(e, gname, g[1])

    def on(self, eng, fn):
        self.ops[eng].append(fn)

    def mark(self, eng):
        return self.cnt[eng]

    def wait(self, eng, src, v):
        if v <= 0:
            return
        self.on(eng, lambda e, s=self.sems[src], v=v: e.wait_ge(s, v))

    def do(self, eng, fn, guard=True):
        s = self.sems[eng]
        nb = self.cnt[eng]
        self.cnt[eng] = nb + 1
        if eng in GUARDED and guard and nb > 0:
            def run(e, fn=fn, s=s, nb=nb):
                e.wait_ge(s, nb)
                fn(e).then_inc(s, 1)
        else:
            def run(e, fn=fn, s=s):
                fn(e).then_inc(s, 1)
        self.on(eng, run)
        return self.cnt[eng]

    def gsem(self, name):
        if name not in self.gsems:
            h = self.es.enter_context(self.nc.semaphore(f"g_{name}"))
            self.gsems[name] = [h, 0]
        return self.gsems[name]

    def dma(self, q, out, in_, grp, **kw):
        g = self.gsem(grp)
        g[1] += 16

        def run(e, out=out, in_=in_, s=g[0], kw=kw):
            e.dma_start(out=out, in_=in_, **kw).then_inc(s, 16)
        self.on(q, run)
        return g[1]

    def waitg(self, eng, grp, v):
        g = self.gsem(grp)
        if v <= 0:
            return
        self.on(eng, lambda e, s=g[0], v=v: e.wait_ge(s, v))

    def cc(self, kind, op, ins, outs, grp="cc"):
        g = self.gsem(grp)
        g[1] += 1

        def run(e, kind=kind, op=op, ins=ins, outs=outs, s=g[0]):
            e.collective_compute(kind, op, replica_groups=[list(range(NCORE))],
                                 ins=ins, outs=outs).then_inc(s)
        self.on("gpsimd", run)
        return g[1]

    def finalize(self):
        with self.nc.Block() as blk:
            for eng in self.ENGS:
                if not self.ops[eng]:
                    continue
                getattr(blk, eng)(lambda e, ops=tuple(self.ops[eng]): [f(e) for f in ops])
        self.es.close()
        return self.nc



def make_seg_bufs(p, B):
    B["x_res"] = p.sb("x_res", [128, NT * D], F32)
    B["xn"] = p.sb("xn", [128, NT * D], F32)
    B["xnT"] = p.sb("xnT", [128, KD * T], BF16)
    B["xn_bf"] = p.sb("xn_bf", [128, D], BF16)
    return B


def make_common(p, small=False):
    B = {}
    if not small:
        B["x_res"] = p.sb("x_res", [128, NT * D], F32)
        B["xn"] = p.sb("xn", [128, NT * D], F32)
        B["xnT"] = p.sb("xnT", [128, KD * T], BF16)
        B["xn_bf"] = p.sb("xn_bf", [128, D], BF16)
    B["ident32"] = p.sb("ident32", [128, 128], F32)
    B["identbf"] = p.sb("identbf", [128, 128], BF16)
    B["ones_row"] = p.sb("ones_row", [1, 512], F32)
    B["zero_row"] = p.sb("zero_row", [1, 512], F32)
    B["ones_col"] = p.sb("ones_col", [128, 1], F32)
    B["stat"] = p.sb("stat", [128, 16], F32)
    B["sq"] = p.sb("sq", [128, D], F32)
    B["ps_w"] = p.ps("ps_w", [128, 2048], F32)
    B["ps_a"] = p.ps("ps_a", [128, 512], F32)
    B["ps_b"] = p.ps("ps_b", [128, 512], F32)
    B["ps_c"] = p.ps("ps_c", [128, 512], F32)
    B["ps_d"] = p.ps("ps_d", [128, 512], F32)
    p.do("gpsimd", lambda e: e.memset(B["ones_row"][:, :], 1.0))
    p.do("gpsimd", lambda e: e.memset(B["zero_row"][:, :], 0.0))
    p.do("gpsimd", lambda e: e.memset(B["ones_col"][:, :], 1.0))
    p.do("gpsimd", lambda e: e.iota(B["ident32"][:, :], pattern=[[0, 128]], base=0,
                                    channel_multiplier=1, allow_small_or_imprecise_dtypes=True))
    gm = p.do("gpsimd", lambda e: e.iota(B["sq"][:, 0:128], pattern=[[1, 128]], base=0,
                                         channel_multiplier=0, allow_small_or_imprecise_dtypes=True))
    p.wait("vector", "gpsimd", gm)
    p.do("vector", lambda e: e.tensor_tensor(B["ident32"][:, :], B["ident32"][:, :],
                                             B["sq"][:, 0:128], OP.is_equal))
    B["init_v"] = p.do("vector", lambda e: e.tensor_copy(B["identbf"][:, :], B["ident32"][:, :]))
    return B


def emit_load_x(p, B, x_in):
    return p.dma("sync", B["x_res"][:, :].rearrange("p (t d) -> p t d", t=NT),
                 x_in.rearrange("(t p) d -> p t d", p=128), "xio")


def emit_store_x(p, B, x_out):
    return p.dma("sync", x_out.rearrange("(t p) d -> p t d", p=128),
                 B["x_res"][:, :].rearrange("p (t d) -> p t d", t=NT), "xio")


def emit_ln(p, B):
    x, xn, xnT = B["x_res"], B["xn"], B["xnT"]
    stat, sq = B["stat"], B["sq"]
    psA = B["ps_a"].bitcast(BF16)
    psB = B["ps_b"].bitcast(BF16)
    p.wait("tensor", "vector", B["init_v"])
    tile_done = []
    copy_marks = []
    for t in range(NT):
        xt = x[:, t * D:(t + 1) * D]
        p.do("vector", lambda e, xt=xt: e.tensor_reduce(stat[:, 0:1], xt, AX.X, OP.add))
        m = p.do("vector", lambda e: e.tensor_scalar(stat[:, 1:2], stat[:, 0:1],
                                                     -1.0 / D, None, OP.mult, OP.bypass))
        p.wait("scalar", "vector", m)
        m = p.do("scalar", lambda e, xt=xt: e.activation(sq[:, :], xt, AF.Square,
                                                         bias=stat[:, 1:2], scale=1.0,
                                                         accum_out=stat[:, 2:3]))
        p.wait("vector", "scalar", m)
        m = p.do("vector", lambda e: e.tensor_scalar(stat[:, 3:4], stat[:, 2:3],
                                                     1.0 / D, LN_EPS, OP.mult, OP.add))
        p.wait("scalar", "vector", m)
        p.do("scalar", lambda e: e.activation(stat[:, 4:5], stat[:, 3:4], AF.Ln))
        m = p.do("scalar", lambda e: e.activation(stat[:, 5:6], stat[:, 4:5], AF.Exp, scale=-0.5))
        p.wait("vector", "scalar", m)
        m = p.do("vector", lambda e: e.tensor_tensor(stat[:, 6:7], stat[:, 1:2],
                                                     stat[:, 5:6], OP.mult))
        p.wait("scalar", "vector", m)
        m = p.do("scalar", lambda e, xt=xt, t=t: e.activation(
            xn[:, t * D:(t + 1) * D], xt, AF.Identity, bias=stat[:, 6:7], scale=stat[:, 5:6]))
        p.wait("vector", "scalar", m)
        m = p.do("vector", lambda e, t=t: e.tensor_copy(B["xn_bf"][:, :],
                                                        xn[:, t * D:(t + 1) * D]))
        p.wait("tensor", "vector", m)
        for c in range(KD):
            g = t * KD + c
            slot = psA if c % 2 == 0 else psB
            if g >= 2:
                p.wait("tensor", "vector", copy_marks[g - 2])
            p.do("tensor", lambda e, c=c, slot=slot: e.transpose(
                slot[:, 0:128], B["xn_bf"][:, c * 128:(c + 1) * 128], B["identbf"][:, :]))
            p.wait("vector", "tensor", p.mark("tensor"))
            m = p.do("vector", lambda e, t=t, c=c, slot=slot: e.tensor_copy(
                xnT[:, c * T + t * 128: c * T + (t + 1) * 128], slot[:, 0:128]))
            copy_marks.append(m)
        tile_done.append(p.mark("vector"))
    return tile_done


def emit_router(p, B, gwT, gb_row, n_out, gates, ln_tiles, name):
    xn = B["xn"]
    psA, psB = B["ps_a"], B["ps_b"]
    ps_r = B["ps_c"][:, 0:n_out]
    tch = p.sb(f"{name}_tch", [128, 2 * 128], F32)
    p.wait("tensor", "vector", ln_tiles[-1])
    copy_marks = []
    ev = 0
    for t in range(NT):
        if t > 0:
            p.wait("tensor", "vector", ev)
        for c in range(KD):
            g = t * KD + c
            slot, off = (psA, 0) if c % 2 == 0 else (psB, 128)
            if g >= 2:
                p.wait("tensor", "vector", copy_marks[g - 2])
            p.do("tensor", lambda e, t=t, c=c, slot=slot: e.transpose(
                slot[:, 0:128], xn[:, t * D + c * 128: t * D + (c + 1) * 128],
                B["ident32"][:, :]))
            p.wait("vector", "tensor", p.mark("tensor"))
            m = p.do("vector", lambda e, off=off, slot=slot: e.tensor_copy(
                tch[:, off:off + 128], slot[:, 0:128]))
            copy_marks.append(m)
            p.wait("tensor", "vector", m)
            p.do("tensor", lambda e, c=c, off=off: e.matmul(
                ps_r, tch[:, off:off + 128], gwT[:, c * n_out:(c + 1) * n_out],
                start=(c == 0), stop=False))
        mm = p.do("tensor", lambda e: e.matmul(ps_r, B["ones_row"][0:1, 0:128], gb_row,
                                               start=False, stop=True))
        p.wait("vector", "tensor", mm)
        ev = p.do("vector", lambda e, t=t: e.tensor_copy(
            gates[:, t * n_out:(t + 1) * n_out], ps_r))
    return ev


def emit_top2(p, B, gates, n_e, ohsum, name):
    tmp = p.sb(f"{name}_t2", [128, 2 * n_e], F32)
    m1 = B["stat"][:, 9:10]
    m2 = B["stat"][:, 10:11]
    last = 0
    for t in range(NT):
        gt = gates[:, t * n_e:(t + 1) * n_e]
        ot = ohsum[:, t * n_e:(t + 1) * n_e]
        p.do("vector", lambda e, gt=gt: e.tensor_reduce(m1, gt, AX.X, OP.max))
        p.do("vector", lambda e, gt=gt, ot=ot: e.tensor_scalar(
            ot, gt, m1, None, OP.is_ge, OP.bypass))
        p.do("vector", lambda e, ot=ot: e.tensor_scalar(
            tmp[:, 0:n_e], ot, -1e30, None, OP.mult, OP.bypass))
        p.do("vector", lambda e, gt=gt: e.tensor_tensor(
            tmp[:, 0:n_e], tmp[:, 0:n_e], gt, OP.add))
        p.do("vector", lambda e: e.tensor_reduce(m2, tmp[:, 0:n_e], AX.X, OP.max))
        p.do("vector", lambda e: e.tensor_scalar(
            tmp[:, n_e:2 * n_e], tmp[:, 0:n_e], m2, None, OP.is_ge, OP.bypass))
        last = p.do("vector", lambda e, ot=ot: e.tensor_tensor(
            ot, ot, tmp[:, n_e:2 * n_e], OP.add))
    return last


def emit_allreduce_row(p, B, row_ap, n, tag, op=OP.add, src_ready=("vector", 0)):
    din = p.dram(f"{tag}_in", [1, n], F32)
    dout = p.dram(f"{tag}_out", [1, n], F32)
    res = p.sb(f"{tag}_res", [1, n], F32)
    p.wait("gpsimd", src_ready[0], src_ready[1])
    m = p.dma("gpsimd", din[:, :], row_ap, tag)
    p.waitg("gpsimd", tag, m)
    m = p.cc("AllReduce", op, [din.ap()], [dout.ap()], grp=tag + "c")
    p.waitg("gpsimd", tag + "c", m)
    m = p.dma("gpsimd", res[0:1, :], dout[:, :], tag)
    return m, res


def bcast_row(p, B, row_ap, n, dst, src_ready):
    src, mk = src_ready
    p.wait("tensor", src, mk)
    mm = p.do("tensor", lambda e: e.matmul(B["ps_c"][:, 0:n], B["ones_row"][0:1, 0:128],
                                           row_ap, start=True, stop=True))
    p.wait("vector", "tensor", mm)
    return p.do("vector", lambda e: e.tensor_copy(dst[:, :], B["ps_c"][:, 0:n]))



def emit_l1_ffn(p, B, W, coef):
    E = 8
    ln_tiles = emit_ln(p, B)
    gwT = p.sb("l1_gwT", [128, KD * E], F32)
    gb = p.sb("l1_gb", [1, E], F32)
    g8 = p.sb("l1_g8", [E, D], F32)
    p.dma("sync", g8[:, :], W["gate_w"][:, :], "l1w")
    dm = p.dma("sync", gb[0:1, :], W["gate_b"][None, :], "l1w")
    p.waitg("tensor", "l1w", dm)
    p.wait("tensor", "vector", B["init_v"])
    for c in range(KD):
        mm = p.do("tensor", lambda e, c=c: e.matmul(
            B["ps_c"][:, 0:E], g8[0:E, c * 128:(c + 1) * 128], B["ident32"][0:E, 0:E],
            start=True, stop=True))
        p.wait("vector", "tensor", mm)
        m = p.do("vector", lambda e, c=c: e.tensor_copy(
            gwT[:, c * E:(c + 1) * E], B["ps_c"][:, 0:E]))
        p.wait("tensor", "vector", m)
    gates = p.sb("l1_gates", [128, NT * E], F32)
    ev = emit_router(p, B, gwT, gb[0:1, :], E, gates, ln_tiles, "l1")
    p.dbg["gates"] = gates
    ohsum = p.sb("l1_ohsum", [128, NT * E], F32)
    t2 = emit_top2(p, B, gates, E, ohsum, "l1")
    p.dbg["ohsum"] = ohsum

    ps_cnt = B["ps_d"][0:1, 0:E]
    p.wait("tensor", "vector", t2)
    mm = 0
    for t in range(NT):
        mm = p.do("tensor", lambda e, t=t: e.matmul(
            ps_cnt, B["ones_col"][:, 0:1], ohsum[:, t * E:(t + 1) * E],
            start=(t == 0), stop=(t == NT - 1)))
    cnt_row = p.sb("l1_cnt", [1, E], F32)
    p.wait("vector", "tensor", mm)
    m = p.do("vector", lambda e: e.tensor_copy(cnt_row[0:1, :], ps_cnt))
    arm, cnt_g = emit_allreduce_row(p, B, cnt_row[0:1, :], E, "l1cnt",
                                    src_ready=("vector", m))
    p.dbg["cnt_g"] = cnt_g
    p.waitg("vector", "l1cnt", arm)
    m = p.do("vector", lambda e: e.tensor_scalar(
        cnt_g[0:1, :], cnt_g[0:1, :], coef / (NCORE * T * 2), None, OP.mult, OP.bypass))
    scales = p.sb("l1_scales", [128, E], F32)
    sm = bcast_row(p, B, cnt_g[0:1, :], E, scales, ("vector", m))
    p.dbg["scales"] = scales

    acc = B["xn"]
    p.wait("gpsimd", "vector", ev)
    gz = p.do("gpsimd", lambda e: e.memset(acc[:, :], 0.0))

    sq = B["sq"]
    w0 = p.sb("l1_w0", [128, KD * D], BF16)
    b0 = p.sb("l1_b0", [1, D], F32)
    last_mm_of_expert = 0
    sq_consumed = 0
    evac = {}
    for e_i in range(E):
        if e_i >= 1:
            p.wait("gpsimd", "tensor", last_mm_of_expert)
        p.dma("gpsimd", w0[:, :].rearrange("p (c n) -> p c n", c=KD),
              W["expert_w"][e_i].rearrange("(c p) n -> p c n", p=128), "l1ew")
        wm = p.dma("gpsimd", b0[0:1, :], W["expert_b"][e_i:e_i + 1, :], "l1ew")
        p.waitg("tensor", "l1ew", wm)
        if e_i == 0:
            p.wait("tensor", "vector", sm)
            p.wait("tensor", "gpsimd", gz)
        for t in range(NT):
            g = e_i * NT + t
            half = (t % 2) * 1024
            ph0 = B["ps_w"][:, half:half + 512]
            ph1 = B["ps_w"][:, half + 512:half + 1024]
            if g >= 2:
                p.wait("tensor", "scalar", evac[g - 2])
            for c in range(KD):
                lhs = B["xnT"][:, c * T + t * 128: c * T + (t + 1) * 128]
                p.do("tensor", lambda e, lhs=lhs, c=c, ph0=ph0: e.matmul(
                    ph0, lhs, w0[:, c * D: c * D + 512], start=(c == 0), stop=False))
                p.do("tensor", lambda e, lhs=lhs, c=c, ph1=ph1: e.matmul(
                    ph1, lhs, w0[:, c * D + 512: (c + 1) * D], start=(c == 0), stop=False))
            p.do("tensor", lambda e, ph0=ph0: e.matmul(
                ph0, B["ones_row"][0:1, 0:128], b0[0:1, 0:512], start=False, stop=True))
            mm = p.do("tensor", lambda e, ph1=ph1: e.matmul(
                ph1, B["ones_row"][0:1, 0:128], b0[0:1, 512:1024], start=False, stop=True))
            last_mm_of_expert = mm
            p.wait("scalar", "tensor", mm)
            if g >= 1:
                p.wait("scalar", "vector", sq_consumed)
            p.do("scalar", lambda e, e_i=e_i, ph0=ph0: e.activation(
                sq[:, 0:512], ph0, AF.Relu, scale=scales[:, e_i:e_i + 1]))
            m = p.do("scalar", lambda e, e_i=e_i, ph1=ph1: e.activation(
                sq[:, 512:1024], ph1, AF.Relu, scale=scales[:, e_i:e_i + 1]))
            evac[g] = m
            p.wait("vector", "scalar", m)
            sq_consumed = p.do("vector", lambda e, t=t: e.tensor_tensor(
                acc[:, t * D:(t + 1) * D], acc[:, t * D:(t + 1) * D],
                sq[:, 0:D], OP.add))
    last = 0
    for t in range(NT):
        last = p.do("vector", lambda e, t=t: e.tensor_tensor(
            B["x_res"][:, t * D:(t + 1) * D], B["x_res"][:, t * D:(t + 1) * D],
            acc[:, t * D:(t + 1) * D], OP.add))
    return ("vector", last)




def emit_l2_ffn(p, B, W, coef):
    E = 16
    ln_tiles = emit_ln(p, B)
    gwT = p.sb("l2_gwT", [128, KD * E], F32)
    gb = p.sb("l2_gb", [1, E], F32)
    p.dma("sync", gwT[:, :].rearrange("p (c e) -> p c e", c=KD),
          W["router_w"].rearrange("(c p) e -> p c e", p=128), "l2w")
    dm = p.dma("sync", gb[0:1, :], W["router_bias"][None, :], "l2w")
    p.waitg("tensor", "l2w", dm)
    gates = p.sb("l2_gates", [128, NT * E], F32)
    ev = emit_router(p, B, gwT, gb[0:1, :], E, gates, ln_tiles, "l2")
    p.dbg["gates"] = gates

    comb = p.sb("l2_comb", [128, NT * E], F32)
    tmp = p.sb("l2_t2", [128, 2 * E], F32)
    m1 = B["stat"][:, 9:10]
    m2 = B["stat"][:, 10:11]
    s1 = B["stat"][:, 11:12]
    s2 = B["stat"][:, 12:13]
    comb_done = 0
    for t in range(NT):
        gt = gates[:, t * E:(t + 1) * E]
        ct = comb[:, t * E:(t + 1) * E]
        oh1 = tmp[:, 0:E]
        oh2 = tmp[:, E:2 * E]
        p.do("vector", lambda e, gt=gt: e.tensor_reduce(m1, gt, AX.X, OP.max))
        p.do("vector", lambda e, gt=gt, oh1=oh1: e.tensor_scalar(
            oh1, gt, m1, None, OP.is_ge, OP.bypass))
        p.do("vector", lambda e, oh1=oh1, oh2=oh2: e.tensor_scalar(
            oh2, oh1, -1e30, None, OP.mult, OP.bypass))
        p.do("vector", lambda e, gt=gt, oh2=oh2: e.tensor_tensor(
            oh2, oh2, gt, OP.add))
        p.do("vector", lambda e, oh2=oh2: e.tensor_reduce(m2, oh2, AX.X, OP.max))
        m = p.do("vector", lambda e, oh2=oh2: e.tensor_scalar(
            oh2, oh2, m2, None, OP.is_ge, OP.bypass))
        p.do("vector", lambda e: e.tensor_tensor(s1, m1, m2, OP.subtract))
        m = p.do("vector", lambda e: e.tensor_copy(B["stat"][:, 13:14], s1))
        p.wait("scalar", "vector", m)
        m = p.do("scalar", lambda e: e.activation(s1, B["stat"][:, 13:14], AF.Sigmoid))
        p.wait("vector", "scalar", m)
        p.do("vector", lambda e: e.tensor_scalar(s2, s1, -1.0, 1.0, OP.mult, OP.add))
        p.do("vector", lambda e, ct=ct, oh1=oh1: e.tensor_scalar(
            ct, oh1, s1, None, OP.mult, OP.bypass))
        p.do("vector", lambda e, oh2=oh2: e.tensor_scalar(
            oh2, oh2, s2, None, OP.mult, OP.bypass))
        p.do("vector", lambda e, ct=ct, oh2=oh2: e.tensor_tensor(
            ct, ct, oh2, OP.add))
        comb_done = p.do("vector", lambda e, ct=ct: e.tensor_scalar(
            ct, ct, coef, None, OP.mult, OP.bypass))

    F = 512
    FD = F // 128
    sq = B["sq"]
    p.wait("gpsimd", "vector", ev)
    xn_bf_view = B["xn"].bitcast(BF16)
    hidT = xn_bf_view[:, 0:FD * T]
    gpW = xn_bf_view[:, 8192:8192 + KD * F]
    upW = xn_bf_view[:, 12288:12288 + KD * F]
    dnW = xn_bf_view[:, 16384:16384 + FD * D]
    xnT = B["xnT"]
    first = True
    hid_consumed = 0
    evac = {}
    g_iter = 0
    sq_consumed = 0
    hid_marks = {}
    for it in range(E + 2):
        if it < E:
            srcs = [(W["up_proj"][it], upW), (W["gate_proj"][it], gpW)]
            dn_src = W["down_proj"][it]
        else:
            half = it - E
            srcs = [(W["sh_up"][:, half * F:(half + 1) * F], upW),
                    (W["sh_gate"][:, half * F:(half + 1) * F], gpW)]
            dn_src = W["sh_down"][half * F:(half + 1) * F, :]
        if not first:
            p.wait("gpsimd", "tensor", hid_consumed)
        for src, dst in srcs:
            p.dma("gpsimd", dst[:, :].rearrange("p (c f) -> p c f", c=KD),
                  src.rearrange("(c p) f -> p c f", p=128), "l2ew")
        wm = p.dma("gpsimd", dnW[:, :].rearrange("p (c d) -> p c d", c=FD),
                   dn_src.rearrange("(c p) d -> p c d", p=128), "l2ew")
        p.waitg("tensor", "l2ew", wm)
        if first:
            p.wait("tensor", "vector", comb_done)
            first = False
        if it >= 1:
            p.wait("tensor", "scalar", evac[it * NT - 1])
        for ht in range(FD):
            for nb in range(4):
                slot = ((ht * 4 + nb) % 2) * 1024
                psG = B["ps_w"][:, slot:slot + 512]
                psU = B["ps_w"][:, slot + 512:slot + 1024]
                if g_iter >= 2:
                    p.wait("tensor", "vector", hid_marks[g_iter - 2])
                for c in range(KD):
                    p.do("tensor", lambda e, c=c, ht=ht, nb=nb, psG=psG: e.matmul(
                        psG, gpW[:, c * F + ht * 128: c * F + (ht + 1) * 128],
                        xnT[:, c * T + nb * 512: c * T + (nb + 1) * 512],
                        start=(c == 0), stop=(c == KD - 1)))
                mm = 0
                for c in range(KD):
                    mm = p.do("tensor", lambda e, c=c, ht=ht, nb=nb, psU=psU: e.matmul(
                        psU, upW[:, c * F + ht * 128: c * F + (ht + 1) * 128],
                        xnT[:, c * T + nb * 512: c * T + (nb + 1) * 512],
                        start=(c == 0), stop=(c == KD - 1)))
                p.wait("scalar", "tensor", mm)
                if g_iter >= 1:
                    p.wait("scalar", "vector", max(hid_marks[g_iter - 1], sq_consumed))
                m = p.do("scalar", lambda e, psG=psG: e.activation(
                    sq[:, 0:512], psG, AF.Silu))
                p.wait("vector", "scalar", m)
                hid_marks[g_iter] = p.do("vector", lambda e, ht=ht, nb=nb, psU=psU: e.tensor_tensor(
                    hidT[:, ht * T + nb * 512: ht * T + (nb + 1) * 512],
                    sq[:, 0:512], psU, OP.mult))
                g_iter += 1
        p.wait("tensor", "vector", hid_marks[g_iter - 1])
        for t in range(NT):
            gidx = it * NT + t
            half = (t % 2) * 1024
            ph0 = B["ps_w"][:, half:half + 512]
            ph1 = B["ps_w"][:, half + 512:half + 1024]
            if gidx >= 2 and (it * NT + t - 2) in evac:
                p.wait("tensor", "scalar", evac[it * NT + t - 2])
            for c in range(FD):
                lhs = hidT[:, c * T + t * 128: c * T + (t + 1) * 128]
                p.do("tensor", lambda e, lhs=lhs, c=c, ph0=ph0: e.matmul(
                    ph0, lhs, dnW[:, c * D: c * D + 512], start=(c == 0), stop=False))
                last = p.do("tensor", lambda e, lhs=lhs, c=c, ph1=ph1: e.matmul(
                    ph1, lhs, dnW[:, c * D + 512: (c + 1) * D],
                    start=(c == 0), stop=(c == FD - 1)))
            mm = p.do("tensor", lambda e, ph0=ph0: e.matmul(
                ph0, B["ones_row"][0:1, 0:128], B["zero_row"][0:1, 0:512],
                start=False, stop=True))
            hid_consumed = mm
            p.wait("scalar", "tensor", mm)
            p.wait("scalar", "vector", sq_consumed)
            if it < E:
                sc = comb[:, t * E + it: t * E + it + 1]
                p.do("scalar", lambda e, ph0=ph0, sc=sc: e.activation(
                    sq[:, 0:512], ph0, AF.Identity, scale=sc))
                m = p.do("scalar", lambda e, ph1=ph1, sc=sc: e.activation(
                    sq[:, 512:1024], ph1, AF.Identity, scale=sc))
            else:
                p.do("scalar", lambda e, ph0=ph0: e.activation(
                    sq[:, 0:512], ph0, AF.Identity, scale=coef))
                m = p.do("scalar", lambda e, ph1=ph1: e.activation(
                    sq[:, 512:1024], ph1, AF.Identity, scale=coef))
            evac[it * NT + t] = m
            p.wait("vector", "scalar", m)
            sq_consumed = p.do("vector", lambda e, t=t: e.tensor_tensor(
                B["x_res"][:, t * D:(t + 1) * D], B["x_res"][:, t * D:(t + 1) * D],
                sq[:, 0:D], OP.add))
    return ("vector", sq_consumed)




def emit_l0_attn(p, B, W, coef):
    NH, DK, DC, DR = 8, 128, 32, 32
    SCALE = 1.0 / float(np.sqrt(DK + DR))
    ln_tiles = emit_ln(p, B)
    xnT = B["xnT"]

    wdq = p.sb("a0_wdq", [128, KD * DC], BF16)
    wdkv = p.sb("a0_wdkv", [128, KD * DC], BF16)
    wkr = p.sb("a0_wkr", [128, KD * DR], BF16)
    wuq = p.sb("a0_wuq", [32, NH * DK], BF16)
    wuk = p.sb("a0_wuk", [32, NH * DK], BF16)
    wuv = p.sb("a0_wuv", [32, NH * DK], BF16)
    wqr = p.sb("a0_wqr", [32, NH * DR], BF16)
    wo = p.sb("a0_wo", [128, D], BF16)
    bdq = p.sb("a0_bdq", [32, 1], F32)
    bdkv = p.sb("a0_bdkv", [32, 1], F32)
    bkr = p.sb("a0_bkr", [32, 1], F32)
    buq = p.sb("a0_buq", [128, NH], F32)
    buk = p.sb("a0_buk", [128, NH], F32)
    bqr = p.sb("a0_bqr", [32, NH], F32)
    buv_row = p.sb("a0_buvr", [1, NH * DK], F32)
    bo = p.sb("a0_bo", [1, D], F32)
    g = "a0w"
    p.dma("gpsimd", wdq[:, :].rearrange("p (c n) -> p c n", c=KD),
          W["W_DQ"].rearrange("(c p) n -> p c n", p=128), g)
    p.dma("gpsimd", wdkv[:, :].rearrange("p (c n) -> p c n", c=KD),
          W["W_DKV"].rearrange("(c p) n -> p c n", p=128), g)
    p.dma("gpsimd", wkr[:, :].rearrange("p (c n) -> p c n", c=KD),
          W["W_KR"].rearrange("(c p) n -> p c n", p=128), g)
    p.dma("gpsimd", wuq[:, :], W["W_UQ"][:, :], g)
    p.dma("gpsimd", wuk[:, :], W["W_UK"][:, :], g)
    p.dma("gpsimd", wuv[:, :], W["W_UV"][:, :], g)
    p.dma("gpsimd", wqr[:, :], W["W_QR"][:, :], g)
    gb_ = "a0b"
    p.dma("sync", bdq[:, 0:1], W["b_DQ"].rearrange("(p one) -> p one", one=1), gb_)
    p.dma("sync", bdkv[:, 0:1], W["b_DKV"].rearrange("(p one) -> p one", one=1), gb_)
    p.dma("sync", bkr[:, 0:1], W["b_KR"].rearrange("(p one) -> p one", one=1), gb_)
    p.dma("sync", buq[:, :], W["b_UQ"].rearrange("(h p) -> p h", p=128), gb_,
          allow_slow_non_contiguous=True)
    p.dma("sync", buk[:, :], W["b_UK"].rearrange("(h p) -> p h", p=128), gb_,
          allow_slow_non_contiguous=True)
    p.dma("sync", bqr[:, :], W["b_QR"].rearrange("(h p) -> p h", p=32), gb_,
          allow_slow_non_contiguous=True)
    p.dma("sync", buv_row[0:1, :], W["b_UV"][None, :], gb_)
    wmb = p.dma("sync", bo[0:1, :], W["b_O"][None, :], gb_)
    wm = p.gsem(g)[1]

    xnbf = B["xn"].bitcast(BF16)
    qcT = xnbf[:, 0:T]
    kcT = xnbf[:, T:2 * T]
    attn_n = xnbf[:, 2 * T:3 * T]
    attnT = xnbf[:, 3 * T:4 * T]
    vtok = xnbf[:, 4 * T:5 * T]
    oT = xnbf[:, 5 * T:6 * T]
    attn_sb = B["xn"][:, 6144:6144 + T]
    cqT = p.sb("a0_cq", [32, T], BF16)
    ckvT = p.sb("a0_ckv", [32, T], BF16)
    krT = p.sb("a0_kr", [32, T], BF16)
    qrT = xnbf[0:32, 16384:16384 + NH * T]

    p.waitg("tensor", g, wm)
    p.waitg("tensor", gb_, wmb)
    p.waitg("scalar", gb_, wmb)
    p.wait("tensor", "vector", ln_tiles[-1])

    rmax = B["stat"][:, 9:10]
    rsum = B["stat"][:, 10:11]
    rinv = B["stat"][:, 11:12]

    def proj32(dst, wT, bias_col, evm):
        for nb in range(4):
            ps = B["ps_c"][0:32, 0:512]
            if evm:
                p.wait("tensor", "scalar", evm)
            for c in range(KD):
                p.do("tensor", lambda e, c=c, nb=nb, wT=wT, ps=ps: e.matmul(
                    ps, wT[:, c * DC:(c + 1) * DC],
                    xnT[:, c * T + nb * 512:c * T + (nb + 1) * 512],
                    start=(c == 0), stop=(c == KD - 1)))
            p.wait("scalar", "tensor", p.mark("tensor"))
            evm = p.do("scalar", lambda e, nb=nb, dst=dst, bias_col=bias_col, ps=ps:
                       e.activation(dst[:, nb * 512:(nb + 1) * 512], ps,
                                    AF.Identity, bias=bias_col))
        return evm

    m1 = proj32(cqT, wdq, bdq[0:32, 0:1], 0)
    m2 = proj32(ckvT, wdkv, bdkv[0:32, 0:1], m1)
    m3 = proj32(krT, wkr, bkr[0:32, 0:1], m2)
    p.wait("tensor", "scalar", m1)
    evm = 0
    for h in range(NH):
        for nb in range(4):
            ps = B["ps_d"][0:32, 0:512]
            if evm:
                p.wait("tensor", "scalar", evm)
            p.do("tensor", lambda e, h=h, nb=nb, ps=ps: e.matmul(
                ps, wqr[:, h * DR:(h + 1) * DR], cqT[:, nb * 512:(nb + 1) * 512],
                start=True, stop=True))
            p.wait("scalar", "tensor", p.mark("tensor"))
            evm = p.do("scalar", lambda e, h=h, nb=nb, ps=ps: e.activation(
                qrT[:, h * T + nb * 512:h * T + (nb + 1) * 512], ps, AF.Identity,
                bias=bqr[0:32, h:h + 1]))
    qr_done = evm
    p.wait("tensor", "scalar", m2)
    p.wait("tensor", "scalar", m3)
    p.wait("tensor", "scalar", qr_done)

    psAb = B["ps_a"].bitcast(BF16)
    psBb = B["ps_b"].bitcast(BF16)
    evm_c = m3
    evm_d = qr_done
    psw_reader_s = 0
    psw_reader_v = 0
    mbf = 0
    tr_last = 0
    res_add = 0
    for h in range(NH):
        for dst, wmat, bcol, src_ in ((qcT, wuq, buq, cqT), (kcT, wuk, buk, ckvT)):
            for nb in range(4):
                ps = B["ps_c"][:, 0:512]
                if evm_c:
                    p.wait("tensor", "scalar", evm_c)
                p.do("tensor", lambda e, h=h, nb=nb, wmat=wmat, src_=src_, ps=ps:
                     e.matmul(ps, wmat[:, h * DK:(h + 1) * DK],
                              src_[:, nb * 512:(nb + 1) * 512], start=True, stop=True))
                p.wait("scalar", "tensor", p.mark("tensor"))
                evm_c = p.do("scalar", lambda e, nb=nb, dst=dst, bcol=bcol, h=h, ps=ps:
                             e.activation(dst[:, nb * 512:(nb + 1) * 512], ps,
                                          AF.Identity, bias=bcol[:, h:h + 1]))
        for t in range(NT):
            ps = B["ps_d"][:, 0:128]
            if evm_d:
                p.wait("tensor", "scalar", evm_d)
            p.do("tensor", lambda e, h=h, t=t, ps=ps: e.matmul(
                ps, ckvT[:, t * 128:(t + 1) * 128], wuv[:, h * DK:(h + 1) * DK],
                start=True, stop=False))
            p.do("tensor", lambda e, h=h, ps=ps: e.matmul(
                ps, B["ones_row"][0:1, 0:128], buv_row[0:1, h * DK:(h + 1) * DK],
                start=False, stop=True))
            p.wait("scalar", "tensor", p.mark("tensor"))
            evm_d = p.do("scalar", lambda e, t=t, ps=ps: e.activation(
                vtok[:, t * 128:(t + 1) * 128], ps, AF.Identity))
        p.wait("tensor", "scalar", evm_c)
        p.wait("tensor", "scalar", evm_d)
        for qt in range(NT):
            if psw_reader_s:
                p.wait("tensor", "scalar", psw_reader_s)
            if psw_reader_v:
                p.wait("tensor", "vector", psw_reader_v)
            for nb in range(4):
                p.do("tensor", lambda e, h=h, qt=qt, nb=nb: e.matmul(
                    B["ps_w"][:, nb * 512:(nb + 1) * 512],
                    qcT[:, qt * 128:(qt + 1) * 128],
                    kcT[:, nb * 512:(nb + 1) * 512], start=True, stop=False))
                p.do("tensor", lambda e, h=h, qt=qt, nb=nb: e.matmul(
                    B["ps_w"][:, nb * 512:(nb + 1) * 512],
                    qrT[0:32, h * T + qt * 128:h * T + (qt + 1) * 128],
                    krT[:, nb * 512:(nb + 1) * 512], start=False, stop=True))
            smm = p.mark("tensor")
            p.wait("vector", "tensor", smm)
            p.do("vector", lambda e: e.tensor_reduce(rmax, B["ps_w"][:, 0:2048],
                                                     AX.X, OP.max))
            mneg = p.do("vector", lambda e: e.tensor_scalar(
                rmax, rmax, -SCALE, None, OP.mult, OP.bypass))
            p.wait("scalar", "vector", mneg)
            psw_reader_s = mexp = p.do("scalar", lambda e: e.activation(
                attn_sb[:, :], B["ps_w"][:, 0:2048], AF.Exp, bias=rmax,
                scale=SCALE, accum_out=rsum))
            p.wait("vector", "scalar", mexp)
            p.do("vector", lambda e: e.reciprocal(rinv, rsum))
            if tr_last:
                p.wait("vector", "tensor", tr_last)
            mbf = p.do("vector", lambda e: e.tensor_scalar(
                attn_n[:, 0:T], attn_sb[:, :], rinv, None, OP.mult, OP.bypass))
            p.wait("tensor", "vector", mbf)
            mcp = {}
            for st in range(NT):
                slot = psAb if st % 2 == 0 else psBb
                if st >= 2:
                    p.wait("tensor", "vector", mcp[st - 2])
                p.do("tensor", lambda e, st=st, slot=slot: e.transpose(
                    slot[:, 0:128], attn_n[:, st * 128:(st + 1) * 128],
                    B["identbf"][:, :]))
                tr_last = p.mark("tensor")
                p.wait("vector", "tensor", tr_last)
                mcp[st] = p.do("vector", lambda e, st=st, slot=slot: e.tensor_copy(
                    attnT[:, st * 128:(st + 1) * 128], slot[:, 0:128]))
            p.wait("tensor", "vector", mcp[NT - 1])
            if evm_c:
                p.wait("tensor", "scalar", evm_c)
            pvm = 0
            for st in range(NT):
                pvm = p.do("tensor", lambda e, st=st: e.matmul(
                    B["ps_c"][:, 0:128], vtok[:, st * 128:(st + 1) * 128],
                    attnT[:, st * 128:(st + 1) * 128], start=(st == 0),
                    stop=(st == NT - 1)))
            p.wait("scalar", "tensor", pvm)
            evm_c = p.do("scalar", lambda e, qt=qt: e.activation(
                oT[:, qt * 128:(qt + 1) * 128], B["ps_c"][:, 0:128], AF.Identity))
        if res_add:
            p.wait("gpsimd", "vector", res_add)
        wom = p.dma("gpsimd", wo[:, :], W["W_O"][h * 128:(h + 1) * 128, :], "a0wo")
        p.waitg("tensor", "a0wo", wom)
        p.wait("tensor", "scalar", evm_c)
        if psw_reader_s:
            p.wait("tensor", "scalar", psw_reader_s)
        for t in range(NT):
            ph0 = B["ps_w"][:, 0:512]
            ph1 = B["ps_w"][:, 512:1024]
            if res_add:
                p.wait("tensor", "vector", res_add)
            p.do("tensor", lambda e, t=t, h=h: e.matmul(
                ph0, oT[:, t * 128:(t + 1) * 128], wo[:, 0:512],
                start=True, stop=(h != 0)))
            wmm = p.do("tensor", lambda e, t=t, h=h: e.matmul(
                ph1, oT[:, t * 128:(t + 1) * 128], wo[:, 512:1024],
                start=True, stop=(h != 0)))
            if h == 0:
                p.do("tensor", lambda e: e.matmul(
                    ph0, B["ones_row"][0:1, 0:128], bo[0:1, 0:512],
                    start=False, stop=True))
                wmm = p.do("tensor", lambda e: e.matmul(
                    ph1, B["ones_row"][0:1, 0:128], bo[0:1, 512:1024],
                    start=False, stop=True))
            p.wait("vector", "tensor", wmm)
            res_add = p.do("vector", lambda e, t=t: e.tensor_tensor(
                B["x_res"][:, t * D:(t + 1) * D], B["x_res"][:, t * D:(t + 1) * D],
                B["ps_w"][:, 0:1024], OP.add))
        psw_reader_v = res_add
    return ("vector", res_add)




def emit_l1_attn(p, B, W, coef):
    NH, HD = 16, 64
    ln_tiles = emit_ln(p, B)
    xnT = B["xnT"]
    gammas = [1.0 - 2.0 ** (-5.0 - h) for h in range(NH)]

    wqkv = p.sb("a1_w", [128, KD * 3 * HD], BF16)
    pkv = p.sb("a1_pkv", [64, NH * HD], BF16)
    pkv32 = p.sb("a1_pkv32", [64, NH * HD], F32)
    gnw = p.sb("a1_gnw", [1, D], F32)
    gnb = p.sb("a1_gnb", [1, D], F32)
    tokidx = p.sb("a1_tok", [128, NT], F32)
    gneg = p.sb("a1_gneg", [128, NT], F32)
    gpos = p.sb("a1_gpos", [128, NT], F32)
    iotaR = p.sb("a1_iR", [128, 128], F32)
    iotaP = p.sb("a1_iP", [128, 1], F32)
    trT = p.sb("a1_trT", [128, 128], F32)
    arow = p.sb("a1_arow", [1, D], F32)
    brow = p.sb("a1_brow", [1, D], F32)
    stats2 = p.sb("a1_st", [128, 8], F32)
    abc = p.sb("a1_abc", [128, 2 * D], F32)

    xnbf = B["xn"].bitcast(BF16)
    qkv_tok = xnbf[:, 0:NT * 192]
    qks = xnbf[:, 4096:4096 + NT * 128]
    qhT = xnbf[0:64, 8192:8192 + T]
    ktT = xnbf[0:64, 10240:10240 + T]
    qTr = xnbf[:, 12288:12288 + T]
    ret_bf = xnbf[:, 14336:14336 + 256]
    oTs = xnbf[:, 14848:14848 + 128]
    r_tok = xnbf[:, 16384:32768]

    gw = "a1w"
    m = p.dma("gpsimd", pkv32[:, :].rearrange("p (h e) -> p h e", h=NH),
              W["past_kv"].rearrange("h p e -> p h e"), "a1pkv")
    p.dma("sync", gnw[0:1, :], W["gn_weight_full"][None, :], "a1b")
    gm = p.dma("sync", gnb[0:1, :], W["gn_bias_full"][None, :], "a1b")
    p.waitg("vector", "a1pkv", m)
    pkm = p.do("vector", lambda e: e.tensor_scalar(
        pkv[:, :], pkv32[:, :], float(NH * HD), None, OP.mult, OP.bypass))
    gi1 = p.do("gpsimd", lambda e: e.iota(tokidx[:, :], pattern=[[128, NT]], base=0,
                                          channel_multiplier=1,
                                          allow_small_or_imprecise_dtypes=True))
    gi2 = p.do("gpsimd", lambda e: e.iota(iotaR[:, :], pattern=[[1, 128]], base=0,
                                          channel_multiplier=0,
                                          allow_small_or_imprecise_dtypes=True))
    gi3 = p.do("gpsimd", lambda e: e.iota(iotaP[:, :], pattern=[[0, 1]], base=0,
                                          channel_multiplier=1,
                                          allow_small_or_imprecise_dtypes=True))
    p.wait("vector", "gpsimd", gi3)
    trm = p.do("vector", lambda e: e.tensor_scalar(
        trT[:, :], iotaR[:, :], iotaP[:, 0:1], None, OP.is_ge, OP.bypass))

    psA, psB = B["ps_a"], B["ps_b"]
    evm_c = 0
    evm_d = 0
    ret_ev = {}
    rtok_w = 0
    tr_cp = {}
    p.wait("tensor", "vector", ln_tiles[-1])
    p.wait("tensor", "vector", trm)
    p.wait("tensor", "vector", pkm)
    p.wait("scalar", "gpsimd", gi3)
    p.wait("gpsimd", "vector", ln_tiles[-1])
    gz0 = p.do("gpsimd", lambda e: e.memset(oTs[:, :], 0.0))
    p.wait("scalar", "gpsimd", gz0)
    for h in range(NH):
        gam = gammas[h]
        lng = float(np.log(gam))
        if rtok_w:
            p.wait("scalar", "vector", rtok_w)
        mgp = p.do("scalar", lambda e, lng=lng: e.activation(gneg[:, :], tokidx[:, :],
                                                             AF.Exp, scale=-lng))
        mgp = p.do("scalar", lambda e, lng=lng: e.activation(gpos[:, :], tokidx[:, :],
                                                             AF.Exp, scale=lng))
        if h > 0:
            p.wait("gpsimd", "tensor", p.mark("tensor"))
        wq3 = wqkv[:, :].rearrange("p (c g) -> p c g", c=KD)
        for j, off in enumerate((0, D, 2 * D)):
            wm = p.dma("gpsimd", wq3[:, :, j * HD:(j + 1) * HD],
                       W["W_qkv"][:, off + h * HD: off + (h + 1) * HD]
                       .rearrange("(c p) f -> p c f", p=128), gw)
        p.waitg("tensor", gw, wm)
        for t in range(NT):
            ps = B["ps_c"][:, 0:192]
            if evm_c:
                p.wait("tensor", "scalar", evm_c)
            for c in range(KD):
                p.do("tensor", lambda e, c=c, t=t, ps=ps: e.matmul(
                    ps, xnT[:, c * T + t * 128:(c * T + (t + 1) * 128)],
                    wqkv[:, c * 192:(c + 1) * 192], start=(c == 0), stop=(c == KD - 1)))
            p.wait("scalar", "tensor", p.mark("tensor"))
            evm_c = p.do("scalar", lambda e, t=t, ps=ps: e.activation(
                qkv_tok[:, t * 192:(t + 1) * 192], ps, AF.Identity))
        p.wait("vector", "scalar", evm_c)
        mqs = 0
        for t in range(NT):
            mqs = p.do("vector", lambda e, t=t: e.tensor_scalar(
                qks[:, t * 128:(t + 1) * 128], qkv_tok[:, t * 192:t * 192 + 128],
                gneg[:, t:t + 1], None, OP.mult, OP.bypass))
        p.wait("tensor", "vector", mqs)
        for t in range(NT):
            slot = psA.bitcast(BF16) if t % 2 == 0 else psB.bitcast(BF16)
            if t >= 2:
                p.wait("tensor", "vector", tr_cp[t - 2])
            p.do("tensor", lambda e, t=t, slot=slot: e.transpose(
                slot[:, 0:128], qks[:, t * 128:(t + 1) * 128], B["identbf"][:, :]))
            p.do("tensor", lambda e, t=t, slot=slot: e.transpose(
                slot[:, 128:256], qkv_tok[:, t * 192:t * 192 + 128],
                B["identbf"][:, :]))
            mt = p.mark("tensor")
            p.wait("vector", "tensor", mt)
            p.do("vector", lambda e, t=t, slot=slot: e.tensor_copy(
                qhT[:, t * 128:(t + 1) * 128], slot[0:64, 0:128]))
            p.do("vector", lambda e, t=t, slot=slot: e.tensor_copy(
                ktT[:, t * 128:(t + 1) * 128], slot[64:128, 0:128]))
            tr_cp[t] = p.do("vector", lambda e, t=t, slot=slot: e.tensor_copy(
                qTr[0:64, t * 128:(t + 1) * 128], slot[0:64, 128:256]))
        p.wait("tensor", "vector", tr_cp[NT - 1])
        for qt in range(NT):
            if evm_c:
                p.wait("tensor", "scalar", evm_c)
            p.do("tensor", lambda e, h=h, qt=qt: e.matmul(
                B["ps_c"][0:64, 0:128], pkv[:, h * HD:(h + 1) * HD],
                qhT[:, qt * 128:(qt + 1) * 128], start=True, stop=False))
            for st in range(qt + 1):
                sidx = qt * (qt + 1) // 2 + st
                sps = psA if sidx % 2 == 0 else psB
                p.do("tensor", lambda e, st=st, qt=qt, sps=sps: e.matmul(
                    sps[:, 0:128], ktT[:, st * 128:(st + 1) * 128],
                    qTr[0:64, qt * 128:(qt + 1) * 128], start=True, stop=True))
                mt = p.mark("tensor")
                if st == qt:
                    p.wait("vector", "tensor", mt)
                    mm_ = p.do("vector", lambda e, sps=sps: e.tensor_tensor(
                        sps[:, 0:128], sps[:, 0:128], trT[:, :], OP.mult))
                    p.wait("scalar", "vector", mm_)
                else:
                    p.wait("scalar", "tensor", mt)
                slotr = (sidx % 2) * 128
                ret_ev[sidx] = p.do("scalar", lambda e, slotr=slotr, sps=sps: e.activation(
                    ret_bf[:, slotr:slotr + 128], sps[:, 0:128], AF.Identity))
                p.wait("tensor", "scalar", ret_ev[sidx])
                p.do("tensor", lambda e, st=st, slotr=slotr, qt=qt: e.matmul(
                    B["ps_c"][0:64, 0:128], qkv_tok[:, st * 192 + 128:(st + 1) * 192],
                    ret_bf[:, slotr:slotr + 128], start=False, stop=(st == qt)))
            p.wait("scalar", "tensor", p.mark("tensor"))
            evm_c = p.do("scalar", lambda e: e.activation(oTs[0:64, 0:128],
                                                          B["ps_c"][0:64, 0:128],
                                                          AF.Identity))
            p.wait("tensor", "scalar", evm_c)
            if evm_d:
                p.wait("tensor", "scalar", evm_d)
            if rtok_w:
                p.wait("tensor", "vector", rtok_w)
            p.do("tensor", lambda e: e.transpose(
                B["ps_d"].bitcast(BF16)[:, 0:128], oTs[:, 0:128],
                B["identbf"][:, :]))
            mt = p.mark("tensor")
            p.wait("vector", "tensor", mt)
            rtok_w = p.do("vector", lambda e, h=h, qt=qt: e.tensor_scalar(
                r_tok[:, qt * D + h * HD: qt * D + (h + 1) * HD],
                B["ps_d"].bitcast(BF16)[:, 0:64], gpos[:, qt:qt + 1], None,
                OP.mult, OP.bypass))
    s1c = stats2[:, 0:1]
    s2c = stats2[:, 1:2]
    p.wait("scalar", "vector", rtok_w)
    p.wait("vector", "vector", rtok_w) if False else None
    acc1 = stats2[:, 2:3]
    acc2 = stats2[:, 3:4]
    vdone = 0
    for t in range(NT):
        if vdone:
            p.wait("scalar", "vector", vdone)
        m_ = p.do("scalar", lambda e, t=t: e.activation(
            B["sq"][:, :], r_tok[:, t * D:(t + 1) * D], AF.Square,
            accum_out=s2c))
        p.wait("vector", "scalar", m_)
        p.do("vector", lambda e, t=t: e.tensor_reduce(
            s1c, r_tok[:, t * D:(t + 1) * D], AX.X, OP.add))
        if t == 0:
            p.do("vector", lambda e: e.tensor_copy(acc1, s1c))
            vdone = p.do("vector", lambda e: e.tensor_copy(acc2, s2c))
        else:
            p.do("vector", lambda e: e.tensor_tensor(acc1, acc1, s1c, OP.add))
            vdone = p.do("vector", lambda e: e.tensor_tensor(acc2, acc2, s2c, OP.add))
    mv = p.do("vector", lambda e: e.tensor_copy(stats2[:, 4:6], stats2[:, 2:4]))
    p.wait("tensor", "vector", mv)
    mm_ = p.do("tensor", lambda e: e.matmul(B["ps_d"][0:1, 0:2], B["ones_col"][:, 0:1],
                                            stats2[:, 4:6], start=True, stop=True))
    p.wait("vector", "tensor", mm_)
    NTOT = float(D * T)
    p.do("vector", lambda e: e.tensor_scalar(stats2[0:1, 6:8], B["ps_d"][0:1, 0:2],
                                             1.0 / NTOT, None, OP.mult, OP.bypass))
    p.do("vector", lambda e: e.tensor_tensor(stats2[0:1, 4:5], stats2[0:1, 6:7],
                                             stats2[0:1, 6:7], OP.mult))
    mv = p.do("vector", lambda e: e.tensor_tensor(stats2[0:1, 5:6], stats2[0:1, 7:8],
                                                  stats2[0:1, 4:5], OP.subtract))
    mv = p.do("vector", lambda e: e.tensor_scalar(stats2[0:1, 5:6], stats2[0:1, 5:6],
                                                   1.0, 1e-5, OP.mult, OP.add))
    p.wait("scalar", "vector", mv)
    p.do("scalar", lambda e: e.activation(stats2[0:1, 4:5], stats2[0:1, 5:6], AF.Ln))
    ms = p.do("scalar", lambda e: e.activation(stats2[0:1, 5:6], stats2[0:1, 4:5],
                                               AF.Exp, scale=-0.5))
    p.waitg("vector", "a1b", gm)
    p.wait("vector", "scalar", ms)
    p.do("vector", lambda e: e.tensor_scalar(arow[0:1, :], gnw[0:1, :],
                                             stats2[0:1, 5:6], None, OP.mult, OP.bypass))
    p.do("vector", lambda e: e.tensor_scalar(brow[0:1, :], arow[0:1, :],
                                             stats2[0:1, 6:7], -1.0, OP.mult, OP.mult))
    mv = p.do("vector", lambda e: e.tensor_tensor(brow[0:1, :], brow[0:1, :],
                                                  gnb[0:1, :], OP.add))
    p.wait("tensor", "vector", mv)
    mm_ = p.do("tensor", lambda e: e.matmul(B["ps_w"][:, 0:512], B["ones_row"][0:1, 0:128],
                                            arow[0:1, 0:512], start=True, stop=True))
    mm_ = p.do("tensor", lambda e: e.matmul(B["ps_w"][:, 512:1024], B["ones_row"][0:1, 0:128],
                                            arow[0:1, 512:1024], start=True, stop=True))
    mm_ = p.do("tensor", lambda e: e.matmul(B["ps_w"][:, 1024:1536], B["ones_row"][0:1, 0:128],
                                            brow[0:1, 0:512], start=True, stop=True))
    mm_ = p.do("tensor", lambda e: e.matmul(B["ps_w"][:, 1536:2048], B["ones_row"][0:1, 0:128],
                                            brow[0:1, 512:1024], start=True, stop=True))
    p.wait("vector", "tensor", mm_)
    p.do("vector", lambda e: e.tensor_copy(abc[:, 0:D], B["ps_w"][:, 0:1024]))
    mv = p.do("vector", lambda e: e.tensor_copy(abc[:, D:2 * D], B["ps_w"][:, 1024:2048]))
    last = 0
    for t in range(NT):
        p.do("vector", lambda e, t=t: e.tensor_tensor(
            B["sq"][:, :], r_tok[:, t * D:(t + 1) * D], abc[:, 0:D], OP.mult))
        p.do("vector", lambda e: e.tensor_tensor(
            B["sq"][:, :], B["sq"][:, :], abc[:, D:2 * D], OP.add))
        last = p.do("vector", lambda e, t=t: e.tensor_tensor(
            B["x_res"][:, t * D:(t + 1) * D], B["x_res"][:, t * D:(t + 1) * D],
            B["sq"][:, :], OP.add))
    return ("vector", last)




def colmax_to_scalar(p, B, maxcol, dst11, qdiv, src_mark=0):
    if src_mark:
        p.wait("tensor", "vector", src_mark)
    mm = p.do("tensor", lambda e: e.transpose(B["ps_d"][0:1, 0:128], maxcol,
                                              B["ident32"][:, :]))
    p.wait("vector", "tensor", mm)
    p.do("vector", lambda e: e.tensor_reduce(dst11, B["ps_d"][0:1, 0:128], AX.X, OP.max))
    return p.do("vector", lambda e: e.tensor_scalar(dst11, dst11, 1.0 / qdiv, 1e-12,
                                                    OP.mult, OP.max))


def emit_l3_ffn(p, B, W, coef, tag="q3"):
    I3 = 3072
    ln_tiles = emit_ln(p, B)
    xn = B["xn"]
    xnbf = xn.bitcast(BF16)
    xni = xn.bitcast(I32)
    xqT = B["xnT"][:, :]
    A = xn[:, 0:I3]
    Bf = xn[:, 3072:2 * 3072]
    Iv = xni[:, 2 * 3072:3 * 3072]
    Ivf = xn[:, 2 * 3072:3 * 3072]
    stage_bf = xnbf[:, 18432:18432 + I3]
    oqT = xnbf[:, 27648:27648 + I3]
    di = p.sb(tag + "_i", [128, D], I32)
    evb = p.sb(tag + "_e", [128, 1024], F32)
    d_i32 = di[:, :]
    d_f32 = B["sq"][:, :]
    ev0 = evb[:, 0:512]
    ev1 = evb[:, 512:1024]
    st3 = p.sb(tag + "_st", [128, 16], F32)
    row2 = p.sb(tag + "_r2", [1, 2], F32)
    bmx = p.sb(tag + "_bm", [128, 96], F32)
    ibm = p.sb(tag + "_ib", [128, 96], F32)
    wblk = p.sb(tag + "_wb", [128, 4 * D], F32)
    wT = p.sb(tag + "_wT", [128, KD * 512], BF16)
    uscr = p.dram(tag + "_u", [T, I3], F32)
    gscr = p.dram(tag + "_g", [T, I3], F32)
    oscr = p.dram(tag + "_o", [T, I3], F32)
    psAb = B["ps_a"].bitcast(BF16)
    psBb = B["ps_b"].bitcast(BF16)
    sq = B["sq"]

    gz = p.do("gpsimd", lambda e: e.memset(st3[:, 0:3], 0.0))
    mxu, mxg, mxp = st3[:, 0:1], st3[:, 1:2], st3[:, 2:3]

    mloc = st3[:, 3:4]
    p.wait("vector", "gpsimd", gz)
    for t in range(NT):
        p.do("vector", lambda e, t=t: e.tensor_reduce(
            st3[:, 4:5], xn[:, t * D:(t + 1) * D], AX.X, OP.max,
            apply_absolute_value=True))
        if t == 0:
            mv = p.do("vector", lambda e: e.tensor_copy(mloc, st3[:, 4:5]))
        else:
            mv = p.do("vector", lambda e: e.tensor_tensor(mloc, mloc, st3[:, 4:5], OP.max))
    p.wait("tensor", "vector", B["init_v"])
    mv = colmax_to_scalar(p, B, mloc, st3[0:1, 5:6], 32767.0, mv)
    arm, sx = emit_allreduce_row(p, B, st3[0:1, 5:6], 1, tag + "x", op=OP.max,
                                 src_ready=("vector", mv))
    p.waitg("vector", tag + "x", arm)
    p.waitg("tensor", tag + "x", arm)
    sxc, isxc = st3[:, 6:7], st3[:, 7:8]
    mm = bcast_row(p, B, sx[0:1, 0:1], 1, sxc, ("gpsimd", 0))
    p.wait("vector", "vector", mm)
    mv = p.do("vector", lambda e: e.reciprocal(isxc, sxc))
    tr_done = 0
    for t in range(NT):
        p.do("vector", lambda e, t=t: e.tensor_scalar(
            d_i32[:, :], xn[:, t * D:(t + 1) * D], isxc, None, OP.mult, OP.bypass))
        p.do("vector", lambda e: e.tensor_copy(d_f32[:, :], d_i32[:, :]))
        mv = p.do("vector", lambda e: e.tensor_scalar(
            B["xn_bf"][:, 0:D], d_f32[:, :], sxc, None, OP.mult, OP.bypass))
        p.wait("tensor", "vector", mv)
        for c in range(KD):
            g = t * KD + c
            slot = psAb if c % 2 == 0 else psBb
            if g >= 2:
                p.wait("tensor", "vector", tr_done - 1 if g >= 2 else 0)
            p.do("tensor", lambda e, c=c, slot=slot: e.transpose(
                slot[:, 0:128], B["xn_bf"][:, c * 128:(c + 1) * 128], B["identbf"][:, :]))
            mt = p.mark("tensor")
            p.wait("vector", "tensor", mt)
            tr_done = p.do("vector", lambda e, t=t, c=c, slot=slot: e.tensor_copy(
                xqT[:, c * T + t * 128:c * T + (t + 1) * 128], slot[:, 0:128]))
    evac_s = 0
    last_mv = 0
    dq_all = 0
    ldq = tag + "w"
    sdq = tag + "s"
    for mi, (wkey, scr, mxcol) in enumerate((("w_up", uscr, mxu), ("w_gate", gscr, mxg))):
        for blk in range(6):
            if dq_all:
                p.wait("gpsimd", "vector", dq_all)
            wm = p.dma("gpsimd", wblk[:, :].rearrange("p (r d) -> p r d", r=4),
                       W[wkey][blk * 512:(blk + 1) * 512, :]
                       .rearrange("(r p) d -> p r d", p=128), ldq)
            p.waitg("vector", ldq, wm)
            dq_v = 0
            for rt in range(4):
                wv = wblk[:, rt * D:(rt + 1) * D]
                w3 = wv.rearrange("p (b k) -> p b k", k=32)
                p.do("vector", lambda e, w3=w3: e.tensor_reduce(
                    bmx[:, 0:32], w3, AX.X, OP.max, apply_absolute_value=True))
                p.do("vector", lambda e: e.tensor_scalar(
                    bmx[:, 0:32], bmx[:, 0:32], 1.0 / 7.0, 1e-12, OP.mult, OP.max))
                p.do("vector", lambda e: e.reciprocal(ibm[:, 0:32], bmx[:, 0:32]))
                for bb in range(32):
                    p.do("vector", lambda e, wv=wv, bb=bb: e.tensor_scalar(
                        d_i32[:, bb * 32:(bb + 1) * 32], wv[:, bb * 32:(bb + 1) * 32],
                        ibm[:, bb:bb + 1], None, OP.mult, OP.bypass))
                p.do("vector", lambda e: e.tensor_copy(d_f32[:, :], d_i32[:, :]))
                for bb in range(32):
                    dq_v = p.do("vector", lambda e, bb=bb: e.tensor_scalar(
                        stage_bf[:, bb * 32:(bb + 1) * 32], d_f32[:, bb * 32:(bb + 1) * 32],
                        bmx[:, bb:bb + 1], None, OP.mult, OP.bypass))
                p.wait("tensor", "vector", dq_v)
                cpg = {}
                for c in range(KD):
                    slot = psAb if c % 2 == 0 else psBb
                    if c >= 2:
                        p.wait("tensor", "vector", cpg[c - 2])
                    p.do("tensor", lambda e, c=c, slot=slot: e.transpose(
                        slot[:, 0:128], stage_bf[:, c * 128:(c + 1) * 128],
                        B["identbf"][:, :]))
                    mt = p.mark("tensor")
                    p.wait("vector", "tensor", mt)
                    dq_v = cpg[c] = p.do("vector", lambda e, c=c, rt=rt, slot=slot: e.tensor_copy(
                        wT[:, c * 512 + rt * 128:c * 512 + (rt + 1) * 128],
                        slot[:, 0:128]))
            dq_all = dq_v
            p.wait("tensor", "vector", dq_v)
            for t in range(NT):
                half = (t % 2) * 512
                ph = B["ps_w"][:, half:half + 512]
                if evac_s:
                    p.wait("tensor", "scalar", evac_s)
                for c in range(KD):
                    p.do("tensor", lambda e, c=c, t=t, ph=ph: e.matmul(
                        ph, xqT[:, c * T + t * 128:c * T + (t + 1) * 128],
                        wT[:, c * 512:(c + 1) * 512], start=(c == 0), stop=(c == KD - 1)))
                mt = p.mark("tensor")
                p.wait("scalar", "tensor", mt)
                p.waitg("scalar", sdq, p.gsem(sdq)[1])
                if last_mv:
                    p.wait("scalar", "vector", last_mv)
                slot = ev0 if t % 2 == 0 else ev1
                evac_s = p.do("scalar", lambda e, ph=ph, slot=slot: e.activation(
                    slot, ph, AF.Identity))
                p.wait("vector", "scalar", evac_s)
                p.do("vector", lambda e, slot=slot: e.tensor_reduce(
                    st3[:, 4:5], slot, AX.X, OP.max, apply_absolute_value=True))
                last_mv = mv = p.do("vector", lambda e, mxcol=mxcol: e.tensor_tensor(
                    mxcol, mxcol, st3[:, 4:5], OP.max))
                p.wait("gpsimd", "scalar", evac_s)
                p.dma("gpsimd", scr[t * 128:(t + 1) * 128, blk * 512:(blk + 1) * 512],
                      slot, sdq)
    mv1 = colmax_to_scalar(p, B, mxu, row2[0:1, 0:1], 32767.0, last_mv)
    mv2 = colmax_to_scalar(p, B, mxg, row2[0:1, 1:2], 32767.0, mv1)
    arm, sug = emit_allreduce_row(p, B, row2[0:1, :], 2, tag + "ug", op=OP.max,
                                  src_ready=("vector", mv2))
    p.waitg("vector", tag + "ug", arm)
    p.waitg("tensor", tag + "ug", arm)
    su_c, sg_c, isu_c, isg_c = st3[:, 8:9], st3[:, 9:10], st3[:, 10:11], st3[:, 11:12]
    mm = bcast_row(p, B, sug[0:1, 0:1], 1, su_c, ("gpsimd", 0))
    mm = bcast_row(p, B, sug[0:1, 1:2], 1, sg_c, ("vector", mm))
    p.wait("vector", "vector", mm)
    p.do("vector", lambda e: e.reciprocal(isu_c, su_c))
    mv = p.do("vector", lambda e: e.reciprocal(isg_c, sg_c))

    def qdq_inplace(dstA, inv_c, s_c):
        Wd = dstA.shape[-1]
        p.do("vector", lambda e: e.tensor_scalar(Iv[:, 0:Wd], dstA, inv_c, None,
                                                 OP.mult, OP.bypass))
        p.do("vector", lambda e: e.tensor_copy(Ivf[:, 0:Wd], Iv[:, 0:Wd]))
        return p.do("vector", lambda e: e.tensor_scalar(dstA, Ivf[:, 0:Wd], s_c, None,
                                                        OP.mult, OP.bypass))

    lda = tag + "la"
    sta = tag + "sa"
    st_m = 0
    prev_v = 0
    for t in range(NT):
        p.waitg("gpsimd", sdq, p.gsem(sdq)[1])
        if st_m:
            p.waitg("gpsimd", sta, st_m)
        if prev_v:
            p.wait("gpsimd", "vector", prev_v)
        lm = p.dma("gpsimd", A, gscr[t * 128:(t + 1) * 128, :], lda)
        p.waitg("vector", lda, lm)
        qdq_inplace(A, isg_c, sg_c)
        mv = p.mark("vector")
        p.wait("scalar", "vector", mv)
        ms = p.do("scalar", lambda e: e.activation(Bf, A, AF.Sigmoid))
        p.wait("vector", "scalar", ms)
        p.do("vector", lambda e: e.tensor_scalar(Iv[:, 0:I3], Bf, 65536.0, None,
                                                 OP.mult, OP.bypass))
        p.do("vector", lambda e: e.tensor_copy(Ivf[:, 0:I3], Iv[:, 0:I3]))
        p.do("vector", lambda e: e.tensor_scalar(Bf, Ivf[:, 0:I3], 65535.0, 1.0 / 65536.0,
                                                 OP.min, OP.mult))
        p.do("vector", lambda e: e.tensor_tensor(A, A, Bf, OP.mult))
        p.do("vector", lambda e: e.tensor_reduce(st3[:, 4:5], A, AX.X, OP.max,
                                                 apply_absolute_value=True))
        mv = p.do("vector", lambda e: e.tensor_tensor(mxp, mxp, st3[:, 4:5], OP.max))
        prev_v = mv
        p.wait("gpsimd", "vector", mv)
        st_m = p.dma("gpsimd", gscr[t * 128:(t + 1) * 128, :], A, sta)
    mv = colmax_to_scalar(p, B, mxp, row2[0:1, 0:1], 32767.0, mv)
    arm, sp = emit_allreduce_row(p, B, row2[0:1, 0:1], 1, tag + "p", op=OP.max,
                                 src_ready=("vector", mv))
    p.waitg("vector", tag + "p", arm)
    p.waitg("tensor", tag + "p", arm)
    sp_c, isp_c = st3[:, 12:13], st3[:, 13:14]
    mm = bcast_row(p, B, sp[0:1, 0:1], 1, sp_c, ("gpsimd", 0))
    p.wait("vector", "vector", mm)
    mv = p.do("vector", lambda e: e.reciprocal(isp_c, sp_c))
    p.do("vector", lambda e: e.tensor_scalar(mxp, mxp, 0.0, None, OP.mult, OP.bypass))
    st_m2 = 0
    prev_v = 0
    for t in range(NT):
        p.waitg("gpsimd", sta, p.gsem(sta)[1])
        if st_m2:
            p.waitg("gpsimd", tag + "sb", st_m2)
        if prev_v:
            p.wait("gpsimd", "vector", prev_v)
        lm = p.dma("gpsimd", A, gscr[t * 128:(t + 1) * 128, :], lda)
        lm = p.dma("gpsimd", Bf, uscr[t * 128:(t + 1) * 128, :], lda)
        p.waitg("vector", lda, lm)
        qdq_inplace(A, isp_c, sp_c)
        qdq_inplace(Bf, isu_c, su_c)
        p.do("vector", lambda e: e.tensor_tensor(A, A, Bf, OP.mult))
        p.do("vector", lambda e: e.tensor_reduce(st3[:, 4:5], A, AX.X, OP.max,
                                                 apply_absolute_value=True))
        mv = p.do("vector", lambda e: e.tensor_tensor(mxp, mxp, st3[:, 4:5], OP.max))
        prev_v = mv
        p.wait("gpsimd", "vector", mv)
        st_m2 = p.dma("gpsimd", oscr[t * 128:(t + 1) * 128, :], A, tag + "sb")
    mv = colmax_to_scalar(p, B, mxp, row2[0:1, 0:1], 32767.0, mv)
    arm, so = emit_allreduce_row(p, B, row2[0:1, 0:1], 1, tag + "o", op=OP.max,
                                 src_ready=("vector", mv))
    p.waitg("vector", tag + "o", arm)
    p.waitg("tensor", tag + "o", arm)
    so_c, iso_c = st3[:, 14:15], st3[:, 15:16]
    mm = bcast_row(p, B, so[0:1, 0:1], 1, so_c, ("gpsimd", 0))
    p.wait("vector", "vector", mm)
    mv = p.do("vector", lambda e: e.reciprocal(iso_c, so_c))

    res_add = 0
    evac_s = 0
    wdT = B["xnT"]
    for half in range(2):
        dq_v = 0
        for rt in range(4):
            hr = half * 512 + rt * 128
            if dq_all or dq_v:
                p.wait("gpsimd", "vector", max(dq_all, dq_v))
            wm = p.dma("gpsimd", wblk[:, 0:I3], W["w_down"][hr:hr + 128, :], ldq)
            p.waitg("vector", ldq, wm)
            w3 = wblk[:, 0:I3].rearrange("p (b k) -> p b k", k=32)
            p.do("vector", lambda e, w3=w3: e.tensor_reduce(
                bmx[:, :], w3, AX.X, OP.max, apply_absolute_value=True))
            p.do("vector", lambda e: e.tensor_scalar(
                bmx[:, :], bmx[:, :], 1.0 / 7.0, 1e-12, OP.mult, OP.max))
            p.do("vector", lambda e: e.reciprocal(ibm[:, :], bmx[:, :]))
            for bb in range(96):
                p.do("vector", lambda e, bb=bb: e.tensor_scalar(
                    Iv[:, bb * 32:(bb + 1) * 32], wblk[:, bb * 32:(bb + 1) * 32],
                    ibm[:, bb:bb + 1], None, OP.mult, OP.bypass))
            p.do("vector", lambda e: e.tensor_copy(Ivf[:, 0:I3], Iv[:, 0:I3]))
            for bb in range(96):
                dq_v = p.do("vector", lambda e, bb=bb: e.tensor_scalar(
                    stage_bf[:, bb * 32:(bb + 1) * 32], Ivf[:, bb * 32:(bb + 1) * 32],
                    bmx[:, bb:bb + 1], None, OP.mult, OP.bypass))
            p.wait("tensor", "vector", dq_v)
            cpg = {}
            for c in range(24):
                slot = psAb if c % 2 == 0 else psBb
                if c >= 2:
                    p.wait("tensor", "vector", cpg[c - 2])
                p.do("tensor", lambda e, c=c, slot=slot: e.transpose(
                    slot[:, 0:128], stage_bf[:, c * 128:(c + 1) * 128],
                    B["identbf"][:, :]))
                mt = p.mark("tensor")
                p.wait("vector", "tensor", mt)
                dq_v = cpg[c] = p.do("vector", lambda e, c=c, rt=rt, slot=slot: e.tensor_copy(
                    wdT[:, c * 512 + rt * 128:c * 512 + (rt + 1) * 128],
                    slot[:, 0:128]))
        p.wait("tensor", "vector", dq_v)
        for t in range(NT):
            p.waitg("gpsimd", tag + "sb", p.gsem(tag + "sb")[1])
            if prev_v:
                p.wait("gpsimd", "vector", prev_v)
            lm = p.dma("gpsimd", A, oscr[t * 128:(t + 1) * 128, :], lda)
            p.waitg("vector", lda, lm)
            p.do("vector", lambda e: e.tensor_scalar(Iv[:, 0:I3], A, iso_c, None,
                                                     OP.mult, OP.bypass))
            p.do("vector", lambda e: e.tensor_copy(Ivf[:, 0:I3], Iv[:, 0:I3]))
            mv = p.do("vector", lambda e: e.tensor_scalar(stage_bf[:, 0:I3], Ivf[:, 0:I3],
                                                          so_c, None, OP.mult, OP.bypass))
            prev_v = mv
            p.wait("tensor", "vector", mv)
            cpg = {}
            for c in range(24):
                slot = psAb if c % 2 == 0 else psBb
                if c >= 2:
                    p.wait("tensor", "vector", cpg[c - 2])
                p.do("tensor", lambda e, c=c, slot=slot: e.transpose(
                    slot[:, 0:128], stage_bf[:, c * 128:(c + 1) * 128],
                    B["identbf"][:, :]))
                mt = p.mark("tensor")
                p.wait("vector", "tensor", mt)
                mv = cpg[c] = p.do("vector", lambda e, c=c, slot=slot: e.tensor_copy(
                    oqT[:, c * 128:(c + 1) * 128], slot[:, 0:128]))
            p.wait("tensor", "vector", mv)
            ph = B["ps_w"][:, (t % 2) * 512:(t % 2) * 512 + 512]
            if evac_s:
                p.wait("tensor", "scalar", evac_s)
            for c in range(24):
                p.do("tensor", lambda e, c=c, ph=ph: e.matmul(
                    ph, oqT[:, c * 128:(c + 1) * 128], wdT[:, c * 512:(c + 1) * 512],
                    start=(c == 0), stop=(c == 23)))
            mt = p.mark("tensor")
            p.wait("scalar", "tensor", mt)
            if res_add:
                p.wait("scalar", "vector", res_add)
            evac_s = p.do("scalar", lambda e, ph=ph: e.activation(ev0, ph, AF.Identity,
                                                                  scale=coef))
            p.wait("vector", "scalar", evac_s)
            res_add = p.do("vector", lambda e, t=t, half=half: e.tensor_tensor(
                B["x_res"][:, t * D + half * 512:t * D + half * 512 + 512],
                B["x_res"][:, t * D + half * 512:t * D + half * 512 + 512],
                ev0, OP.add))
    return ("vector", res_add)




def emit_l0_ffn(p, B, W, coef):
    E = 16
    CAP = 2560.0
    ln_tiles = emit_ln(p, B)
    idt = p.sb("l0_idt", [128, D], F32)
    for t in range(NT):
        p.do("vector", lambda e, t=t: e.tensor_scalar(
            idt[:, :], B["xn"][:, t * D:(t + 1) * D], coef, None,
            OP.mult, OP.bypass))
        p.do("vector", lambda e, t=t: e.tensor_tensor(
            B["x_res"][:, t * D:(t + 1) * D], B["x_res"][:, t * D:(t + 1) * D],
            idt[:, :], OP.add))
    gwT = p.sb("l0_gwT", [128, KD * E], F32)
    gb = p.sb("l0_gb", [1, E], F32)
    p.dma("sync", gwT[:, :].rearrange("p (c e) -> p c e", c=KD),
          W["router_w"].rearrange("(c p) e -> p c e", p=128), "l0w")
    dm = p.dma("sync", gb[0:1, :], W["router_b"][None, :], "l0w")
    p.waitg("tensor", "l0w", dm)
    gates = p.sb("l0_gates", [128, NT * E], F32)
    ev = emit_router(p, B, gwT, gb[0:1, :], E, gates, ln_tiles, "l0")

    oh1 = p.sb("l0_oh1", [128, NT * E], F32)
    oh2 = p.sb("l0_oh2", [128, NT * E], F32)
    ohs = p.sb("l0_ohs", [128, NT * E], F32)
    comb = p.sb("l0_comb", [128, NT * E], F32)
    vals = p.sb("l0_vals", [128, 8 * NT], F32)
    tmp = p.sb("l0_tmp", [128, 2 * E], F32)
    st = B["stat"]
    m1 = st[:, 9:10]
    zin = st[:, 10:11]
    prev = 0
    for t in range(NT):
        gt = gates[:, t * E:(t + 1) * E]
        o1 = oh1[:, t * E:(t + 1) * E]
        o2 = oh2[:, t * E:(t + 1) * E]
        p.do("vector", lambda e, gt=gt: e.tensor_reduce(m1, gt, AX.X, OP.max))
        p.do("vector", lambda e, gt=gt, o1=o1: e.tensor_scalar(
            o1, gt, m1, None, OP.is_ge, OP.bypass))
        p.do("vector", lambda e, o1=o1: e.tensor_scalar(
            tmp[:, 0:E], o1, -1e30, None, OP.mult, OP.bypass))
        p.do("vector", lambda e, gt=gt: e.tensor_tensor(
            tmp[:, 0:E], tmp[:, 0:E], gt, OP.add))
        p.do("vector", lambda e: e.tensor_reduce(st[:, 11:12], tmp[:, 0:E], AX.X, OP.max))
        p.do("vector", lambda e, o2=o2: e.tensor_scalar(
            o2, tmp[:, 0:E], st[:, 11:12], None, OP.is_ge, OP.bypass))
        p.do("vector", lambda e, o1=o1, o2=o2, t=t: e.tensor_tensor(
            ohs[:, t * E:(t + 1) * E], o1, o2, OP.add))
        mq = p.do("vector", lambda e: e.tensor_scalar(m1, m1, -1.0, None, OP.mult, OP.bypass))
        p.wait("scalar", "vector", mq)
        ms = p.do("scalar", lambda e, gt=gt: e.activation(tmp[:, 0:E], gt, AF.Exp,
                                                          bias=m1, accum_out=zin))
        p.wait("vector", "scalar", ms)
        p.do("vector", lambda e: e.reciprocal(st[:, 11:12], zin))
        p.do("vector", lambda e: e.tensor_scalar(tmp[:, 0:E], tmp[:, 0:E],
                                                 st[:, 11:12], None, OP.mult, OP.bypass))
        p.do("vector", lambda e, o1=o1: e.tensor_tensor(tmp[:, E:2 * E], tmp[:, 0:E],
                                                        o1, OP.mult))
        p.do("vector", lambda e, t=t: e.tensor_reduce(vals[:, t * 8:t * 8 + 1],
                                                      tmp[:, E:2 * E], AX.X, OP.add))
        p.do("vector", lambda e, o2=o2: e.tensor_tensor(tmp[:, E:2 * E], tmp[:, 0:E],
                                                        o2, OP.mult))
        prev = p.do("vector", lambda e, t=t: e.tensor_reduce(vals[:, t * 8 + 1:t * 8 + 2],
                                                             tmp[:, E:2 * E], AX.X, OP.add))

    ps_cnt = B["ps_d"][0:1, 0:E]
    p.wait("tensor", "vector", prev)
    mm = 0
    for t in range(NT):
        mm = p.do("tensor", lambda e, t=t: e.matmul(
            ps_cnt, B["ones_col"][:, 0:1], ohs[:, t * E:(t + 1) * E],
            start=(t == 0), stop=(t == NT - 1)))
    cnt_row = p.sb("l0_cnt", [1, E], F32)
    p.wait("vector", "tensor", mm)
    cm = p.do("vector", lambda e: e.tensor_copy(cnt_row[0:1, :], ps_cnt))
    agin = p.dram("l0ag_in", [1, E], F32)
    agout = p.dram("l0ag_out", [NCORE, E], F32)
    p.wait("gpsimd", "vector", cm)
    am = p.dma("gpsimd", agin[:, :], cnt_row[0:1, :], "l0ag")
    p.waitg("gpsimd", "l0ag", am)
    ccm = p.cc("AllGather", OP.bypass, [agin.ap()], [agout.ap()], grp="l0agc")
    p.waitg("gpsimd", "l0agc", ccm)
    agt = p.sb("l0_agt", [NCORE, E], F32)
    am = p.dma("gpsimd", agt[:, :], agout[:, :], "l0ag")
    iotac = p.sb("l0_iota", [NCORE, 1], I32)
    iotaf = p.sb("l0_iotaf", [NCORE, 1], F32)
    pidf = p.sb("l0_pid", [1, 1], I32)
    pidf32 = p.sb("l0_pidf32", [1, 1], F32)
    cmsk = p.sb("l0_cm", [NCORE, 1], F32)
    offrow = p.sb("l0_off", [1, E], F32)
    gp = p.do("gpsimd", lambda e: e.iota(iotac[:, :], pattern=[[0, 1]], base=0,
                                         channel_multiplier=1))

    def save_pid(e):
        pid = p.nc.partition_id([mybir.EngineType.Pool])
        r = e.to_reg(pid)
        return e.reg_save(pidf[0:1, 0:1], r)
    gp2 = p.do("gpsimd", save_pid)
    p.wait("vector", "gpsimd", gp2)
    p.do("vector", lambda e: e.tensor_copy(iotaf[:, :], iotac[:, :]))
    mv = p.do("vector", lambda e: e.tensor_copy(pidf32[0:1, :], pidf[0:1, :]))
    p.wait("tensor", "vector", mv)
    mmb = p.do("tensor", lambda e: e.matmul(B["ps_c"][0:NCORE, 0:1],
                                            B["ones_row"][0:1, 0:NCORE],
                                            pidf32[0:1, 0:1], start=True, stop=True))
    p.wait("vector", "tensor", mmb)
    p.do("vector", lambda e: e.tensor_copy(cmsk[:, :], B["ps_c"][0:NCORE, 0:1]))
    mv = p.do("vector", lambda e: e.tensor_tensor(cmsk[:, :], iotaf[:, :], cmsk[:, :],
                                                  OP.is_lt))
    p.wait("tensor", "vector", mv)
    p.waitg("tensor", "l0ag", am)
    mmb = p.do("tensor", lambda e: e.matmul(B["ps_c"][0:1, 0:E], cmsk[0:NCORE, 0:1],
                                            agt[0:NCORE, 0:E], start=True, stop=True))
    p.wait("vector", "tensor", mmb)
    roff_m = p.do("vector", lambda e: e.tensor_copy(offrow[0:1, :], B["ps_c"][0:1, 0:E]))

    sut = p.sb("l0_sut", [128, 128], F32)
    p.do("gpsimd", lambda e: e.iota(B["sq"][:, 0:128], pattern=[[1, 128]], base=0,
                                    channel_multiplier=0,
                                    allow_small_or_imprecise_dtypes=True))
    gp3 = p.do("gpsimd", lambda e: e.iota(sut[:, :], pattern=[[0, 128]], base=0,
                                          channel_multiplier=1,
                                          allow_small_or_imprecise_dtypes=True))
    p.wait("vector", "gpsimd", gp3)
    sut_m = p.do("vector", lambda e: e.tensor_tensor(sut[:, :], B["sq"][:, 0:128],
                                                     sut[:, :], OP.is_gt))
    Pg = p.sb("l0_Pg", [128, NT * E], F32)
    p.wait("tensor", "vector", sut_m)
    for t in range(NT):
        p.wait("tensor", "vector", roff_m)
        p.do("tensor", lambda e, t=t: e.matmul(
            B["ps_c"][:, 0:E], sut[:, :], ohs[:, t * E:(t + 1) * E],
            start=True, stop=False))
        mm = p.do("tensor", lambda e: e.matmul(
            B["ps_c"][:, 0:E], B["ones_row"][0:1, 0:128], offrow[0:1, :],
            start=False, stop=True))
        p.wait("vector", "tensor", mm)
        mv = p.do("vector", lambda e, t=t: e.tensor_copy(Pg[:, t * E:(t + 1) * E],
                                                         B["ps_c"][:, 0:E]))
        mm = p.do("tensor", lambda e, t=t: e.matmul(
            B["ps_d"][0:1, 0:E], B["ones_col"][:, 0:1], ohs[:, t * E:(t + 1) * E],
            start=True, stop=True))
        p.wait("vector", "tensor", mm)
        roff_m = p.do("vector", lambda e: e.tensor_tensor(offrow[0:1, :], offrow[0:1, :],
                                                          B["ps_d"][0:1, 0:E], OP.add))

    prev = 0
    for t in range(NT):
        o1 = oh1[:, t * E:(t + 1) * E]
        o2 = oh2[:, t * E:(t + 1) * E]
        Pt = Pg[:, t * E:(t + 1) * E]
        p.do("vector", lambda e, o1=o1, Pt=Pt: e.tensor_tensor(tmp[:, 0:E], o1, Pt, OP.mult))
        p.do("vector", lambda e, t=t: e.tensor_reduce(vals[:, t * 8 + 2:t * 8 + 3],
                                                      tmp[:, 0:E], AX.X, OP.add))
        p.do("vector", lambda e, o2=o2, Pt=Pt: e.tensor_tensor(tmp[:, 0:E], o2, Pt, OP.mult))
        p.do("vector", lambda e, t=t: e.tensor_reduce(vals[:, t * 8 + 3:t * 8 + 4],
                                                      tmp[:, 0:E], AX.X, OP.add))
        p.do("vector", lambda e, t=t: e.tensor_scalar(
            vals[:, t * 8 + 4:t * 8 + 5], vals[:, t * 8 + 2:t * 8 + 3], CAP, None,
            OP.is_lt, OP.bypass))
        p.do("vector", lambda e, t=t: e.tensor_scalar(
            vals[:, t * 8 + 5:t * 8 + 6], vals[:, t * 8 + 3:t * 8 + 4], CAP, None,
            OP.is_lt, OP.bypass))
        p.do("vector", lambda e, t=t: e.tensor_tensor(
            st[:, 11:12], vals[:, t * 8:t * 8 + 1], vals[:, t * 8 + 1:t * 8 + 2], OP.add))
        p.do("vector", lambda e: e.tensor_scalar(st[:, 11:12], st[:, 11:12], 1.0, 1e-9,
                                                 OP.mult, OP.add))
        p.do("vector", lambda e: e.reciprocal(st[:, 12:13], st[:, 11:12]))
        p.do("vector", lambda e, t=t: e.tensor_scalar(
            st[:, 13:14], vals[:, t * 8:t * 8 + 1], st[:, 12:13], None, OP.mult, OP.bypass))
        p.do("vector", lambda e, t=t: e.tensor_tensor(
            st[:, 13:14], st[:, 13:14], vals[:, t * 8 + 4:t * 8 + 5], OP.mult))
        p.do("vector", lambda e: e.tensor_scalar(
            st[:, 13:14], st[:, 13:14], coef, None, OP.mult, OP.bypass))
        p.do("vector", lambda e, t=t: e.tensor_scalar(
            st[:, 14:15], vals[:, t * 8 + 1:t * 8 + 2], st[:, 12:13], None, OP.mult, OP.bypass))
        p.do("vector", lambda e, t=t: e.tensor_tensor(
            st[:, 14:15], st[:, 14:15], vals[:, t * 8 + 5:t * 8 + 6], OP.mult))
        p.do("vector", lambda e: e.tensor_scalar(
            st[:, 14:15], st[:, 14:15], coef, None, OP.mult, OP.bypass))
        p.do("vector", lambda e, o1=o1: e.tensor_scalar(
            tmp[:, 0:E], o1, st[:, 13:14], None, OP.mult, OP.bypass))
        p.do("vector", lambda e, o2=o2: e.tensor_scalar(
            tmp[:, E:2 * E], o2, st[:, 14:15], None, OP.mult, OP.bypass))
        prev = p.do("vector", lambda e, t=t: e.tensor_tensor(
            comb[:, t * E:(t + 1) * E], tmp[:, 0:E], tmp[:, E:2 * E], OP.add))
    comb_done = prev

    xnbf = B["xn"].bitcast(BF16)
    wgA = xnbf[:, 0:6144]
    wuA = xnbf[:, 6144:12288]
    hidT = xnbf[:, 12288:24576]
    dnA = xnbf[:, 24576:30720]
    sq = B["sq"]
    p.wait("gpsimd", "vector", ev)
    subblocks = []
    for e_i in range(E):
        subblocks.append(("e", e_i, 0, 6))
        subblocks.append(("e", e_i, 6, 5))
    subblocks.append(("s", 0, 0, 6))
    subblocks.append(("s", 0, 6, 5))
    evac = {}
    hid_marks = {}
    g_iter = 0
    sq_consumed = 0
    last_dn = 0
    first = True
    gidx = 0
    for kind, e_i, f0, nf in subblocks:
        FW = nf * 128
        if kind == "e":
            gsrc = W["w_gate"][e_i, :, f0 * 128:f0 * 128 + FW]
            usrc = W["w_up"][e_i, :, f0 * 128:f0 * 128 + FW]
            dsrc = W["w_down"][e_i, f0 * 128:f0 * 128 + FW, :]
        else:
            gsrc = W["sw_gate"][:, f0 * 128:f0 * 128 + FW]
            usrc = W["sw_up"][:, f0 * 128:f0 * 128 + FW]
            dsrc = W["sw_down"][f0 * 128:f0 * 128 + FW, :]
        if not first:
            p.wait("gpsimd", "tensor", last_dn)
        p.dma("gpsimd", wgA[:, 0:KD * FW].rearrange("p (c f) -> p c f", c=KD),
              gsrc.rearrange("(c p) f -> p c f", p=128), "l0ew")
        p.dma("gpsimd", wuA[:, 0:KD * FW].rearrange("p (c f) -> p c f", c=KD),
              usrc.rearrange("(c p) f -> p c f", p=128), "l0ew")
        wm = p.dma("gpsimd", dnA[:, 0:nf * D].rearrange("p (c d) -> p c d", c=nf),
                   dsrc.rearrange("(c p) d -> p c d", p=128), "l0ew")
        p.waitg("tensor", "l0ew", wm)
        if first:
            p.wait("tensor", "vector", comb_done)
            first = False
        else:
            p.wait("tensor", "scalar", evac[gidx * NT - 1])
        for ht in range(nf):
            for nb in range(4):
                slot = (g_iter % 2) * 1024
                psG = B["ps_w"][:, slot:slot + 512]
                psU = B["ps_w"][:, slot + 512:slot + 1024]
                if g_iter >= 2:
                    p.wait("tensor", "vector", hid_marks[g_iter - 2])
                for c in range(KD):
                    p.do("tensor", lambda e, c=c, ht=ht, nb=nb, psG=psG, FW=FW: e.matmul(
                        psG, wgA[:, c * FW + ht * 128: c * FW + (ht + 1) * 128],
                        B["xnT"][:, c * T + nb * 512: c * T + (nb + 1) * 512],
                        start=(c == 0), stop=(c == KD - 1)))
                mmx = 0
                for c in range(KD):
                    mmx = p.do("tensor", lambda e, c=c, ht=ht, nb=nb, psU=psU, FW=FW: e.matmul(
                        psU, wuA[:, c * FW + ht * 128: c * FW + (ht + 1) * 128],
                        B["xnT"][:, c * T + nb * 512: c * T + (nb + 1) * 512],
                        start=(c == 0), stop=(c == KD - 1)))
                p.wait("scalar", "tensor", mmx)
                if g_iter >= 1:
                    p.wait("scalar", "vector", max(hid_marks[g_iter - 1], sq_consumed))
                msx = p.do("scalar", lambda e, psG=psG: e.activation(
                    sq[:, 0:512], psG, AF.Silu))
                p.wait("vector", "scalar", msx)
                hid_marks[g_iter] = p.do("vector", lambda e, ht=ht, nb=nb, psU=psU:
                                         e.tensor_tensor(
                    hidT[:, ht * T + nb * 512: ht * T + (nb + 1) * 512],
                    sq[:, 0:512], psU, OP.mult))
                g_iter += 1
        p.wait("tensor", "vector", hid_marks[g_iter - 1])
        for t in range(NT):
            half = (t % 2) * 1024
            ph0 = B["ps_w"][:, half:half + 512]
            ph1 = B["ps_w"][:, half + 512:half + 1024]
            if (gidx * NT + t - 2) in evac:
                p.wait("tensor", "scalar", evac[gidx * NT + t - 2])
            for c in range(nf):
                lhs = hidT[:, c * T + t * 128: c * T + (t + 1) * 128]
                p.do("tensor", lambda e, lhs=lhs, c=c, ph0=ph0: e.matmul(
                    ph0, lhs, dnA[:, c * D: c * D + 512], start=(c == 0), stop=False))
                p.do("tensor", lambda e, lhs=lhs, c=c, ph1=ph1, nf=nf: e.matmul(
                    ph1, lhs, dnA[:, c * D + 512: (c + 1) * D],
                    start=(c == 0), stop=(c == nf - 1)))
            mmx = p.do("tensor", lambda e, ph0=ph0: e.matmul(
                ph0, B["ones_row"][0:1, 0:128], B["zero_row"][0:1, 0:512],
                start=False, stop=True))
            last_dn = mmx
            p.wait("scalar", "tensor", mmx)
            p.wait("scalar", "vector", sq_consumed)
            if kind == "e":
                sc = comb[:, t * E + e_i: t * E + e_i + 1]
                p.do("scalar", lambda e, ph0=ph0, sc=sc: e.activation(
                    sq[:, 0:512], ph0, AF.Identity, scale=sc))
                msx = p.do("scalar", lambda e, ph1=ph1, sc=sc: e.activation(
                    sq[:, 512:1024], ph1, AF.Identity, scale=sc))
            else:
                p.do("scalar", lambda e, ph0=ph0: e.activation(
                    sq[:, 0:512], ph0, AF.Identity, scale=float(coef)))
                msx = p.do("scalar", lambda e, ph1=ph1: e.activation(
                    sq[:, 512:1024], ph1, AF.Identity, scale=float(coef)))
            evac[gidx * NT + t] = msx
            p.wait("vector", "scalar", msx)
            sq_consumed = p.do("vector", lambda e, t=t: e.tensor_tensor(
                B["x_res"][:, t * D:(t + 1) * D], B["x_res"][:, t * D:(t + 1) * D],
                sq[:, 0:D], OP.add))
        gidx += 1
    return ("vector", sq_consumed)




def _bar(p):
    mv, ms, mt = p.mark("vector"), p.mark("scalar"), p.mark("tensor")
    p.wait("tensor", "vector", mv)
    p.wait("tensor", "scalar", ms)
    p.wait("vector", "tensor", mt)
    p.wait("scalar", "tensor", mt)
    p.wait("vector", "scalar", ms)
    p.wait("scalar", "vector", mv)


def emit_l3_attn(p, B, W, coef, x_in, x_out):
    assert coef == 1.0
    NH, HD, DS, DI = 32, 64, 64, 2048
    NXT = 18
    CW = NXT * 128
    SCW = 256
    NSC = T // SCW
    stat, sq = B["stat"], B["sq"]
    psA = B["ps_a"].bitcast(BF16)
    psB = B["ps_b"].bitcast(BF16)

    xnT = p.sb("a3_xnT", [128, KD * T], BF16)
    Fb = p.sb("a3_F", [128, 16 * 1024], BF16)
    WZ = p.sb("a3_WZ", [128, KD * DI], BF16)
    WX = p.sb("a3_WX", [128, KD * CW], BF16)
    Wdt = p.sb("a3_Wdt", [128, KD * 32], BF16)
    xbct = p.sb("a3_xbct", [128, NXT * SCW], BF16)
    ybuf = p.sb("a3_y", [128, DI], F32)
    Xtok = p.sb("a3_Xtok", [128, DI], BF16)
    xw = p.sb("a3_xw", [128, DI], BF16)
    xtile = p.sb("a3_xtile", [128, D], F32)
    obuf = p.sb("a3_obuf", [128, D], BF16)
    hT = p.sb("a3_hT", [64, DI], F32)
    xnb = p.sb("a3_xnb", [128, D], BF16)
    catp = sq[:, 512:512 + SCW + 3]
    tails = [p.sb("a3_tailA", [128, NXT * 3], F32), p.sb("a3_tailB", [128, NXT * 3], F32)]
    UT1 = p.sb("a3_UT1", [128, 128], F32)
    LTS = p.sb("a3_LTS", [128, 128], F32)
    Rbf = p.sb("a3_Rbf", [128, 128], BF16)
    Ssb = p.sb("a3_S", [128, 128], F32)
    e1 = p.sb("a3_e1", [128, 128], F32)
    Mbf = p.sb("a3_Mbf", [128, 128], BF16)
    Btok = p.sb("a3_Btok", [128, 128], BF16)
    CTf = p.sb("a3_CTf", [64, 128], F32)
    cumA = p.sb("a3_cumA", [128, 32], F32)
    cumAT = p.sb("a3_cumAT", [32, 128], F32)
    ones32 = p.sb("a3_ones32", [32, 128], F32)
    amat = p.sb("a3_amat", [128, 32], F32)
    wmat = p.sb("a3_wmat", [128, 32], F32)
    gam = p.sb("a3_gam", [128, 32], F32)
    w32 = p.sb("a3_w32", [128, 128], F32)
    dtc = p.sb("a3_dtc", [128, 64], F32)
    glc = p.sb("a3_glc", [32, 1], F32)
    gLb = p.sb("a3_gLb", [64, 32], F32)
    Lh = p.sb("a3_Lh", [32, 128], F32)
    Abc = p.sb("a3_Abc", [128, 32], F32)
    arow = p.sb("a3_arow", [1, 32], F32)
    cwr = p.sb("a3_cwr", [128, NXT * 4], F32)
    cbr = p.sb("a3_cbr", [128, NXT], F32)
    r32a = p.sb("a3_r32a", [1, 32], F32)
    rD = p.sb("a3_rD", [1, DI], BF16)
    rnw = p.sb("a3_rnw", [1, DI], BF16)
    rbo = p.sb("a3_rbo", [1, D], BF16)
    onesbf = p.sb("a3_onesbf", [1, 128], BF16)
    tA = p.sb("a3_tA", [128, 128], BF16)
    tB = p.sb("a3_tB", [128, 128], BF16)
    psl = [B["ps_a"], B["ps_b"], B["ps_c"], B["ps_d"]]

    p.wait("gpsimd", "vector", B["init_v"])
    p.do("gpsimd", lambda e: e.memset(ones32[:, :], 1.0))
    p.do("gpsimd", lambda e: e.memset(tails[0][:, :], 0.0))
    p.do("gpsimd", lambda e: e.memset(tails[1][:, :], 0.0))
    p.do("gpsimd", lambda e: e.iota(sq[:, 0:128], pattern=[[1, 128]], base=0,
                                    channel_multiplier=0, allow_small_or_imprecise_dtypes=True))
    p.do("gpsimd", lambda e: e.iota(UT1[:, :], pattern=[[0, 128]], base=0,
                                    channel_multiplier=1, allow_small_or_imprecise_dtypes=True))
    p.do("gpsimd", lambda e: e.memset(onesbf[:, :], 1.0))
    gm = p.mark("gpsimd")
    p.wait("vector", "gpsimd", gm)
    p.do("vector", lambda e: e.tensor_tensor(LTS[:, :], sq[:, 0:128], UT1[:, :], OP.is_lt))
    p.do("vector", lambda e: e.tensor_tensor(w32[:, :], sq[:, 0:128], UT1[:, :], OP.add))
    p.do("vector", lambda e: e.tensor_scalar(w32[:, :], w32[:, :], 127.0, None,
                                             OP.is_equal, OP.bypass))
    p.do("vector", lambda e: e.tensor_copy(Rbf[:, :], w32[:, :]))
    p.do("vector", lambda e: e.tensor_tensor(UT1[:, :], sq[:, 0:128], UT1[:, :], OP.is_ge))
    _bar(p)

    for t in range(NT):
        dm = p.dma("sync", xtile[:, :], x_in.rearrange("(t p) d -> p t d", p=128)[:, t, :],
                   "l3x")
        p.waitg("vector", "l3x", dm)
        p.waitg("scalar", "l3x", dm)
        xt = xtile[:, :]
        p.do("vector", lambda e, xt=xt: e.tensor_reduce(stat[:, 0:1], xt, AX.X, OP.add))
        m = p.do("vector", lambda e: e.tensor_scalar(stat[:, 1:2], stat[:, 0:1],
                                                     -1.0 / D, None, OP.mult, OP.bypass))
        p.wait("scalar", "vector", m)
        m = p.do("scalar", lambda e, xt=xt: e.activation(sq[:, 0:D], xt, AF.Square,
                                                         bias=stat[:, 1:2], scale=1.0,
                                                         accum_out=stat[:, 2:3]))
        p.wait("vector", "scalar", m)
        m = p.do("vector", lambda e: e.tensor_scalar(stat[:, 3:4], stat[:, 2:3],
                                                     1.0 / D, LN_EPS, OP.mult, OP.add))
        p.wait("scalar", "vector", m)
        p.do("scalar", lambda e: e.activation(stat[:, 4:5], stat[:, 3:4], AF.Ln))
        m = p.do("scalar", lambda e: e.activation(stat[:, 5:6], stat[:, 4:5], AF.Exp, scale=-0.5))
        p.wait("vector", "scalar", m)
        m = p.do("vector", lambda e: e.tensor_tensor(stat[:, 6:7], stat[:, 1:2],
                                                     stat[:, 5:6], OP.mult))
        p.wait("scalar", "vector", m)
        m = p.do("scalar", lambda e, xt=xt: e.activation(
            xnb[:, :], xt, AF.Identity, bias=stat[:, 6:7], scale=stat[:, 5:6]))
        p.wait("vector", "scalar", m)
        p.wait("tensor", "scalar", m)
        cps = []
        for c in range(KD):
            slot = psA if c % 2 == 0 else psB
            if c >= 2:
                p.wait("tensor", "vector", cps[c - 2])
            p.do("tensor", lambda e, c=c, slot=slot: e.transpose(
                slot[:, 0:128], xnb[:, c * 128:(c + 1) * 128], B["identbf"][:, :]))
            p.wait("vector", "tensor", p.mark("tensor"))
            m = p.do("vector", lambda e, t=t, c=c, slot=slot: e.tensor_copy(
                xnT[:, c * T + t * 128: c * T + (t + 1) * 128], slot[:, 0:128]))
            cps.append(m)
        _bar(p)
        p.wait("sync", "vector", cps[-1])
        p.wait("sync", "scalar", p.mark("scalar"))

    store_marks = []
    for di in range(2):
        pf = "f" if di == 0 else "b"
        _bar(p)
        if di == 1:
            p.wait("gpsimd", "vector", p.mark("vector"))
            gm = p.do("gpsimd", lambda e: e.memset(hT[:, :], 0.0))
            p.do("gpsimd", lambda e: e.memset(tails[0][:, :], 0.0), guard=False)
            p.do("gpsimd", lambda e: e.memset(tails[1][:, :], 0.0), guard=False)
            p.wait("vector", "gpsimd", gm + 2)
            for c in range(KD):
                for b in range(NT // 2):
                    s1 = xnT[:, c * T + b * 128: c * T + (b + 1) * 128]
                    s2 = xnT[:, c * T + (15 - b) * 128: c * T + (16 - b) * 128]
                    p.do("tensor", lambda e, s1=s1: e.transpose(psA[:, 0:128], s1,
                                                                B["identbf"][:, :]))
                    p.do("tensor", lambda e, s2=s2: e.transpose(psB[:, 0:128], s2,
                                                                B["identbf"][:, :]))
                    p.wait("vector", "tensor", p.mark("tensor"))
                    p.do("vector", lambda e: e.tensor_copy(tA[:, :], psA[:, 0:128]))
                    m = p.do("vector", lambda e: e.tensor_copy(tB[:, :], psB[:, 0:128]))
                    p.wait("tensor", "vector", m)
                    p.do("tensor", lambda e: e.matmul(B["ps_c"][:, 0:128], tA[:, :],
                                                      Rbf[:, :], start=True, stop=True))
                    mm = p.do("tensor", lambda e: e.matmul(B["ps_d"][:, 0:128], tB[:, :],
                                                           Rbf[:, :], start=True, stop=True))
                    p.wait("vector", "tensor", mm)
                    p.do("vector", lambda e, s2=s2: e.tensor_copy(s2, B["ps_c"][:, 0:128]))
                    m = p.do("vector", lambda e, s1=s1: e.tensor_copy(s1, B["ps_d"][:, 0:128]))
                    p.wait("tensor", "vector", m)
            _bar(p)
        else:
            gm = p.do("gpsimd", lambda e: e.memset(hT[:, :], 0.0))
            p.wait("vector", "gpsimd", gm)

        p.wait("gpsimd", "tensor", p.mark("tensor"))
        p.wait("gpsimd", "vector", p.mark("vector"))
        p.wait("sync", "tensor", p.mark("tensor"))
        p.wait("sync", "vector", p.mark("vector"))
        p.wait("sync", "scalar", p.mark("scalar"))
        p.dma("gpsimd", WZ[:, :].rearrange("p (c n) -> p c n", c=KD),
              W["woutT" + pf].rearrange("(c p) n -> p c n", p=128), "l3fw")
        dm = p.dma("gpsimd", WX[:, 0:KD * 1024].rearrange("p (c n) -> p c n", c=KD),
                   W["wo" + ("A" if di == 0 else "B")].rearrange("(c p) n -> p c n", p=128),
                   "l3fw")
        p.waitg("tensor", "l3fw", dm)
        fcp = []
        for i in range(16):
            for hf in range(2):
                g = i * 2 + hf
                slot = B["ps_c"] if g % 2 == 0 else B["ps_d"]
                if g >= 2:
                    p.wait("tensor", "scalar", fcp[g - 2])
                mm = 0
                for c in range(KD):
                    mm = p.do("tensor", lambda e, c=c, i=i, hf=hf, slot=slot: e.matmul(
                        slot[:, 0:512], WZ[:, c * DI + i * 128: c * DI + (i + 1) * 128],
                        WX[:, c * 1024 + hf * 512: c * 1024 + hf * 512 + 512],
                        start=(c == 0), stop=(c == KD - 1)))
                p.wait("scalar", "tensor", mm)
                m = p.do("scalar", lambda e, i=i, hf=hf, slot=slot: e.activation(
                    Fb[:, i * 1024 + hf * 512: i * 1024 + hf * 512 + 512],
                    slot[:, 0:512], AF.Identity))
                fcp.append(m)
        _bar(p)
        p.wait("gpsimd", "scalar", fcp[-1])

        p.dma("gpsimd", WZ[:, :].rearrange("p (c n) -> p c n", c=KD),
              W["winz" + pf].rearrange("(c p) n -> p c n", p=128), "l3rw")
        p.dma("gpsimd", WX[:, :].rearrange("p (c n) -> p c n", c=KD),
              W["winx" + pf].rearrange("(c p) n -> p c n", p=128), "l3rw")
        p.dma("gpsimd", Wdt[:, :].rearrange("p (c n) -> p c n", c=KD),
              W["windt" + pf].rearrange("(c p) n -> p c n", p=128), "l3rw")
        p.dma("sync", cwr[:, :], W["cw" + pf][:, :], "l3rs")
        p.dma("sync", cbr[:, :], W["cb" + pf][:, :], "l3rs")
        p.dma("sync", r32a[0:1, :], W["dtb" + pf][:, :], "l3rs")
        dms = p.dma("sync", arow[0:1, :], W["alog" + pf][:, :], "l3rs")
        p.dma("gpsimd", rD[0:1, :], W["D" + pf][:, :], "l3rw")
        dm = p.dma("gpsimd", rnw[0:1, :], W["nw" + pf][:, :], "l3rw")
        if di == 0:
            dm = p.dma("gpsimd", rbo[0:1, :], W["bo"][:, :], "l3rw")
        p.waitg("scalar", "l3rw", dm)
        p.waitg("vector", "l3rw", dm)
        p.waitg("tensor", "l3rw", dm)
        p.waitg("scalar", "l3rs", dms)
        p.waitg("vector", "l3rs", dms)
        p.waitg("tensor", "l3rs", dms)
        m = p.do("scalar", lambda e: e.activation(arow[0:1, :], arow[0:1, :], AF.Exp,
                                                  scale=1.0))
        p.wait("vector", "scalar", m)
        m = p.do("vector", lambda e: e.tensor_scalar(arow[0:1, :], arow[0:1, :], -1.0,
                                                     None, OP.mult, OP.bypass))
        p.wait("tensor", "vector", m)
        mm = p.do("tensor", lambda e: e.matmul(B["ps_c"][:, 0:32], B["ones_row"][0:1, 0:128],
                                               arow[0:1, :], start=True, stop=True))
        p.wait("vector", "tensor", mm)
        p.do("vector", lambda e: e.tensor_copy(Abc[:, :], B["ps_c"][:, 0:32]))
        _bar(p)

        for sc in range(NSC):
            prc = []
            for i in range(NXT):
                slot = B["ps_w"][:, (i % 4) * 512: (i % 4) * 512 + SCW]
                if i >= 4:
                    p.wait("tensor", "scalar", prc[i - 4])
                mm = 0
                for c in range(KD):
                    mm = p.do("tensor", lambda e, c=c, i=i, sc=sc, slot=slot: e.matmul(
                        slot, WX[:, c * CW + i * 128: c * CW + (i + 1) * 128],
                        xnT[:, c * T + sc * SCW: c * T + (sc + 1) * SCW],
                        start=(c == 0), stop=(c == KD - 1)))
                p.wait("scalar", "tensor", mm)
                m = p.do("scalar", lambda e, i=i, slot=slot: e.activation(
                    xbct[:, i * SCW:(i + 1) * SCW], slot, AF.Identity))
                prc.append(m)
            _bar(p)
            tailO = tails[sc % 2]
            tailN = tails[1 - sc % 2]
            p.do("vector", lambda e, tailN=tailN: e.tensor_copy(
                tailN[:, :].rearrange("p (i k) -> p i k", i=NXT),
                xbct[:, :].rearrange("p (i w) -> p i w", i=NXT)[:, :, SCW - 3:SCW]))
            for i in range(NXT):
                p.do("vector", lambda e, i=i, tailO=tailO: e.tensor_copy(
                    catp[:, 0:3], tailO[:, i * 3:(i + 1) * 3]))
                p.do("vector", lambda e, i=i: e.tensor_copy(
                    catp[:, 3:3 + SCW], xbct[:, i * SCW:(i + 1) * SCW]))
                p.do("vector", lambda e, i=i: e.tensor_scalar(
                    sq[:, 0:SCW], catp[:, 0:SCW], cwr[:, i * 4: i * 4 + 1], None,
                    OP.mult, OP.bypass))
                for k in range(1, 4):
                    p.do("vector", lambda e, i=i, k=k: e.tensor_scalar(
                        sq[:, SCW:2 * SCW], catp[:, k:k + SCW], cwr[:, i * 4 + k: i * 4 + k + 1],
                        None, OP.mult, OP.bypass))
                    p.do("vector", lambda e: e.tensor_tensor(
                        sq[:, 0:SCW], sq[:, 0:SCW], sq[:, SCW:2 * SCW], OP.add))
                m = p.do("vector", lambda e, i=i: e.tensor_scalar(
                    sq[:, 0:SCW], sq[:, 0:SCW], cbr[:, i:i + 1], None, OP.add, OP.bypass))
                p.wait("scalar", "vector", m)
                m = p.do("scalar", lambda e, i=i: e.activation(
                    xbct[:, i * SCW:(i + 1) * SCW], sq[:, 0:SCW], AF.Silu))
                p.wait("vector", "scalar", m)
            _bar(p)
            for qq in range(2):
                mm = 0
                for c in range(KD):
                    p.do("tensor", lambda e, c=c, qq=qq, sc=sc: e.matmul(
                        B["ps_c"][:, 0:32],
                        xnT[:, c * T + sc * SCW + qq * 128: c * T + sc * SCW + (qq + 1) * 128],
                        Wdt[:, c * 32:(c + 1) * 32], start=(c == 0), stop=False))
                mm = p.do("tensor", lambda e: e.matmul(
                    B["ps_c"][:, 0:32], B["ones_row"][0:1, 0:128], r32a[0:1, :],
                    start=False, stop=True))
                p.wait("vector", "tensor", mm)
                p.do("vector", lambda e: e.tensor_copy(w32[:, 0:32], B["ps_c"][:, 0:32]))
                m = p.do("vector", lambda e: e.tensor_scalar(w32[:, 32:64], w32[:, 0:32],
                                                             0.0, None, OP.max, OP.bypass))
                p.wait("scalar", "vector", m)
                m = p.do("scalar", lambda e: e.activation(w32[:, 64:96], w32[:, 0:32], AF.Abs))
                m = p.do("scalar", lambda e: e.activation(w32[:, 64:96], w32[:, 64:96],
                                                          AF.Exp, scale=-1.0))
                p.wait("vector", "scalar", m)
                m = p.do("vector", lambda e: e.tensor_scalar(w32[:, 64:96], w32[:, 64:96],
                                                             1.0, None, OP.add, OP.bypass))
                p.wait("scalar", "vector", m)
                m = p.do("scalar", lambda e: e.activation(w32[:, 64:96], w32[:, 64:96], AF.Ln))
                p.wait("vector", "scalar", m)
                m = p.do("vector", lambda e, qq=qq: e.tensor_tensor(
                    dtc[:, qq * 32:(qq + 1) * 32], w32[:, 32:64], w32[:, 64:96], OP.add))
                p.wait("tensor", "vector", m)

            for qi in range(2):
                q = sc * 2 + qi
                dtq = dtc[:, qi * 32:(qi + 1) * 32]
                if store_marks:
                    p.waitg("sync", "l3o", store_marks[-1])
                if di == 0:
                    xdm = p.dma("sync", xtile[:, :],
                                x_in.rearrange("(t p) d -> p t d", p=128)[:, q, :], "l3x")
                else:
                    xdm = p.dma("sync", xtile[:, :],
                                x_out.rearrange("(t p) d -> p t d", p=128)[:, 15 - q, :],
                                "l3x")
                p.do("vector", lambda e, dtq=dtq: e.tensor_tensor(amat[:, :], dtq,
                                                                  Abc[:, :], OP.mult))
                _bar(p)
                p.do("tensor", lambda e: e.matmul(B["ps_b"][:, 0:32], UT1[:, :], amat[:, :],
                                                  start=True, stop=True))
                p.do("tensor", lambda e: e.matmul(B["ps_b"][:, 32:64], LTS[:, :], amat[:, :],
                                                  start=True, stop=True))
                p.do("tensor", lambda e: e.matmul(B["ps_b"][0:32, 64:65], amat[:, :],
                                                  B["ones_col"][:, 0:1], start=True, stop=True))
                _bar(p)
                p.do("vector", lambda e: e.tensor_copy(cumA[:, :], B["ps_b"][:, 0:32]))
                m = p.do("vector", lambda e: e.tensor_copy(wmat[:, :], B["ps_b"][:, 32:64]))
                p.wait("scalar", "vector", m)
                p.do("scalar", lambda e: e.activation(gam[:, :], cumA[:, :], AF.Exp))
                p.do("scalar", lambda e: e.activation(wmat[:, :], wmat[:, :], AF.Exp))
                m = p.do("scalar", lambda e: e.activation(glc[:, 0:1], B["ps_b"][0:32, 64:65],
                                                          AF.Exp))
                p.wait("vector", "scalar", m)
                m = p.do("vector", lambda e, dtq=dtq: e.tensor_tensor(wmat[:, :], wmat[:, :],
                                                                      dtq, OP.mult))
                _bar(p)
                p.do("tensor", lambda e: e.transpose(B["ps_b"][0:32, 128:256], cumA[:, :],
                                                     B["ident32"][:, :]))
                _bar(p)
                p.do("vector", lambda e: e.tensor_copy(cumAT[:, :], B["ps_b"][0:32, 128:256]))
                p.do("tensor", lambda e, qi=qi: e.transpose(
                    psA[:, 0:128], xbct[:, 16 * SCW + qi * 128: 16 * SCW + (qi + 1) * 128],
                    B["identbf"][:, :]))
                mm = p.do("tensor", lambda e, qi=qi: e.matmul(
                    B["ps_c"][:, 0:128],
                    xbct[0:64, 16 * SCW + qi * 128: 16 * SCW + (qi + 1) * 128],
                    xbct[0:64, 17 * SCW + qi * 128: 17 * SCW + (qi + 1) * 128],
                    start=True, stop=True))
                p.wait("vector", "tensor", mm)
                p.do("vector", lambda e: e.tensor_copy(Btok[:, :], psA[:, 0:128]))
                p.do("vector", lambda e: e.tensor_copy(Ssb[:, :], B["ps_c"][:, 0:128]))
                p.do("vector", lambda e, qi=qi: e.tensor_copy(
                    CTf[:, :], xbct[0:64, 17 * SCW + qi * 128: 17 * SCW + (qi + 1) * 128]))
                cps = []
                _bar(p)
                for i in range(16):
                    slot = psA if i % 2 == 0 else psB
                    if i >= 2:
                        p.wait("tensor", "vector", cps[i - 2])
                    p.do("tensor", lambda e, i=i, qi=qi, slot=slot: e.transpose(
                        slot[:, 0:128], xbct[:, i * SCW + qi * 128: i * SCW + (qi + 1) * 128],
                        B["identbf"][:, :]))
                    p.wait("vector", "tensor", p.mark("tensor"))
                    m = p.do("vector", lambda e, i=i, slot=slot: e.tensor_copy(
                        Xtok[:, i * 128:(i + 1) * 128], slot[:, 0:128]))
                    cps.append(m)
                _bar(p)
                for j in range(4):
                    p.do("tensor", lambda e, j=j: e.matmul(
                        psl[j][:, 0:512], CTf[:, :], hT[:, j * 512:(j + 1) * 512],
                        start=True, stop=True))
                _bar(p)
                for h in range(NH):
                    j, jj = h // 8, h % 8
                    p.do("vector", lambda e, h=h, j=j, jj=jj: e.tensor_scalar(
                        ybuf[:, h * 64:(h + 1) * 64], psl[j][:, jj * 64:(jj + 1) * 64],
                        gam[:, h:h + 1], None, OP.mult, OP.bypass))
                _bar(p)
                for h in range(NH):
                    m = p.do("vector", lambda e, h=h: e.tensor_scalar(
                        Lh[:, :], ones32[:, :], B["ident32"][0:32, h:h + 1], None,
                        OP.mult, OP.bypass))
                    p.wait("tensor", "vector", m)
                    mm = p.do("tensor", lambda e: e.matmul(
                        B["ps_a"][:, 0:128], Lh[:, :], cumAT[:, :],
                        start=True, stop=True))
                    p.wait("vector", "tensor", mm)
                    p.do("vector", lambda e, h=h: e.tensor_scalar(
                        e1[:, :], B["ps_a"][:, 0:128], cumA[:, h:h + 1], None,
                        OP.subtract, OP.bypass))
                    m = p.do("vector", lambda e: e.tensor_tensor(e1[:, :], e1[:, :],
                                                                 UT1[:, :], OP.mult))
                    p.wait("scalar", "vector", m)
                    m = p.do("scalar", lambda e: e.activation(e1[:, :], e1[:, :], AF.Exp))
                    p.wait("vector", "scalar", m)
                    p.do("vector", lambda e: e.tensor_tensor(e1[:, :], e1[:, :],
                                                             UT1[:, :], OP.mult))
                    p.do("vector", lambda e, h=h, dtq=dtq: e.tensor_scalar(
                        e1[:, :], e1[:, :], dtq[:, h:h + 1], None, OP.mult, OP.bypass))
                    m = p.do("vector", lambda e: e.tensor_tensor(Mbf[:, :], e1[:, :],
                                                                 Ssb[:, :], OP.mult))
                    p.wait("tensor", "vector", m)
                    p.do("tensor", lambda e, h=h: e.matmul(
                        B["ps_w"][:, h * 64:(h + 1) * 64], Mbf[:, :],
                        Xtok[:, h * 64:(h + 1) * 64], start=True, stop=True))
                _bar(p)
                for j in range(4):
                    p.do("vector", lambda e, j=j: e.tensor_tensor(
                        ybuf[:, j * 512:(j + 1) * 512], ybuf[:, j * 512:(j + 1) * 512],
                        B["ps_w"][:, j * 512:(j + 1) * 512], OP.add))
                _bar(p)
                for j in range(4):
                    p.do("tensor", lambda e, j=j: e.matmul(
                        psl[j][:, 0:512], onesbf[0:1, :],
                        rD[0:1, j * 512:(j + 1) * 512], start=True, stop=True))
                _bar(p)
                for j in range(4):
                    p.do("vector", lambda e, j=j: e.tensor_tensor(
                        sq[:, 0:512], psl[j][:, 0:512], Xtok[:, j * 512:(j + 1) * 512],
                        OP.mult))
                    p.do("vector", lambda e, j=j: e.tensor_tensor(
                        ybuf[:, j * 512:(j + 1) * 512], ybuf[:, j * 512:(j + 1) * 512],
                        sq[:, 0:512], OP.add))
                _bar(p)
                p.do("vector", lambda e: e.tensor_scalar(
                    w32[0:32, 96:128], B["ident32"][0:32, 0:32], glc[:, 0:1], None,
                    OP.mult, OP.bypass))
                for h in range(NH):
                    p.do("vector", lambda e, h=h: e.tensor_scalar(
                        xw[:, h * 64:(h + 1) * 64], Xtok[:, h * 64:(h + 1) * 64],
                        wmat[:, h:h + 1], None, OP.mult, OP.bypass))
                _bar(p)
                p.do("tensor", lambda e: e.matmul(
                    B["ps_a"][0:64, 0:32], ones32[:, 0:64], w32[0:32, 96:128],
                    start=True, stop=True))
                _bar(p)
                p.do("vector", lambda e: e.tensor_copy(gLb[:, :], B["ps_a"][0:64, 0:32]))
                _bar(p)
                for h in range(NH):
                    p.do("vector", lambda e, h=h: e.tensor_scalar(
                        hT[:, h * 64:(h + 1) * 64], hT[:, h * 64:(h + 1) * 64],
                        gLb[:, h:h + 1], None, OP.mult, OP.bypass))
                _bar(p)
                for j in range(4):
                    p.do("tensor", lambda e, j=j: e.matmul(
                        psl[j][0:64, 0:512], Btok[:, 0:64], xw[:, j * 512:(j + 1) * 512],
                        start=True, stop=True))
                _bar(p)
                for j in range(4):
                    p.do("vector", lambda e, j=j: e.tensor_tensor(
                        hT[:, j * 512:(j + 1) * 512], hT[:, j * 512:(j + 1) * 512],
                        psl[j][0:64, 0:512], OP.add))
                _bar(p)
                tok0 = sc * SCW + qi * 128
                for j in range(4):
                    mm = 0
                    for c in range(KD):
                        mm = p.do("tensor", lambda e, c=c, j=j, tok0=tok0: e.matmul(
                            psl[j][:, 0:512], xnT[:, c * T + tok0: c * T + tok0 + 128],
                            WZ[:, c * DI + j * 512: c * DI + (j + 1) * 512],
                            start=(c == 0), stop=(c == KD - 1)))
                _bar(p)
                for j in range(4):
                    m = p.do("scalar", lambda e, j=j: e.activation(sq[:, 0:512],
                                                                   psl[j][:, 0:512], AF.Silu))
                    p.wait("vector", "scalar", m)
                    m = p.do("vector", lambda e, j=j: e.tensor_tensor(
                        ybuf[:, j * 512:(j + 1) * 512], ybuf[:, j * 512:(j + 1) * 512],
                        sq[:, 0:512], OP.mult))
                    p.wait("scalar", "vector", m)
                p.do("scalar", lambda e: e.activation(sq[:, 0:D], ybuf[:, 0:D], AF.Square,
                                                      accum_out=stat[:, 0:1]))
                m = p.do("scalar", lambda e: e.activation(sq[:, 0:D], ybuf[:, D:2 * D],
                                                          AF.Square, accum_out=stat[:, 1:2]))
                p.wait("vector", "scalar", m)
                p.do("vector", lambda e: e.tensor_tensor(stat[:, 2:3], stat[:, 0:1],
                                                         stat[:, 1:2], OP.add))
                m = p.do("vector", lambda e: e.tensor_scalar(stat[:, 3:4], stat[:, 2:3],
                                                             1.0 / DI, 1e-5, OP.mult, OP.add))
                p.wait("scalar", "vector", m)
                p.do("scalar", lambda e: e.activation(stat[:, 4:5], stat[:, 3:4], AF.Ln))
                m = p.do("scalar", lambda e: e.activation(stat[:, 5:6], stat[:, 4:5],
                                                          AF.Exp, scale=-0.5))
                p.wait("vector", "scalar", m)
                p.do("vector", lambda e: e.tensor_scalar(ybuf[:, :], ybuf[:, :],
                                                         stat[:, 5:6], None, OP.mult,
                                                         OP.bypass))
                _bar(p)
                for j in range(4):
                    p.do("tensor", lambda e, j=j: e.matmul(
                        psl[j][:, 0:512], onesbf[0:1, :],
                        rnw[0:1, j * 512:(j + 1) * 512], start=True, stop=True))
                _bar(p)
                for j in range(4):
                    p.do("vector", lambda e, j=j: e.tensor_tensor(
                        ybuf[:, j * 512:(j + 1) * 512], ybuf[:, j * 512:(j + 1) * 512],
                        psl[j][:, 0:512], OP.mult))
                _bar(p)
                cps = []
                for i in range(16):
                    slot = B["ps_a"] if i % 2 == 0 else B["ps_b"]
                    if i >= 2:
                        p.wait("tensor", "vector", cps[i - 2])
                    p.do("tensor", lambda e, i=i, slot=slot: e.transpose(
                        slot[:, 0:128], ybuf[:, i * 128:(i + 1) * 128], B["ident32"][:, :]))
                    p.wait("vector", "tensor", p.mark("tensor"))
                    m = p.do("vector", lambda e, i=i, slot=slot: e.tensor_copy(
                        xw[:, i * 128:(i + 1) * 128], slot[:, 0:128]))
                    cps.append(m)
                _bar(p)
                for hf in range(2):
                    for i in range(16):
                        p.do("tensor", lambda e, i=i, hf=hf, di=di: e.matmul(
                            B["ps_w"][:, hf * 512:(hf + 1) * 512], xw[:, i * 128:(i + 1) * 128],
                            Fb[:, i * 1024 + hf * 512: i * 1024 + (hf + 1) * 512],
                            start=(i == 0), stop=(i == 15 and di == 1)))
                    if di == 0:
                        p.do("tensor", lambda e, hf=hf: e.matmul(
                            B["ps_w"][:, hf * 512:(hf + 1) * 512], onesbf[0:1, :],
                            rbo[0:1, hf * 512:(hf + 1) * 512], start=False, stop=True))
                _bar(p)
                p.waitg("vector", "l3x", xdm)
                if di == 0:
                    for hf in range(2):
                        p.do("vector", lambda e, hf=hf: e.tensor_tensor(
                            xtile[:, hf * 512:(hf + 1) * 512], xtile[:, hf * 512:(hf + 1) * 512],
                            B["ps_w"][:, hf * 512:(hf + 1) * 512], OP.add))
                    m = p.mark("vector")
                    p.wait("sync", "vector", m)
                    sm = p.dma("sync", x_out.rearrange("(t p) d -> p t d", p=128)[:, q, :],
                               xtile[:, :], "l3o")
                    store_marks.append(sm)
                else:
                    m = 0
                    for hf in range(2):
                        m = p.do("scalar", lambda e, hf=hf: e.activation(
                            obuf[:, hf * 512:(hf + 1) * 512], B["ps_w"][:, hf * 512:(hf + 1) * 512],
                            AF.Identity))
                    p.wait("tensor", "scalar", m)
                    for hf in range(2):
                        p.do("tensor", lambda e, hf=hf: e.matmul(
                            psl[hf][:, 0:512], Rbf[:, :].bitcast(BF16),
                            obuf[:, hf * 512:(hf + 1) * 512], start=True, stop=True))
                    _bar(p)
                    for hf in range(2):
                        p.do("vector", lambda e, hf=hf: e.tensor_tensor(
                            xtile[:, hf * 512:(hf + 1) * 512], xtile[:, hf * 512:(hf + 1) * 512],
                            psl[hf][:, 0:512], OP.add))
                    m = p.mark("vector")
                    p.wait("sync", "vector", m)
                    sm = p.dma("sync",
                               x_out.rearrange("(t p) d -> p t d", p=128)[:, 15 - q, :],
                               xtile[:, :], "l3o")
                    store_marks.append(sm)
                _bar(p)
        if di == 0:
            p.waitg("sync", "l3o", store_marks[-1])
            p.waitg("vector", "l3o", store_marks[-1])
    return ("sync", store_marks[-1])




def emit_l2_attn(p, B, W, coef, x_in, x_out):
    assert coef == 1.0
    NH, DK, DV = 12, 64, 128
    KDIM, VDIM = NH * DK, NH * DV
    NQT = 24
    CW3 = NQT * 128
    SCW = 256
    NSC = T // SCW
    SCALE = DK ** (-0.5)
    stat, sq = B["stat"], B["sq"]
    psA = B["ps_a"].bitcast(BF16)
    psB = B["ps_b"].bitcast(BF16)
    psl = [B["ps_a"], B["ps_b"], B["ps_c"], B["ps_d"]]

    xnT = p.sb("a2_xnT", [128, KD * T], BF16)
    WQKV = p.sb("a2_WQKV", [128, KD * CW3], BF16)
    WG = p.sb("a2_WG", [128, KD * VDIM], BF16)
    WO = p.sb("a2_WO", [128, NH * 1024], BF16)
    Wgkb = p.sb("a2_Wgkb", [128, KD * 24], BF16)
    xqkv = p.sb("a2_xqkv", [128, NQT * SCW], BF16)
    qtok = p.sb("a2_qtok", [128, KDIM], BF16)
    ktok = p.sb("a2_ktok", [128, KDIM], BF16)
    vtok = p.sb("a2_vtok", [128, VDIM], BF16)
    qT = p.sb("a2_qT", [64, NH * 128], BF16)
    kT = p.sb("a2_kT", [64, NH * 128], BF16)
    hS = p.sb("a2_hS", [64, NH * 128], F32)
    hSb = p.sb("a2_hSb", [64, NH * 128], BF16)
    obuf = p.sb("a2_obuf", [128, VDIM], F32)
    oT = p.sb("a2_oT", [128, NH * 128], BF16)
    xtile = p.sb("a2_xtile", [128, D], F32)
    xnb = p.sb("a2_xnb", [128, D], BF16)
    Eincl = p.sb("a2_Ei", [128, 128], F32)
    Estr = p.sb("a2_Es", [128, 128], F32)
    bbc = p.sb("a2_bbc", [128, 128], F32)
    wtile = p.sb("a2_wt", [128, 128], F32)
    Df = p.sb("a2_Df", [128, 128], F32)
    dlb = p.sb("a2_dlb", [128, 128], BF16)
    Nb = [p.sb("a2_Nb0", [128, 128], F32), p.sb("a2_Nb1", [128, 128], F32)]
    NTb = [p.sb("a2_NTb0", [128, 128], F32), p.sb("a2_NTb1", [128, 128], F32)]
    Mqb = p.sb("a2_Mqb", [128, 128], F32)
    UT1 = p.sb("a2_UT1", [128, 128], F32)
    UTs = p.sb("a2_UTs", [128, 128], F32)
    LTS = p.sb("a2_LTS", [128, 128], F32)
    Lh = p.sb("a2_Lh", [12, 128], F32)
    ones32 = p.sb("a2_ones32", [32, 128], F32)
    GT2 = p.sb("a2_GT2", [12, 256], F32)
    gmat = p.sb("a2_gmat", [128, 2 * 12], F32)
    bmat = p.sb("a2_bmat", [128, 2 * 12], F32)
    Gm = p.sb("a2_Gm", [128, 12], F32)
    lam = p.sb("a2_lam", [128, 12], F32)
    wmat = p.sb("a2_wmat", [128, 12], F32)
    glc = p.sb("a2_glc", [12, 1], F32)
    gLb = p.sb("a2_gLb", [64, 12], F32)
    nrm = p.sb("a2_nrm", [128, 2 * 12], F32)
    Abc = p.sb("a2_Abc", [128, 12], F32)
    arow = p.sb("a2_arow", [1, 12], F32)
    r24 = p.sb("a2_r24", [1, 24], F32)
    gnb = p.sb("a2_gnb", [128, 128], F32)
    cwr = p.sb("a2_cwr", [128, NQT * 4], F32)
    tails = [p.sb("a2_tailA", [128, NQT * 3], F32), p.sb("a2_tailB", [128, NQT * 3], F32)]
    catp = sq[:, 512:512 + SCW + 3]
    w32 = p.sb("a2_w32", [128, 128], F32)

    p.wait("gpsimd", "vector", B["init_v"])
    p.do("gpsimd", lambda e: e.memset(ones32[:, :], 1.0))
    p.do("gpsimd", lambda e: e.memset(tails[0][:, :], 0.0))
    p.do("gpsimd", lambda e: e.memset(tails[1][:, :], 0.0))
    p.do("gpsimd", lambda e: e.memset(hS[:, :], 0.0))
    p.do("gpsimd", lambda e: e.memset(hSb[:, :], 0.0))
    p.do("gpsimd", lambda e: e.iota(sq[:, 0:128], pattern=[[1, 128]], base=0,
                                    channel_multiplier=0, allow_small_or_imprecise_dtypes=True))
    gm = p.do("gpsimd", lambda e: e.iota(UT1[:, :], pattern=[[0, 128]], base=0,
                                         channel_multiplier=1,
                                         allow_small_or_imprecise_dtypes=True))
    p.wait("vector", "gpsimd", gm)
    p.do("vector", lambda e: e.tensor_tensor(LTS[:, :], sq[:, 0:128], UT1[:, :], OP.is_lt))
    p.do("vector", lambda e: e.tensor_tensor(UTs[:, :], sq[:, 0:128], UT1[:, :], OP.is_gt))
    p.do("vector", lambda e: e.tensor_tensor(UT1[:, :], sq[:, 0:128], UT1[:, :], OP.is_ge))
    _bar(p)

    for t in range(NT):
        dm = p.dma("sync", xtile[:, :], x_in.rearrange("(t p) d -> p t d", p=128)[:, t, :],
                   "l2x")
        p.waitg("vector", "l2x", dm)
        p.waitg("scalar", "l2x", dm)
        xt = xtile[:, :]
        p.do("vector", lambda e, xt=xt: e.tensor_reduce(stat[:, 0:1], xt, AX.X, OP.add))
        m = p.do("vector", lambda e: e.tensor_scalar(stat[:, 1:2], stat[:, 0:1],
                                                     -1.0 / D, None, OP.mult, OP.bypass))
        p.wait("scalar", "vector", m)
        m = p.do("scalar", lambda e, xt=xt: e.activation(sq[:, 0:D], xt, AF.Square,
                                                         bias=stat[:, 1:2], scale=1.0,
                                                         accum_out=stat[:, 2:3]))
        p.wait("vector", "scalar", m)
        m = p.do("vector", lambda e: e.tensor_scalar(stat[:, 3:4], stat[:, 2:3],
                                                     1.0 / D, LN_EPS, OP.mult, OP.add))
        p.wait("scalar", "vector", m)
        p.do("scalar", lambda e: e.activation(stat[:, 4:5], stat[:, 3:4], AF.Ln))
        m = p.do("scalar", lambda e: e.activation(stat[:, 5:6], stat[:, 4:5], AF.Exp, scale=-0.5))
        p.wait("vector", "scalar", m)
        m = p.do("vector", lambda e: e.tensor_tensor(stat[:, 6:7], stat[:, 1:2],
                                                     stat[:, 5:6], OP.mult))
        p.wait("scalar", "vector", m)
        m = p.do("scalar", lambda e, xt=xt: e.activation(
            xnb[:, :], xt, AF.Identity, bias=stat[:, 6:7], scale=stat[:, 5:6]))
        p.wait("vector", "scalar", m)
        p.wait("tensor", "scalar", m)
        cps = []
        for c in range(KD):
            slot = psA if c % 2 == 0 else psB
            if c >= 2:
                p.wait("tensor", "vector", cps[c - 2])
            p.do("tensor", lambda e, c=c, slot=slot: e.transpose(
                slot[:, 0:128], xnb[:, c * 128:(c + 1) * 128], B["identbf"][:, :]))
            p.wait("vector", "tensor", p.mark("tensor"))
            m = p.do("vector", lambda e, t=t, c=c, slot=slot: e.tensor_copy(
                xnT[:, c * T + t * 128: c * T + (t + 1) * 128], slot[:, 0:128]))
            cps.append(m)
        _bar(p)
        p.wait("sync", "vector", cps[-1])
        p.wait("sync", "scalar", p.mark("scalar"))

    p.wait("gpsimd", "tensor", p.mark("tensor"))
    p.dma("gpsimd", WQKV[:, :].rearrange("p (c n) -> p c n", c=KD)[:, :, 0:KDIM],
          W["wq"].rearrange("(c p) n -> p c n", p=128), "l2w")
    p.dma("gpsimd", WQKV[:, :].rearrange("p (c n) -> p c n", c=KD)[:, :, KDIM:2 * KDIM],
          W["wk"].rearrange("(c p) n -> p c n", p=128), "l2w")
    p.dma("gpsimd", WQKV[:, :].rearrange("p (c n) -> p c n", c=KD)[:, :, 2 * KDIM:CW3],
          W["wv"].rearrange("(c p) n -> p c n", p=128), "l2w")
    p.dma("gpsimd", WG[:, :].rearrange("p (c n) -> p c n", c=KD),
          W["wg"].rearrange("(c p) n -> p c n", p=128), "l2w")
    p.dma("gpsimd", WO[:, :].rearrange("p (c n) -> p c n", c=NH),
          W["wo"].rearrange("(c p) n -> p c n", p=128), "l2w")
    p.dma("gpsimd", Wgkb[:, :].rearrange("p (c n) -> p c n", c=KD)[:, :, 0:12],
          W["wgk"].rearrange("(c p) n -> p c n", p=128), "l2w")
    dm = p.dma("gpsimd", Wgkb[:, :].rearrange("p (c n) -> p c n", c=KD)[:, :, 12:24],
               W["wb"].rearrange("(c p) n -> p c n", p=128), "l2w")
    p.dma("sync", cwr[:, 0:24], W["cwq"][:, :], "l2ws")
    p.dma("sync", cwr[:, 24:48], W["cwk"][:, :], "l2ws")
    p.dma("sync", cwr[:, 48:96], W["cwv"][:, :], "l2ws")
    p.dma("sync", arow[0:1, :], W["alog"][:, :], "l2ws")
    p.dma("sync", r24[0:1, 0:12], W["dtb"][:, :], "l2ws")
    p.dma("sync", r24[0:1, 12:24], W["bb"][:, :], "l2ws")
    dms = p.dma("sync", w32[0:1, 0:128], W["gnw"][:, :], "l2ws")
    p.waitg("scalar", "l2w", dm)
    p.waitg("vector", "l2w", dm)
    p.waitg("tensor", "l2w", dm)
    p.waitg("scalar", "l2ws", dms)
    p.waitg("vector", "l2ws", dms)
    p.waitg("tensor", "l2ws", dms)
    m = p.do("scalar", lambda e: e.activation(arow[0:1, :], arow[0:1, :], AF.Exp, scale=1.0))
    p.wait("vector", "scalar", m)
    m = p.do("vector", lambda e: e.tensor_scalar(arow[0:1, :], arow[0:1, :], -1.0,
                                                 None, OP.mult, OP.bypass))
    p.wait("tensor", "vector", m)
    p.do("tensor", lambda e: e.matmul(B["ps_c"][:, 0:12], B["ones_row"][0:1, 0:128],
                                      arow[0:1, :], start=True, stop=True))
    mm = p.do("tensor", lambda e: e.matmul(B["ps_c"][:, 128:256], B["ones_row"][0:1, 0:128],
                                           w32[0:1, 0:128], start=True, stop=True))
    p.wait("vector", "tensor", mm)
    p.do("vector", lambda e: e.tensor_copy(Abc[:, :], B["ps_c"][:, 0:12]))
    p.do("vector", lambda e: e.tensor_copy(gnb[:, :], B["ps_c"][:, 128:256]))
    _bar(p)

    store_marks = []
    for sc in range(NSC):
        prc = []
        for i in range(NQT):
            slot = B["ps_w"][:, (i % 4) * 512: (i % 4) * 512 + SCW]
            if i >= 4:
                p.wait("tensor", "scalar", prc[i - 4])
            mm = 0
            for c in range(KD):
                mm = p.do("tensor", lambda e, c=c, i=i, sc=sc, slot=slot: e.matmul(
                    slot, WQKV[:, c * CW3 + i * 128: c * CW3 + (i + 1) * 128],
                    xnT[:, c * T + sc * SCW: c * T + (sc + 1) * SCW],
                    start=(c == 0), stop=(c == KD - 1)))
            p.wait("scalar", "tensor", mm)
            m = p.do("scalar", lambda e, i=i, slot=slot: e.activation(
                xqkv[:, i * SCW:(i + 1) * SCW], slot, AF.Identity))
            prc.append(m)
        _bar(p)
        tailO = tails[sc % 2]
        tailN = tails[1 - sc % 2]
        p.do("vector", lambda e, tailN=tailN: e.tensor_copy(
            tailN[:, :].rearrange("p (i k) -> p i k", i=NQT),
            xqkv[:, :].rearrange("p (i w) -> p i w", i=NQT)[:, :, SCW - 3:SCW]))
        for i in range(NQT):
            p.do("vector", lambda e, i=i, tailO=tailO: e.tensor_copy(
                catp[:, 0:3], tailO[:, i * 3:(i + 1) * 3]))
            p.do("vector", lambda e, i=i: e.tensor_copy(
                catp[:, 3:3 + SCW], xqkv[:, i * SCW:(i + 1) * SCW]))
            p.do("vector", lambda e, i=i: e.tensor_scalar(
                sq[:, 0:SCW], catp[:, 0:SCW], cwr[:, i * 4: i * 4 + 1], None,
                OP.mult, OP.bypass))
            for k in range(1, 4):
                p.do("vector", lambda e, i=i, k=k: e.tensor_scalar(
                    sq[:, SCW:2 * SCW], catp[:, k:k + SCW], cwr[:, i * 4 + k: i * 4 + k + 1],
                    None, OP.mult, OP.bypass))
                m = p.do("vector", lambda e: e.tensor_tensor(
                    sq[:, 0:SCW], sq[:, 0:SCW], sq[:, SCW:2 * SCW], OP.add))
            p.wait("scalar", "vector", m)
            m = p.do("scalar", lambda e, i=i: e.activation(
                xqkv[:, i * SCW:(i + 1) * SCW], sq[:, 0:SCW], AF.Silu))
            p.wait("vector", "scalar", m)
        _bar(p)
        for qq in range(2):
            mm = 0
            for c in range(KD):
                p.do("tensor", lambda e, c=c, qq=qq, sc=sc: e.matmul(
                    B["ps_c"][:, 0:24],
                    xnT[:, c * T + sc * SCW + qq * 128: c * T + sc * SCW + (qq + 1) * 128],
                    Wgkb[:, c * 24:(c + 1) * 24], start=(c == 0), stop=False))
            mm = p.do("tensor", lambda e: e.matmul(
                B["ps_c"][:, 0:24], B["ones_row"][0:1, 0:128], r24[0:1, :],
                start=False, stop=True))
            p.wait("vector", "tensor", mm)
            p.do("vector", lambda e: e.tensor_copy(w32[:, 0:24], B["ps_c"][:, 0:24]))
            m = p.do("vector", lambda e: e.tensor_scalar(w32[:, 24:36], w32[:, 0:12],
                                                         0.0, None, OP.max, OP.bypass))
            p.wait("scalar", "vector", m)
            m = p.do("scalar", lambda e: e.activation(w32[:, 36:48], w32[:, 0:12], AF.Abs))
            m = p.do("scalar", lambda e: e.activation(w32[:, 36:48], w32[:, 36:48],
                                                      AF.Exp, scale=-1.0))
            p.wait("vector", "scalar", m)
            m = p.do("vector", lambda e: e.tensor_scalar(w32[:, 36:48], w32[:, 36:48],
                                                         1.0, None, OP.add, OP.bypass))
            p.wait("scalar", "vector", m)
            m = p.do("scalar", lambda e: e.activation(w32[:, 36:48], w32[:, 36:48], AF.Ln))
            ms = p.do("scalar", lambda e, qq=qq: e.activation(bmat[:, qq * 12:(qq + 1) * 12],
                                                              w32[:, 12:24], AF.Sigmoid))
            p.wait("vector", "scalar", ms)
            p.do("vector", lambda e: e.tensor_tensor(w32[:, 24:36], w32[:, 24:36],
                                                     w32[:, 36:48], OP.add))
            m = p.do("vector", lambda e, qq=qq: e.tensor_tensor(
                gmat[:, qq * 12:(qq + 1) * 12], w32[:, 24:36], Abc[:, :], OP.mult))
            p.wait("tensor", "vector", m)

        for qi in range(2):
            q = sc * 2 + qi
            gkq = gmat[:, qi * 12:(qi + 1) * 12]
            btq = bmat[:, qi * 12:(qi + 1) * 12]
            if store_marks:
                p.waitg("sync", "l2o", store_marks[-1])
            xdm = p.dma("sync", xtile[:, :],
                        x_in.rearrange("(t p) d -> p t d", p=128)[:, q, :], "l2x")
            cps = []
            g = 0
            for i in range(NQT):
                slot = psA if g % 2 == 0 else psB
                if g >= 2:
                    p.wait("tensor", "vector", cps[g - 2])
                p.do("tensor", lambda e, i=i, qi=qi, slot=slot: e.transpose(
                    slot[:, 0:128], xqkv[:, i * SCW + qi * 128: i * SCW + (qi + 1) * 128],
                    B["identbf"][:, :]))
                p.wait("vector", "tensor", p.mark("tensor"))
                if i < 6:
                    dst = qtok[:, i * 128:(i + 1) * 128]
                elif i < 12:
                    dst = ktok[:, (i - 6) * 128:(i - 5) * 128]
                else:
                    dst = vtok[:, (i - 12) * 128:(i - 11) * 128]
                m = p.do("vector", lambda e, dst=dst, slot=slot: e.tensor_copy(
                    dst, slot[:, 0:128]))
                cps.append(m)
                g += 1
            _bar(p)
            p.do("vector", lambda e: e.tensor_tensor(sq[:, 0:KDIM], qtok[:, :],
                                                     qtok[:, :], OP.mult))
            for h in range(NH):
                p.do("vector", lambda e, h=h: e.tensor_reduce(
                    nrm[:, h:h + 1], sq[:, h * 64:(h + 1) * 64], AX.X, OP.add))
            p.do("vector", lambda e: e.tensor_tensor(sq[:, 0:KDIM], ktok[:, :],
                                                     ktok[:, :], OP.mult))
            for h in range(NH):
                p.do("vector", lambda e, h=h: e.tensor_reduce(
                    nrm[:, 12 + h:13 + h], sq[:, h * 64:(h + 1) * 64], AX.X, OP.add))
            m = p.do("vector", lambda e: e.tensor_scalar(nrm[:, :], nrm[:, :], 1.0, 1e-6,
                                                         OP.mult, OP.add))
            p.wait("scalar", "vector", m)
            p.do("scalar", lambda e: e.activation(nrm[:, :], nrm[:, :], AF.Ln))
            m = p.do("scalar", lambda e: e.activation(nrm[:, :], nrm[:, :], AF.Exp,
                                                      scale=-0.5))
            p.wait("vector", "scalar", m)
            m = p.do("vector", lambda e: e.tensor_scalar(nrm[:, 0:12], nrm[:, 0:12], SCALE,
                                                         None, OP.mult, OP.bypass))
            for h in range(NH):
                p.do("vector", lambda e, h=h: e.tensor_scalar(
                    qtok[:, h * 64:(h + 1) * 64], qtok[:, h * 64:(h + 1) * 64],
                    nrm[:, h:h + 1], None, OP.mult, OP.bypass))
                m = p.do("vector", lambda e, h=h: e.tensor_scalar(
                    ktok[:, h * 64:(h + 1) * 64], ktok[:, h * 64:(h + 1) * 64],
                    nrm[:, 12 + h:13 + h], None, OP.mult, OP.bypass))
            _bar(p)
            p.do("tensor", lambda e, gkq=gkq: e.matmul(B["ps_b"][:, 0:12], UT1[:, :], gkq,
                                                       start=True, stop=True))
            p.do("tensor", lambda e, gkq=gkq: e.matmul(B["ps_b"][:, 12:24], LTS[:, :], gkq,
                                                       start=True, stop=True))
            p.do("tensor", lambda e, gkq=gkq: e.matmul(B["ps_b"][0:12, 24:25], gkq,
                                                       B["ones_col"][:, 0:1],
                                                       start=True, stop=True))
            _bar(p)
            p.do("vector", lambda e: e.tensor_copy(Gm[:, :], B["ps_b"][:, 0:12]))
            m = p.do("vector", lambda e: e.tensor_copy(wmat[:, :], B["ps_b"][:, 12:24]))
            p.wait("scalar", "vector", m)
            p.do("scalar", lambda e: e.activation(lam[:, :], Gm[:, :], AF.Exp))
            p.do("scalar", lambda e: e.activation(wmat[:, :], wmat[:, :], AF.Exp))
            m = p.do("scalar", lambda e: e.activation(glc[:, 0:1], B["ps_b"][0:12, 24:25],
                                                      AF.Exp))
            p.wait("vector", "scalar", m)
            m = p.do("vector", lambda e: e.tensor_scalar(
                w32[0:12, 64:76], B["ident32"][0:12, 0:12], glc[:, 0:1], None,
                OP.mult, OP.bypass))
            p.wait("tensor", "vector", m)
            mm = p.do("tensor", lambda e: e.matmul(
                B["ps_c"][0:64, 0:12], ones32[0:12, 0:64], w32[0:12, 64:76],
                start=True, stop=True))
            p.do("tensor", lambda e: e.transpose(B["ps_c"][0:12, 128:256], Gm[:, :],
                                                 B["ident32"][:, :]))
            mm = p.do("tensor", lambda e, btq=btq: e.transpose(
                B["ps_c"][0:12, 256:384], btq, B["ident32"][:, :]))
            _bar(p)
            p.do("vector", lambda e: e.tensor_copy(gLb[:, :], B["ps_c"][0:64, 0:12]))
            p.do("vector", lambda e: e.tensor_copy(GT2[:, 0:128], B["ps_c"][0:12, 128:256]))
            m = p.do("vector", lambda e: e.tensor_copy(GT2[:, 128:256],
                                                       B["ps_c"][0:12, 256:384]))
            p.wait("tensor", "vector", m)
            cps = []
            g = 0
            for h in range(NH):
                for which, tok, dstb in (("q", qtok, qT), ("k", ktok, kT)):
                    slot = B["ps_d"] if g % 2 == 0 else B["ps_w"]
                    if g >= 2:
                        p.wait("tensor", "vector", cps[g - 2])
                    p.do("tensor", lambda e, h=h, tok=tok, slot=slot: e.transpose(
                        slot.bitcast(BF16)[0:64, 0:128], tok[:, h * 64:(h + 1) * 64],
                        B["identbf"][:, :]))
                    p.wait("vector", "tensor", p.mark("tensor"))
                    m = p.do("vector", lambda e, h=h, dstb=dstb, slot=slot: e.tensor_copy(
                        dstb[:, h * 128:(h + 1) * 128], slot.bitcast(BF16)[0:64, 0:128]))
                    cps.append(m)
                    g += 1
            _bar(p)
            if DEBUG_TAPS and sc == 0 and qi == 0:
                p.wait("sync", "vector", p.mark("vector"))
                p.wait("sync", "scalar", p.mark("scalar"))
                for nm, b_ in (("qtok", qtok), ("ktok", ktok), ("vtok", vtok),
                               ("gk0", gmat), ("bt0", bmat), ("Gm0", Gm),
                               ("lam0", lam), ("wm0", wmat)):
                    db = p.dram("dbg_" + nm, list(b_.shape), b_.ap().dtype,
                                kind="ExternalOutput")
                    dmt = p.dma("sync", db[:, :], b_[:, :], "l2o")
                p.waitg("vector", "l2o", dmt)
                p.waitg("scalar", "l2o", dmt)
                _bar(p)
            for h in range(NH):
                m = p.do("vector", lambda e, h=h: e.tensor_scalar(
                    Lh[:, :], ones32[0:12, :], B["ident32"][0:12, h:h + 1], None,
                    OP.mult, OP.bypass))
                p.wait("tensor", "vector", m)
                mm = p.do("tensor", lambda e: e.matmul(
                    B["ps_a"][:, 0:256], Lh[:, :], GT2[:, :], start=True, stop=True))
                p.do("tensor", lambda e, h=h: e.matmul(
                    B["ps_b"][:, 0:128], kT[:, h * 128:(h + 1) * 128],
                    kT[:, h * 128:(h + 1) * 128], start=True, stop=True))
                p.do("tensor", lambda e, h=h: e.matmul(
                    B["ps_b"][:, 128:256], kT[:, h * 128:(h + 1) * 128],
                    qT[:, h * 128:(h + 1) * 128], start=True, stop=True))
                p.do("tensor", lambda e, h=h: e.matmul(
                    B["ps_c"][:, 0:128], kT[:, h * 128:(h + 1) * 128],
                    hSb[:, h * 128:(h + 1) * 128], start=True, stop=True))
                mm = p.do("tensor", lambda e, h=h: e.matmul(
                    B["ps_c"][:, 128:256], qT[:, h * 128:(h + 1) * 128],
                    hSb[:, h * 128:(h + 1) * 128], start=True, stop=True))
                _bar(p)
                p.do("vector", lambda e: e.tensor_copy(bbc[:, :], B["ps_a"][:, 128:256]))
                p.do("vector", lambda e, h=h: e.tensor_scalar(
                    Eincl[:, :], B["ps_a"][:, 0:128], Gm[:, h:h + 1], None,
                    OP.subtract, OP.bypass))
                m = p.do("vector", lambda e: e.tensor_tensor(Eincl[:, :], Eincl[:, :],
                                                             UT1[:, :], OP.mult))
                p.wait("scalar", "vector", m)
                m = p.do("scalar", lambda e: e.activation(Eincl[:, :], Eincl[:, :], AF.Exp))
                p.wait("vector", "scalar", m)
                p.do("vector", lambda e: e.tensor_tensor(Eincl[:, :], Eincl[:, :],
                                                         UT1[:, :], OP.mult))
                p.do("vector", lambda e: e.tensor_tensor(Estr[:, :], Eincl[:, :],
                                                         UTs[:, :], OP.mult))
                p.do("vector", lambda e: e.tensor_tensor(Mqb[:, :], Eincl[:, :],
                                                         B["ps_b"][:, 128:256], OP.mult))
                p.do("vector", lambda e: e.tensor_tensor(Estr[:, :], Estr[:, :],
                                                         B["ps_b"][:, 0:128], OP.mult))
                p.do("vector", lambda e: e.tensor_tensor(Estr[:, :], Estr[:, :],
                                                         bbc[:, :], OP.mult))
                p.do("vector", lambda e: e.tensor_scalar(NTb[0][:, :], Estr[:, :], -1.0,
                                                         None, OP.mult, OP.bypass))
                p.do("vector", lambda e, h=h: e.tensor_scalar(
                    Estr[:, :], B["ps_a"][:, 0:128], Gm[:, h:h + 1], None,
                    OP.subtract, OP.bypass))
                p.do("vector", lambda e: e.tensor_scalar(Estr[:, :], Estr[:, :], -1.0,
                                                         None, OP.mult, OP.bypass))
                m = p.do("vector", lambda e: e.tensor_tensor(Estr[:, :], Estr[:, :],
                                                             LTS[:, :], OP.mult))
                p.wait("scalar", "vector", m)
                m = p.do("scalar", lambda e: e.activation(Estr[:, :], Estr[:, :], AF.Exp))
                p.wait("vector", "scalar", m)
                p.do("vector", lambda e: e.tensor_tensor(Estr[:, :], Estr[:, :],
                                                         LTS[:, :], OP.mult))
                p.do("vector", lambda e: e.tensor_tensor(Estr[:, :], Estr[:, :],
                                                         B["ps_b"][:, 0:128], OP.mult))
                p.do("vector", lambda e, h=h, btq=btq: e.tensor_scalar(
                    Estr[:, :], Estr[:, :], btq[:, h:h + 1], None, OP.mult, OP.bypass))
                p.do("vector", lambda e: e.tensor_scalar(Nb[0][:, :], Estr[:, :], -1.0,
                                                         None, OP.mult, OP.bypass))
                p.do("vector", lambda e, h=h: e.tensor_scalar(
                    wtile[:, :], B["ps_c"][:, 0:128], lam[:, h:h + 1], None,
                    OP.mult, OP.bypass))
                p.do("vector", lambda e, h=h: e.tensor_tensor(
                    Df[:, :], vtok[:, h * 128:(h + 1) * 128], wtile[:, :], OP.subtract))
                m = p.do("vector", lambda e, h=h, btq=btq: e.tensor_scalar(
                    Df[:, :], Df[:, :], btq[:, h:h + 1], None, OP.mult, OP.bypass))
                p.wait("tensor", "vector", m)
                for j in range(7):
                    cur = j % 2
                    nxt = 1 - cur
                    mm = p.do("tensor", lambda e, cur=cur: e.matmul(
                        B["ps_w"][:, 0:128], NTb[cur][:, :], Df[:, :],
                        start=True, stop=True))
                    if j < 6:
                        p.do("tensor", lambda e, cur=cur: e.matmul(
                            B["ps_w"][:, 128:256], NTb[cur][:, :], Nb[cur][:, :],
                            start=True, stop=True))
                        mm = p.do("tensor", lambda e, cur=cur: e.matmul(
                            B["ps_w"][:, 256:384], Nb[cur][:, :], NTb[cur][:, :],
                            start=True, stop=True))
                    p.wait("vector", "tensor", mm)
                    m = p.do("vector", lambda e: e.tensor_tensor(Df[:, :], Df[:, :],
                                                                B["ps_w"][:, 0:128], OP.add))
                    if j < 6:
                        p.do("vector", lambda e, nxt=nxt: e.tensor_copy(
                            Nb[nxt][:, :], B["ps_w"][:, 128:256]))
                        m = p.do("vector", lambda e, nxt=nxt: e.tensor_copy(
                            NTb[nxt][:, :], B["ps_w"][:, 256:384]))
                    p.wait("tensor", "vector", m)
                mm = p.do("tensor", lambda e: e.matmul(
                    B["ps_d"][:, 0:128], Mqb[:, :], Df[:, :], start=True, stop=True))
                p.wait("vector", "tensor", mm)
                p.do("vector", lambda e, h=h: e.tensor_scalar(
                    obuf[:, h * 128:(h + 1) * 128], B["ps_c"][:, 128:256],
                    lam[:, h:h + 1], None, OP.mult, OP.bypass))
                p.do("vector", lambda e, h=h: e.tensor_tensor(
                    obuf[:, h * 128:(h + 1) * 128], obuf[:, h * 128:(h + 1) * 128],
                    B["ps_d"][:, 0:128], OP.add))
                m = p.do("vector", lambda e, h=h: e.tensor_scalar(
                    dlb[:, :], Df[:, :], wmat[:, h:h + 1], None, OP.mult, OP.bypass))
                p.wait("tensor", "vector", m)
                mm = p.do("tensor", lambda e, h=h: e.matmul(
                    B["ps_d"][0:64, 128:256], ktok[:, h * 64:(h + 1) * 64], dlb[:, :],
                    start=True, stop=True))
                p.wait("vector", "tensor", mm)
                p.do("vector", lambda e, h=h: e.tensor_scalar(
                    hS[:, h * 128:(h + 1) * 128], hS[:, h * 128:(h + 1) * 128],
                    gLb[:, h:h + 1], None, OP.mult, OP.bypass))
                p.do("vector", lambda e, h=h: e.tensor_tensor(
                    hS[:, h * 128:(h + 1) * 128], hS[:, h * 128:(h + 1) * 128],
                    B["ps_d"][0:64, 128:256], OP.add))
                m = p.do("vector", lambda e, h=h: e.tensor_copy(
                    hSb[:, h * 128:(h + 1) * 128], hS[:, h * 128:(h + 1) * 128]))
                p.wait("tensor", "vector", m)
            _bar(p)
            if DEBUG_TAPS and sc == 0 and qi == 0:
                p.wait("sync", "vector", p.mark("vector"))
                p.wait("sync", "scalar", p.mark("scalar"))
                for nm, b_ in (("obuf", obuf), ("hS0", hS)):
                    db = p.dram("dbg_" + nm, list(b_.shape), b_.ap().dtype,
                                kind="ExternalOutput")
                    dmt = p.dma("sync", db[:, :], b_[:, :], "l2o")
                p.waitg("vector", "l2o", dmt)
                p.waitg("scalar", "l2o", dmt)
                _bar(p)
            p.do("vector", lambda e: e.tensor_tensor(sq[:, 0:D], obuf[:, 0:D],
                                                     obuf[:, 0:D], OP.mult))
            for h in range(8):
                p.do("vector", lambda e, h=h: e.tensor_reduce(
                    nrm[:, h:h + 1], sq[:, h * 128:(h + 1) * 128], AX.X, OP.add))
            p.do("vector", lambda e: e.tensor_tensor(sq[:, 0:512], obuf[:, D:VDIM],
                                                     obuf[:, D:VDIM], OP.mult))
            for h in range(4):
                p.do("vector", lambda e, h=h: e.tensor_reduce(
                    nrm[:, 8 + h:9 + h], sq[:, h * 128:(h + 1) * 128], AX.X, OP.add))
            m = p.do("vector", lambda e: e.tensor_scalar(nrm[:, 0:12], nrm[:, 0:12],
                                                         1.0 / DV, 1e-5, OP.mult, OP.add))
            p.wait("scalar", "vector", m)
            p.do("scalar", lambda e: e.activation(nrm[:, 0:12], nrm[:, 0:12], AF.Ln))
            m = p.do("scalar", lambda e: e.activation(nrm[:, 0:12], nrm[:, 0:12],
                                                      AF.Exp, scale=-0.5))
            p.wait("vector", "scalar", m)
            for h in range(NH):
                p.do("vector", lambda e, h=h: e.tensor_scalar(
                    obuf[:, h * 128:(h + 1) * 128], obuf[:, h * 128:(h + 1) * 128],
                    nrm[:, h:h + 1], None, OP.mult, OP.bypass))
                p.do("vector", lambda e, h=h: e.tensor_tensor(
                    obuf[:, h * 128:(h + 1) * 128], obuf[:, h * 128:(h + 1) * 128],
                    gnb[:, :], OP.mult))
            _bar(p)
            tok0 = sc * SCW + qi * 128
            for j in range(3):
                mm = 0
                for c in range(KD):
                    mm = p.do("tensor", lambda e, c=c, j=j, tok0=tok0: e.matmul(
                        psl[j][:, 0:512], xnT[:, c * T + tok0: c * T + tok0 + 128],
                        WG[:, c * VDIM + j * 512: c * VDIM + (j + 1) * 512],
                        start=(c == 0), stop=(c == KD - 1)))
            _bar(p)
            for j in range(3):
                m = p.do("scalar", lambda e, j=j: e.activation(sq[:, 0:512],
                                                               psl[j][:, 0:512], AF.Silu))
                p.wait("vector", "scalar", m)
                m = p.do("vector", lambda e, j=j: e.tensor_tensor(
                    obuf[:, j * 512:(j + 1) * 512], obuf[:, j * 512:(j + 1) * 512],
                    sq[:, 0:512], OP.mult))
                p.wait("scalar", "vector", m)
            _bar(p)
            cps = []
            for i in range(NH):
                slot = B["ps_a"] if i % 2 == 0 else B["ps_b"]
                if i >= 2:
                    p.wait("tensor", "vector", cps[i - 2])
                p.do("tensor", lambda e, i=i, slot=slot: e.transpose(
                    slot[:, 0:128], obuf[:, i * 128:(i + 1) * 128], B["ident32"][:, :]))
                p.wait("vector", "tensor", p.mark("tensor"))
                m = p.do("vector", lambda e, i=i, slot=slot: e.tensor_copy(
                    oT[:, i * 128:(i + 1) * 128], slot[:, 0:128]))
                cps.append(m)
            _bar(p)
            for hf in range(2):
                for i in range(NH):
                    p.do("tensor", lambda e, i=i, hf=hf: e.matmul(
                        B["ps_w"][:, hf * 512:(hf + 1) * 512], oT[:, i * 128:(i + 1) * 128],
                        WO[:, i * 1024 + hf * 512: i * 1024 + (hf + 1) * 512],
                        start=(i == 0), stop=(i == NH - 1)))
            _bar(p)
            p.waitg("vector", "l2x", xdm)
            for hf in range(2):
                p.do("vector", lambda e, hf=hf: e.tensor_tensor(
                    xtile[:, hf * 512:(hf + 1) * 512], xtile[:, hf * 512:(hf + 1) * 512],
                    B["ps_w"][:, hf * 512:(hf + 1) * 512], OP.add))
            m = p.mark("vector")
            p.wait("sync", "vector", m)
            sm = p.dma("sync", x_out.rearrange("(t p) d -> p t d", p=128)[:, q, :],
                       xtile[:, :], "l2o")
            store_marks.append(sm)
            _bar(p)
    return ("sync", store_marks[-1])


def _l2a_cw(cw):
    C = cw.shape[0]
    nt = C // 128
    return np.ascontiguousarray(cw.reshape(nt, 128, 4).transpose(1, 0, 2).reshape(128, nt * 4))


def _l2a_wspec():
    return [("wq", "L2_attn_Wq", None), ("wk", "L2_attn_Wk", None),
            ("wv", "L2_attn_Wv", None), ("wg", "L2_attn_Wg", None),
            ("wo", "L2_attn_Wo", None), ("wgk", "L2_attn_Wgk", None),
            ("wb", "L2_attn_Wb", None),
            ("cwq", "L2_attn_conv_q", _l2a_cw), ("cwk", "L2_attn_conv_k", _l2a_cw),
            ("cwv", "L2_attn_conv_v", _l2a_cw),
            ("alog", "L2_attn_A_log", _row), ("dtb", "L2_attn_dt_bias", _row),
            ("bb", "L2_attn_b_b", _row), ("gnw", "L2_attn_g_norm_weight", _row)]



def _tr_last2(a):
    return np.ascontiguousarray(np.swapaxes(a, -1, -2))


WSPECS = {
    "l1f": [("gate_w", "L1_ffn_gate_w", None), ("gate_b", "L1_ffn_gate_b", None),
            ("expert_w", "L1_ffn_expert_w", _tr_last2), ("expert_b", "L1_ffn_expert_b", None)],
    "l0a": [("W_DQ", "L0_attn_W_DQ", None), ("b_DQ", "L0_attn_b_DQ", None),
            ("W_UQ", "L0_attn_W_UQ", None), ("b_UQ", "L0_attn_b_UQ", None),
            ("W_DKV", "L0_attn_W_DKV", None), ("b_DKV", "L0_attn_b_DKV", None),
            ("W_UK", "L0_attn_W_UK", None), ("b_UK", "L0_attn_b_UK", None),
            ("W_UV", "L0_attn_W_UV", None), ("b_UV", "L0_attn_b_UV", None),
            ("W_QR", "L0_attn_W_QR", None), ("b_QR", "L0_attn_b_QR", None),
            ("W_KR", "L0_attn_W_KR", None), ("b_KR", "L0_attn_b_KR", None),
            ("W_O", "L0_attn_W_O", None), ("b_O", "L0_attn_b_O", None)],
    "l1a": [("past_kv", "L1_attn_past_kv", None), ("W_qkv", "L1_attn_W_qkv", None),
            ("gn_weight_full", "L1_attn_gn_weight", lambda a: np.ascontiguousarray(np.repeat(a, 64))),
            ("gn_bias_full", "L1_attn_gn_bias", lambda a: np.ascontiguousarray(np.repeat(a, 64)))],
    "l0f": [("router_w", "L0_ffn_router_w", None), ("router_b", "L0_ffn_router_b", None),
            ("w_gate", "L0_ffn_w_gate", None), ("w_up", "L0_ffn_w_up", None),
            ("w_down", "L0_ffn_w_down", None), ("sw_gate", "L0_ffn_sw_gate", None),
            ("sw_up", "L0_ffn_sw_up", None), ("sw_down", "L0_ffn_sw_down", None)],
    "l3a": None,
    "l2a": None,
    "l3f": [("w_gate", "L3_ffn_w_gate", None), ("w_up", "L3_ffn_w_up", None),
            ("w_down", "L3_ffn_w_down", None)],
    "l2f": [("router_w", "L2_ffn_router_w", _tr_last2), ("router_bias", "L2_ffn_router_bias", None),
            ("up_proj", "L2_ffn_up_proj", None), ("gate_proj", "L2_ffn_gate_proj", None),
            ("down_proj", "L2_ffn_down_proj", None),
            ("sh_gate", "L2_ffn_sh_gate", _tr_last2), ("sh_up", "L2_ffn_sh_up", _tr_last2),
            ("sh_down", "L2_ffn_sh_down", _tr_last2)],
}

EMITTERS = {}



def _l3a_winx(win):
    out = np.zeros((1024, 2304), np.float32)
    out[:, 0:2048] = win[:, 2048:4096]
    out[:, 2048:2112] = win[:, 4096:4160]
    out[:, 2176:2240] = win[:, 4160:4224]
    return out


def _l3a_cw(cw):
    pad = np.zeros((2304, 4), np.float32)
    pad[0:2048] = cw[:, 0, 0:2048].T
    pad[2048:2112] = cw[:, 0, 2048:2112].T
    pad[2176:2240] = cw[:, 0, 2112:2176].T
    return np.ascontiguousarray(pad.reshape(18, 128, 4).transpose(1, 0, 2).reshape(128, 72))


def _l3a_cb(cb):
    pad = np.zeros((2304,), np.float32)
    pad[0:2048] = cb[0:2048]
    pad[2048:2112] = cb[2048:2112]
    pad[2176:2240] = cb[2112:2176]
    return np.ascontiguousarray(pad.reshape(18, 128).T)


def _row(a):
    return np.ascontiguousarray(np.asarray(a)[None, :])


def _l3a_wspec():
    sp = [("woA", "L3_attn_Wo", lambda a: np.ascontiguousarray(a[0:1024])),
          ("woB", "L3_attn_Wo", lambda a: np.ascontiguousarray(a[1024:2048])),
          ("bo", "L3_attn_bo", _row)]
    for pf, ik in (("f", "f"), ("b", "b")):
        sp += [("winx" + pf, f"L3_attn_{ik}Win", _l3a_winx),
               ("winz" + pf, f"L3_attn_{ik}Win", lambda a: np.ascontiguousarray(a[:, 0:2048])),
               ("windt" + pf, f"L3_attn_{ik}Win", lambda a: np.ascontiguousarray(a[:, 4224:4256])),
               ("cw" + pf, f"L3_attn_{ik}convw", _l3a_cw),
               ("cb" + pf, f"L3_attn_{ik}convb", _l3a_cb),
               ("dtb" + pf, f"L3_attn_{ik}dtb", _row),
               ("alog" + pf, f"L3_attn_{ik}Alog", _row),
               ("D" + pf, f"L3_attn_{ik}D", lambda a: _row(np.repeat(a, 64))),
               ("nw" + pf, f"L3_attn_{ik}normw", _row),
               ("woutT" + pf, f"L3_attn_{ik}Wout", _tr_last2)]
    return sp


def get_wspec(name):
    sp = WSPECS.get(name)
    if sp is None:
        sp = globals()[WSPECS_LAZY[name]]()
        WSPECS[name] = sp
    return sp


def build_sublayer_prog(name, coef, inputs):
    p = Prog()
    x_in = p.dram("x_in", [T, D], F32, kind="ExternalInput").ap()
    x_out = p.dram("x_out", [T, D], F32, kind="ExternalOutput").ap()
    W = {}
    for wname, key, tf in get_wspec(name):
        arr = np.asarray(inputs[key], dtype=np.float32)
        shape = list(tf(arr).shape) if tf else list(arr.shape)
        W[wname] = p.dram(wname, shape, F32, kind="ExternalInput").ap()
    if name in RAW_EMITTERS:
        B = make_common(p, small=True)
        fin_eng, fin_mark = RAW_EMITTERS[name](p, B, W, coef, x_in, x_out)
        p.wait("sync", fin_eng, fin_mark) if fin_eng != "sync" else None
        grp = "l3o" if name == "l3a" else "l2o"
        if DEBUG_TAPS:
            for nm, buf in p.dbg.items():
                dt_out = p.dram("dbg_" + nm, list(buf.shape), buf.ap().dtype,
                                kind="ExternalOutput")
                fin_mark = p.dma("sync", dt_out[:, :], buf[:, :], grp)
        p.waitg("sync", grp, fin_mark)
        return p.finalize()
    B = make_common(p)
    lm = emit_load_x(p, B, x_in)
    p.waitg("vector", "xio", lm)
    p.waitg("scalar", "xio", lm)
    p.waitg("tensor", "xio", lm)
    fin_eng, fin_mark = EMITTERS[name](p, B, W, coef)
    p.wait("sync", fin_eng, fin_mark)
    sm = emit_store_x(p, B, x_out)
    if DEBUG_TAPS:
        for nm, buf in p.dbg.items():
            dt_out = p.dram("dbg_" + nm, list(buf.shape), buf.ap().dtype, kind="ExternalOutput")
            sm = p.dma("sync", dt_out[:, :], buf[:, :], "xio")
    p.waitg("sync", "xio", sm)
    return p.finalize()


_PROG_CACHE = {}


def run_sublayer(name, coef, xs, inputs):
    key = (name, coef, DEBUG_TAPS)
    if key not in _PROG_CACHE:
        _PROG_CACHE[key] = build_sublayer_prog(name, coef, inputs)
    nc = _PROG_CACHE[key]
    wmap = {}
    for wname, ikey, tf in get_wspec(name):
        a = np.asarray(inputs[ikey], dtype=np.float32)
        wmap[wname] = tf(a) if tf else np.ascontiguousarray(a)
    in_maps = [{"x_in": xs[b], **wmap} for b in range(NCORE)]
    res = run_bass_kernel_spmd(nc, in_maps, core_ids=list(range(NCORE)))
    if DEBUG_TAPS:
        return res
    return [np.asarray(res.results[b]["x_out"]) for b in range(NCORE)]


def _split_batch(x):
    x = np.asarray(x, dtype=np.float32)
    return [np.ascontiguousarray(x[b]) for b in range(NCORE)]




FUSE_SEGMENTS = [
    ("b", [("l0f", 0.5), ("l0a", 1.0), ("l0f", 0.5), ("l1f", 0.5), ("l1a", 1.0),
           ("l1f", 0.5), ("l2f", 0.5)]),
    ("r", [("l2a", 1.0)]),
    ("b", [("l2f", 0.5), ("l3f", 0.5)]),
    ("r", [("l3a", 1.0)]),
    ("b", [("l3f", 0.5)]),
]


def build_fused_prog(inputs):
    p = Prog()
    x_in = p.dram("x_in", [T, D], F32, kind="ExternalInput").ap()
    x_out = p.dram("x_out", [T, D], F32, kind="ExternalOutput").ap()
    Wall = {}
    for name in set(n for _, subs in FUSE_SEGMENTS for n, _ in subs):
        W = {}
        for wname, key, tf in get_wspec(name):
            arr = np.asarray(inputs[key], dtype=np.float32)
            shape = list(tf(arr).shape) if tf else list(arr.shape)
            W[wname] = p.dram(f"{name}_{wname}", shape, F32, kind="ExternalInput").ap()
        Wall[name] = W
    xbufs = [p.dram(f"xh{i}", [T, D], F32).ap() for i in range(len(FUSE_SEGMENTS) - 1)]
    srcs = [x_in] + xbufs
    dsts = xbufs + [x_out]
    Bbase = make_common(p, small=True)
    si = 0
    last_store = 0
    for gi, (kind, subs) in enumerate(FUSE_SEGMENTS):
        xsrc, xdst = srcs[gi], dsts[gi]
        if kind == "b":
            with p.scope(f"g{gi}_"):
                B = dict(Bbase)
                make_seg_bufs(p, B)
                lm = emit_load_x(p, B, xsrc)
                p.waitg("vector", "xio", lm)
                p.waitg("scalar", "xio", lm)
                p.waitg("tensor", "xio", lm)
                for name, coef in subs:
                    with p.scope(f"s{si}_"):
                        fin_eng, fin_mark = EMITTERS[name](p, B, Wall[name], coef)
                        p.join()
                    si += 1
                p.wait("sync", "vector", p.mark("vector"))
                last_store = emit_store_x(p, B, xdst)
                for e in Prog.ENGS:
                    p.waitg(e, "xio", last_store)
                p.join()
        else:
            name, coef = subs[0]
            with p.scope(f"s{si}_"):
                fin_eng, fin_mark = RAW_EMITTERS[name](p, Bbase, Wall[name], coef,
                                                       xsrc, xdst)
                last_store = fin_mark
                p.join()
            si += 1
    p.waitg("sync", "xio", last_store) if False else None
    return p.finalize()


_FUSED_CACHE = {}


def kernel_fused(inputs):
    if "prog" not in _FUSED_CACHE:
        _FUSED_CACHE["prog"] = build_fused_prog(inputs)
    nc = _FUSED_CACHE["prog"]
    wmap = {}
    for name in set(n for _, subs in FUSE_SEGMENTS for n, _ in subs):
        for wname, ikey, tf in get_wspec(name):
            a = np.asarray(inputs[ikey], dtype=np.float32)
            wmap[f"{name}_{wname}"] = tf(a) if tf else np.ascontiguousarray(a)
    xs = _split_batch(inputs["L0_attn_h_t"])
    in_maps = [{"x_in": xs[b], **wmap} for b in range(NCORE)]
    res = run_bass_kernel_spmd(nc, in_maps, core_ids=list(range(NCORE)))
    return np.stack([np.asarray(res.results[b]["x_out"]) for b in range(NCORE)], 0)


SUBLAYERS = [("l0f", 0.5), ("l0a", 1.0), ("l0f", 0.5),
             ("l1f", 0.5), ("l1a", 1.0), ("l1f", 0.5),
             ("l2f", 0.5), ("l2a", 1.0), ("l2f", 0.5),
             ("l3f", 0.5), ("l3a", 1.0), ("l3f", 0.5)]


FUSED = True


def kernel(**inputs):
    if FUSED:
        return kernel_fused(inputs)
    xs = _split_batch(inputs["L0_attn_h_t"])
    for name, coef in SUBLAYERS:
        xs = run_sublayer(name, coef, xs, inputs)
    return np.stack(xs, 0)


EMITTERS["l1f"] = emit_l1_ffn
EMITTERS["l2f"] = emit_l2_ffn
EMITTERS["l0a"] = emit_l0_attn
EMITTERS["l1a"] = emit_l1_attn
EMITTERS["l3f"] = emit_l3_ffn
EMITTERS["l0f"] = emit_l0_ffn
RAW_EMITTERS = {"l3a": emit_l3_attn, "l2a": emit_l2_attn}
WSPECS_LAZY = {"l3a": "_l3a_wspec", "l2a": "_l2a_wspec"}


# version 03 (final)
# speedup vs baseline: 1.0252x; 1.0252x over previous
import numpy as np
import concourse.bass as bass
import concourse.mybir as mybir
from concourse.alu_op_type import AluOpType as OP
from concourse.bass_utils import run_bass_kernel_spmd
from concourse.bass import IndirectOffsetOnAxis
from contextlib import ExitStack, contextmanager

F32 = mybir.dt.float32
BF16 = mybir.dt.bfloat16
U32 = mybir.dt.uint32
I32 = mybir.dt.int32
U8 = mybir.dt.uint8
AF = mybir.ActivationFunctionType
AX = mybir.AxisListType

NCORE = 8
T = 2048
D = 1024
NT = T // 128
KD = D // 128
LN_EPS = 1e-5
DEBUG_TAPS = False

GUARDED = ("vector", "scalar", "gpsimd")


class Prog:
    ENGS = ("sync", "tensor", "vector", "scalar", "gpsimd")

    def __init__(self):
        self.nc = bass.Bass("TRN2", target_bir_lowering=False)
        self.es = ExitStack()
        self.ops = {e: [] for e in self.ENGS}
        self.dbg = {}
        self.cnt = {}
        self.sems = {}
        self.gsems = {}
        self.prefix = ""
        self.es_stack = [self.es]
        for e in self.ENGS:
            self.sems[e] = self.es.enter_context(self.nc.semaphore(f"pe_{e}"))
            self.cnt[e] = 0

    @contextmanager
    def scope(self, prefix):
        old = self.prefix
        self.prefix = old + prefix
        es = ExitStack()
        self.es_stack.append(es)
        try:
            yield
        finally:
            es.close()
            self.es_stack.pop()
            self.prefix = old

    def sb(self, name, shape, dtype=F32):
        return self.es_stack[-1].enter_context(
            self.nc.sbuf_tensor(self.prefix + name, shape, dtype))

    def ps(self, name, shape, dtype=F32):
        return self.es_stack[-1].enter_context(
            self.nc.psum_tensor(self.prefix + name, shape, dtype))

    def dram(self, name, shape, dtype=F32, **kw):
        if kw.get("kind") in ("ExternalInput", "ExternalOutput"):
            return self.nc.dram_tensor(name, shape, dtype, **kw)
        return self.nc.dram_tensor(self.prefix + name, shape, dtype, **kw)

    def join(self):
        marks = {e: self.mark(e) for e in self.ENGS}
        for a in self.ENGS:
            for b in self.ENGS:
                if a != b:
                    self.wait(a, b, marks[b])
        for gname, g in list(self.gsems.items()):
            for e in self.ENGS:
                self.waitg(e, gname, g[1])

    def on(self, eng, fn):
        self.ops[eng].append(fn)

    def mark(self, eng):
        return self.cnt[eng]

    def wait(self, eng, src, v):
        if v <= 0:
            return
        self.on(eng, lambda e, s=self.sems[src], v=v: e.wait_ge(s, v))

    def do(self, eng, fn, guard=True):
        s = self.sems[eng]
        nb = self.cnt[eng]
        self.cnt[eng] = nb + 1
        if eng in GUARDED and guard and nb > 0:
            def run(e, fn=fn, s=s, nb=nb):
                e.wait_ge(s, nb)
                fn(e).then_inc(s, 1)
        else:
            def run(e, fn=fn, s=s):
                fn(e).then_inc(s, 1)
        self.on(eng, run)
        return self.cnt[eng]

    def gsem(self, name):
        if name not in self.gsems:
            h = self.es.enter_context(self.nc.semaphore(f"g_{name}"))
            self.gsems[name] = [h, 0]
        return self.gsems[name]

    def dma(self, q, out, in_, grp, **kw):
        g = self.gsem(grp)
        g[1] += 16

        def run(e, out=out, in_=in_, s=g[0], kw=kw):
            e.dma_start(out=out, in_=in_, **kw).then_inc(s, 16)
        self.on(q, run)
        return g[1]

    def waitg(self, eng, grp, v):
        g = self.gsem(grp)
        if v <= 0:
            return
        self.on(eng, lambda e, s=g[0], v=v: e.wait_ge(s, v))

    def cc(self, kind, op, ins, outs, grp="cc"):
        g = self.gsem(grp)
        g[1] += 1

        def run(e, kind=kind, op=op, ins=ins, outs=outs, s=g[0]):
            e.collective_compute(kind, op, replica_groups=[list(range(NCORE))],
                                 ins=ins, outs=outs).then_inc(s)
        self.on("gpsimd", run)
        return g[1]

    def finalize(self):
        with self.nc.Block() as blk:
            for eng in self.ENGS:
                if not self.ops[eng]:
                    continue
                getattr(blk, eng)(lambda e, ops=tuple(self.ops[eng]): [f(e) for f in ops])
        self.es.close()
        return self.nc



def make_seg_bufs(p, B):
    B["x_res"] = p.sb("x_res", [128, NT * D], F32)
    B["xn"] = p.sb("xn", [128, NT * D], F32)
    B["xnT"] = p.sb("xnT", [128, KD * T], BF16)
    B["xn_bf"] = p.sb("xn_bf", [128, D], BF16)
    return B


def make_common(p, small=False):
    B = {}
    if not small:
        B["x_res"] = p.sb("x_res", [128, NT * D], F32)
        B["xn"] = p.sb("xn", [128, NT * D], F32)
        B["xnT"] = p.sb("xnT", [128, KD * T], BF16)
        B["xn_bf"] = p.sb("xn_bf", [128, D], BF16)
    B["ident32"] = p.sb("ident32", [128, 128], F32)
    B["identbf"] = p.sb("identbf", [128, 128], BF16)
    B["ones_row"] = p.sb("ones_row", [1, 512], F32)
    B["zero_row"] = p.sb("zero_row", [1, 512], F32)
    B["ones_col"] = p.sb("ones_col", [128, 1], F32)
    B["stat"] = p.sb("stat", [128, 16], F32)
    B["sq"] = p.sb("sq", [128, D], F32)
    B["ps_w"] = p.ps("ps_w", [128, 2048], F32)
    B["ps_a"] = p.ps("ps_a", [128, 512], F32)
    B["ps_b"] = p.ps("ps_b", [128, 512], F32)
    B["ps_c"] = p.ps("ps_c", [128, 512], F32)
    B["ps_d"] = p.ps("ps_d", [128, 512], F32)
    p.do("gpsimd", lambda e: e.memset(B["ones_row"][:, :], 1.0))
    p.do("gpsimd", lambda e: e.memset(B["zero_row"][:, :], 0.0))
    p.do("gpsimd", lambda e: e.memset(B["ones_col"][:, :], 1.0))
    p.do("gpsimd", lambda e: e.iota(B["ident32"][:, :], pattern=[[0, 128]], base=0,
                                    channel_multiplier=1, allow_small_or_imprecise_dtypes=True))
    gm = p.do("gpsimd", lambda e: e.iota(B["sq"][:, 0:128], pattern=[[1, 128]], base=0,
                                         channel_multiplier=0, allow_small_or_imprecise_dtypes=True))
    p.wait("vector", "gpsimd", gm)
    p.do("vector", lambda e: e.tensor_tensor(B["ident32"][:, :], B["ident32"][:, :],
                                             B["sq"][:, 0:128], OP.is_equal))
    B["init_v"] = p.do("vector", lambda e: e.tensor_copy(B["identbf"][:, :], B["ident32"][:, :]))
    return B


def emit_load_x(p, B, x_in):
    return p.dma("sync", B["x_res"][:, :].rearrange("p (t d) -> p t d", t=NT),
                 x_in.rearrange("(t p) d -> p t d", p=128), "xio")


def emit_store_x(p, B, x_out):
    return p.dma("sync", x_out.rearrange("(t p) d -> p t d", p=128),
                 B["x_res"][:, :].rearrange("p (t d) -> p t d", t=NT), "xio")


def emit_ln(p, B):
    x, xn, xnT = B["x_res"], B["xn"], B["xnT"]
    stat, sq = B["stat"], B["sq"]
    psA = B["ps_a"].bitcast(BF16)
    psB = B["ps_b"].bitcast(BF16)
    p.wait("tensor", "vector", B["init_v"])
    tile_done = []
    copy_marks = []
    for t in range(NT):
        xt = x[:, t * D:(t + 1) * D]
        p.do("vector", lambda e, xt=xt: e.tensor_reduce(stat[:, 0:1], xt, AX.X, OP.add))
        m = p.do("vector", lambda e: e.tensor_scalar(stat[:, 1:2], stat[:, 0:1],
                                                     -1.0 / D, None, OP.mult, OP.bypass))
        p.wait("scalar", "vector", m)
        m = p.do("scalar", lambda e, xt=xt: e.activation(sq[:, :], xt, AF.Square,
                                                         bias=stat[:, 1:2], scale=1.0,
                                                         accum_out=stat[:, 2:3]))
        p.wait("vector", "scalar", m)
        m = p.do("vector", lambda e: e.tensor_scalar(stat[:, 3:4], stat[:, 2:3],
                                                     1.0 / D, LN_EPS, OP.mult, OP.add))
        p.wait("scalar", "vector", m)
        p.do("scalar", lambda e: e.activation(stat[:, 4:5], stat[:, 3:4], AF.Ln))
        m = p.do("scalar", lambda e: e.activation(stat[:, 5:6], stat[:, 4:5], AF.Exp, scale=-0.5))
        p.wait("vector", "scalar", m)
        m = p.do("vector", lambda e: e.tensor_tensor(stat[:, 6:7], stat[:, 1:2],
                                                     stat[:, 5:6], OP.mult))
        p.wait("scalar", "vector", m)
        m = p.do("scalar", lambda e, xt=xt, t=t: e.activation(
            xn[:, t * D:(t + 1) * D], xt, AF.Identity, bias=stat[:, 6:7], scale=stat[:, 5:6]))
        p.wait("vector", "scalar", m)
        m = p.do("vector", lambda e, t=t: e.tensor_copy(B["xn_bf"][:, :],
                                                        xn[:, t * D:(t + 1) * D]))
        p.wait("tensor", "vector", m)
        for c in range(KD):
            g = t * KD + c
            slot = psA if c % 2 == 0 else psB
            if g >= 2:
                p.wait("tensor", "vector", copy_marks[g - 2])
            p.do("tensor", lambda e, c=c, slot=slot: e.transpose(
                slot[:, 0:128], B["xn_bf"][:, c * 128:(c + 1) * 128], B["identbf"][:, :]))
            p.wait("vector", "tensor", p.mark("tensor"))
            m = p.do("vector", lambda e, t=t, c=c, slot=slot: e.tensor_copy(
                xnT[:, c * T + t * 128: c * T + (t + 1) * 128], slot[:, 0:128]))
            copy_marks.append(m)
        tile_done.append(p.mark("vector"))
    return tile_done


def emit_router(p, B, gwT, gb_row, n_out, gates, ln_tiles, name):
    xn = B["xn"]
    psA, psB = B["ps_a"], B["ps_b"]
    ps_r = B["ps_c"][:, 0:n_out]
    tch = p.sb(f"{name}_tch", [128, 2 * 128], F32)
    p.wait("tensor", "vector", ln_tiles[-1])
    copy_marks = []
    ev = 0
    for t in range(NT):
        if t > 0:
            p.wait("tensor", "vector", ev)
        for c in range(KD):
            g = t * KD + c
            slot, off = (psA, 0) if c % 2 == 0 else (psB, 128)
            if g >= 2:
                p.wait("tensor", "vector", copy_marks[g - 2])
            p.do("tensor", lambda e, t=t, c=c, slot=slot: e.transpose(
                slot[:, 0:128], xn[:, t * D + c * 128: t * D + (c + 1) * 128],
                B["ident32"][:, :]))
            p.wait("vector", "tensor", p.mark("tensor"))
            m = p.do("vector", lambda e, off=off, slot=slot: e.tensor_copy(
                tch[:, off:off + 128], slot[:, 0:128]))
            copy_marks.append(m)
            p.wait("tensor", "vector", m)
            p.do("tensor", lambda e, c=c, off=off: e.matmul(
                ps_r, tch[:, off:off + 128], gwT[:, c * n_out:(c + 1) * n_out],
                start=(c == 0), stop=False))
        mm = p.do("tensor", lambda e: e.matmul(ps_r, B["ones_row"][0:1, 0:128], gb_row,
                                               start=False, stop=True))
        p.wait("vector", "tensor", mm)
        ev = p.do("vector", lambda e, t=t: e.tensor_copy(
            gates[:, t * n_out:(t + 1) * n_out], ps_r))
    return ev


def emit_top2(p, B, gates, n_e, ohsum, name):
    tmp = p.sb(f"{name}_t2", [128, 2 * n_e], F32)
    m1 = B["stat"][:, 9:10]
    m2 = B["stat"][:, 10:11]
    last = 0
    for t in range(NT):
        gt = gates[:, t * n_e:(t + 1) * n_e]
        ot = ohsum[:, t * n_e:(t + 1) * n_e]
        p.do("vector", lambda e, gt=gt: e.tensor_reduce(m1, gt, AX.X, OP.max))
        p.do("vector", lambda e, gt=gt, ot=ot: e.tensor_scalar(
            ot, gt, m1, None, OP.is_ge, OP.bypass))
        p.do("vector", lambda e, ot=ot: e.tensor_scalar(
            tmp[:, 0:n_e], ot, -1e30, None, OP.mult, OP.bypass))
        p.do("vector", lambda e, gt=gt: e.tensor_tensor(
            tmp[:, 0:n_e], tmp[:, 0:n_e], gt, OP.add))
        p.do("vector", lambda e: e.tensor_reduce(m2, tmp[:, 0:n_e], AX.X, OP.max))
        p.do("vector", lambda e: e.tensor_scalar(
            tmp[:, n_e:2 * n_e], tmp[:, 0:n_e], m2, None, OP.is_ge, OP.bypass))
        last = p.do("vector", lambda e, ot=ot: e.tensor_tensor(
            ot, ot, tmp[:, n_e:2 * n_e], OP.add))
    return last


def emit_allreduce_row(p, B, row_ap, n, tag, op=OP.add, src_ready=("vector", 0)):
    din = p.dram(f"{tag}_in", [1, n], F32)
    dout = p.dram(f"{tag}_out", [1, n], F32)
    res = p.sb(f"{tag}_res", [1, n], F32)
    p.wait("gpsimd", src_ready[0], src_ready[1])
    m = p.dma("gpsimd", din[:, :], row_ap, tag)
    p.waitg("gpsimd", tag, m)
    m = p.cc("AllReduce", op, [din.ap()], [dout.ap()], grp=tag + "c")
    p.waitg("gpsimd", tag + "c", m)
    m = p.dma("gpsimd", res[0:1, :], dout[:, :], tag)
    return m, res


def bcast_row(p, B, row_ap, n, dst, src_ready):
    src, mk = src_ready
    p.wait("tensor", src, mk)
    mm = p.do("tensor", lambda e: e.matmul(B["ps_c"][:, 0:n], B["ones_row"][0:1, 0:128],
                                           row_ap, start=True, stop=True))
    p.wait("vector", "tensor", mm)
    return p.do("vector", lambda e: e.tensor_copy(dst[:, :], B["ps_c"][:, 0:n]))



def emit_l1_ffn(p, B, W, coef):
    E = 8
    ln_tiles = emit_ln(p, B)
    gwT = p.sb("l1_gwT", [128, KD * E], F32)
    gb = p.sb("l1_gb", [1, E], F32)
    g8 = p.sb("l1_g8", [E, D], F32)
    p.dma("sync", g8[:, :], W["gate_w"][:, :], "l1w")
    dm = p.dma("sync", gb[0:1, :], W["gate_b"][None, :], "l1w")
    p.waitg("tensor", "l1w", dm)
    p.wait("tensor", "vector", B["init_v"])
    for c in range(KD):
        mm = p.do("tensor", lambda e, c=c: e.matmul(
            B["ps_c"][:, 0:E], g8[0:E, c * 128:(c + 1) * 128], B["ident32"][0:E, 0:E],
            start=True, stop=True))
        p.wait("vector", "tensor", mm)
        m = p.do("vector", lambda e, c=c: e.tensor_copy(
            gwT[:, c * E:(c + 1) * E], B["ps_c"][:, 0:E]))
        p.wait("tensor", "vector", m)
    gates = p.sb("l1_gates", [128, NT * E], F32)
    ev = emit_router(p, B, gwT, gb[0:1, :], E, gates, ln_tiles, "l1")
    p.dbg["gates"] = gates
    ohsum = p.sb("l1_ohsum", [128, NT * E], F32)
    t2 = emit_top2(p, B, gates, E, ohsum, "l1")
    p.dbg["ohsum"] = ohsum

    ps_cnt = B["ps_d"][0:1, 0:E]
    p.wait("tensor", "vector", t2)
    mm = 0
    for t in range(NT):
        mm = p.do("tensor", lambda e, t=t: e.matmul(
            ps_cnt, B["ones_col"][:, 0:1], ohsum[:, t * E:(t + 1) * E],
            start=(t == 0), stop=(t == NT - 1)))
    cnt_row = p.sb("l1_cnt", [1, E], F32)
    p.wait("vector", "tensor", mm)
    m = p.do("vector", lambda e: e.tensor_copy(cnt_row[0:1, :], ps_cnt))
    arm, cnt_g = emit_allreduce_row(p, B, cnt_row[0:1, :], E, "l1cnt",
                                    src_ready=("vector", m))
    p.dbg["cnt_g"] = cnt_g
    p.waitg("vector", "l1cnt", arm)
    m = p.do("vector", lambda e: e.tensor_scalar(
        cnt_g[0:1, :], cnt_g[0:1, :], coef / (NCORE * T * 2), None, OP.mult, OP.bypass))
    scales = p.sb("l1_scales", [128, E], F32)
    sm = bcast_row(p, B, cnt_g[0:1, :], E, scales, ("vector", m))
    p.dbg["scales"] = scales

    acc = B["xn"]
    p.wait("gpsimd", "vector", ev)
    gz = p.do("gpsimd", lambda e: e.memset(acc[:, :], 0.0))

    sq = B["sq"]
    w0 = p.sb("l1_w0", [128, KD * D], BF16)
    b0 = p.sb("l1_b0", [1, D], F32)
    last_mm_of_expert = 0
    sq_consumed = 0
    evac = {}
    for e_i in range(E):
        if e_i >= 1:
            p.wait("gpsimd", "tensor", last_mm_of_expert)
        p.dma("gpsimd", w0[:, :].rearrange("p (c n) -> p c n", c=KD),
              W["expert_w"][e_i].rearrange("(c p) n -> p c n", p=128), "l1ew")
        wm = p.dma("gpsimd", b0[0:1, :], W["expert_b"][e_i:e_i + 1, :], "l1ew")
        p.waitg("tensor", "l1ew", wm)
        if e_i == 0:
            p.wait("tensor", "vector", sm)
            p.wait("tensor", "gpsimd", gz)
        for t in range(NT):
            g = e_i * NT + t
            half = (t % 2) * 1024
            ph0 = B["ps_w"][:, half:half + 512]
            ph1 = B["ps_w"][:, half + 512:half + 1024]
            if g >= 2:
                p.wait("tensor", "scalar", evac[g - 2])
            for c in range(KD):
                lhs = B["xnT"][:, c * T + t * 128: c * T + (t + 1) * 128]
                p.do("tensor", lambda e, lhs=lhs, c=c, ph0=ph0: e.matmul(
                    ph0, lhs, w0[:, c * D: c * D + 512], start=(c == 0), stop=False))
                p.do("tensor", lambda e, lhs=lhs, c=c, ph1=ph1: e.matmul(
                    ph1, lhs, w0[:, c * D + 512: (c + 1) * D], start=(c == 0), stop=False))
            p.do("tensor", lambda e, ph0=ph0: e.matmul(
                ph0, B["ones_row"][0:1, 0:128], b0[0:1, 0:512], start=False, stop=True))
            mm = p.do("tensor", lambda e, ph1=ph1: e.matmul(
                ph1, B["ones_row"][0:1, 0:128], b0[0:1, 512:1024], start=False, stop=True))
            last_mm_of_expert = mm
            p.wait("scalar", "tensor", mm)
            if g >= 1:
                p.wait("scalar", "vector", sq_consumed)
            p.do("scalar", lambda e, e_i=e_i, ph0=ph0: e.activation(
                sq[:, 0:512], ph0, AF.Relu, scale=scales[:, e_i:e_i + 1]))
            m = p.do("scalar", lambda e, e_i=e_i, ph1=ph1: e.activation(
                sq[:, 512:1024], ph1, AF.Relu, scale=scales[:, e_i:e_i + 1]))
            evac[g] = m
            p.wait("vector", "scalar", m)
            sq_consumed = p.do("vector", lambda e, t=t: e.tensor_tensor(
                acc[:, t * D:(t + 1) * D], acc[:, t * D:(t + 1) * D],
                sq[:, 0:D], OP.add))
    last = 0
    for t in range(NT):
        last = p.do("vector", lambda e, t=t: e.tensor_tensor(
            B["x_res"][:, t * D:(t + 1) * D], B["x_res"][:, t * D:(t + 1) * D],
            acc[:, t * D:(t + 1) * D], OP.add))
    return ("vector", last)




def emit_l2_ffn(p, B, W, coef):
    E = 16
    ln_tiles = emit_ln(p, B)
    gwT = p.sb("l2_gwT", [128, KD * E], F32)
    gb = p.sb("l2_gb", [1, E], F32)
    p.dma("sync", gwT[:, :].rearrange("p (c e) -> p c e", c=KD),
          W["router_w"].rearrange("(c p) e -> p c e", p=128), "l2w")
    dm = p.dma("sync", gb[0:1, :], W["router_bias"][None, :], "l2w")
    p.waitg("tensor", "l2w", dm)
    gates = p.sb("l2_gates", [128, NT * E], F32)
    ev = emit_router(p, B, gwT, gb[0:1, :], E, gates, ln_tiles, "l2")
    p.dbg["gates"] = gates

    comb = p.sb("l2_comb", [128, NT * E], F32)
    tmp = p.sb("l2_t2", [128, 2 * E], F32)
    m1 = B["stat"][:, 9:10]
    m2 = B["stat"][:, 10:11]
    s1 = B["stat"][:, 11:12]
    s2 = B["stat"][:, 12:13]
    comb_done = 0
    for t in range(NT):
        gt = gates[:, t * E:(t + 1) * E]
        ct = comb[:, t * E:(t + 1) * E]
        oh1 = tmp[:, 0:E]
        oh2 = tmp[:, E:2 * E]
        p.do("vector", lambda e, gt=gt: e.tensor_reduce(m1, gt, AX.X, OP.max))
        p.do("vector", lambda e, gt=gt, oh1=oh1: e.tensor_scalar(
            oh1, gt, m1, None, OP.is_ge, OP.bypass))
        p.do("vector", lambda e, oh1=oh1, oh2=oh2: e.tensor_scalar(
            oh2, oh1, -1e30, None, OP.mult, OP.bypass))
        p.do("vector", lambda e, gt=gt, oh2=oh2: e.tensor_tensor(
            oh2, oh2, gt, OP.add))
        p.do("vector", lambda e, oh2=oh2: e.tensor_reduce(m2, oh2, AX.X, OP.max))
        m = p.do("vector", lambda e, oh2=oh2: e.tensor_scalar(
            oh2, oh2, m2, None, OP.is_ge, OP.bypass))
        p.do("vector", lambda e: e.tensor_tensor(s1, m1, m2, OP.subtract))
        m = p.do("vector", lambda e: e.tensor_copy(B["stat"][:, 13:14], s1))
        p.wait("scalar", "vector", m)
        m = p.do("scalar", lambda e: e.activation(s1, B["stat"][:, 13:14], AF.Sigmoid))
        p.wait("vector", "scalar", m)
        p.do("vector", lambda e: e.tensor_scalar(s2, s1, -1.0, 1.0, OP.mult, OP.add))
        p.do("vector", lambda e, ct=ct, oh1=oh1: e.tensor_scalar(
            ct, oh1, s1, None, OP.mult, OP.bypass))
        p.do("vector", lambda e, oh2=oh2: e.tensor_scalar(
            oh2, oh2, s2, None, OP.mult, OP.bypass))
        p.do("vector", lambda e, ct=ct, oh2=oh2: e.tensor_tensor(
            ct, ct, oh2, OP.add))
        comb_done = p.do("vector", lambda e, ct=ct: e.tensor_scalar(
            ct, ct, coef, None, OP.mult, OP.bypass))

    F = 512
    FD = F // 128
    sq = B["sq"]
    p.wait("gpsimd", "vector", ev)
    xn_bf_view = B["xn"].bitcast(BF16)
    hidT = xn_bf_view[:, 0:FD * T]
    gpW = xn_bf_view[:, 8192:8192 + KD * F]
    upW = xn_bf_view[:, 12288:12288 + KD * F]
    dnW = xn_bf_view[:, 16384:16384 + FD * D]
    xnT = B["xnT"]
    first = True
    hid_consumed = 0
    evac = {}
    g_iter = 0
    sq_consumed = 0
    hid_marks = {}
    for it in range(E + 2):
        if it < E:
            srcs = [(W["up_proj"][it], upW), (W["gate_proj"][it], gpW)]
            dn_src = W["down_proj"][it]
        else:
            half = it - E
            srcs = [(W["sh_up"][:, half * F:(half + 1) * F], upW),
                    (W["sh_gate"][:, half * F:(half + 1) * F], gpW)]
            dn_src = W["sh_down"][half * F:(half + 1) * F, :]
        if not first:
            p.wait("gpsimd", "tensor", hid_consumed)
        for src, dst in srcs:
            p.dma("gpsimd", dst[:, :].rearrange("p (c f) -> p c f", c=KD),
                  src.rearrange("(c p) f -> p c f", p=128), "l2ew")
        wm = p.dma("gpsimd", dnW[:, :].rearrange("p (c d) -> p c d", c=FD),
                   dn_src.rearrange("(c p) d -> p c d", p=128), "l2ew")
        p.waitg("tensor", "l2ew", wm)
        if first:
            p.wait("tensor", "vector", comb_done)
            first = False
        if it >= 1:
            p.wait("tensor", "scalar", evac[it * NT - 1])
        for ht in range(FD):
            for nb in range(4):
                slot = ((ht * 4 + nb) % 2) * 1024
                psG = B["ps_w"][:, slot:slot + 512]
                psU = B["ps_w"][:, slot + 512:slot + 1024]
                if g_iter >= 2:
                    p.wait("tensor", "vector", hid_marks[g_iter - 2])
                for c in range(KD):
                    p.do("tensor", lambda e, c=c, ht=ht, nb=nb, psG=psG: e.matmul(
                        psG, gpW[:, c * F + ht * 128: c * F + (ht + 1) * 128],
                        xnT[:, c * T + nb * 512: c * T + (nb + 1) * 512],
                        start=(c == 0), stop=(c == KD - 1)))
                mm = 0
                for c in range(KD):
                    mm = p.do("tensor", lambda e, c=c, ht=ht, nb=nb, psU=psU: e.matmul(
                        psU, upW[:, c * F + ht * 128: c * F + (ht + 1) * 128],
                        xnT[:, c * T + nb * 512: c * T + (nb + 1) * 512],
                        start=(c == 0), stop=(c == KD - 1)))
                p.wait("scalar", "tensor", mm)
                if g_iter >= 1:
                    p.wait("scalar", "vector", max(hid_marks[g_iter - 1], sq_consumed))
                m = p.do("scalar", lambda e, psG=psG: e.activation(
                    sq[:, 0:512], psG, AF.Silu))
                p.wait("vector", "scalar", m)
                hid_marks[g_iter] = p.do("vector", lambda e, ht=ht, nb=nb, psU=psU: e.tensor_tensor(
                    hidT[:, ht * T + nb * 512: ht * T + (nb + 1) * 512],
                    sq[:, 0:512], psU, OP.mult))
                g_iter += 1
        p.wait("tensor", "vector", hid_marks[g_iter - 1])
        for t in range(NT):
            gidx = it * NT + t
            half = (t % 2) * 1024
            ph0 = B["ps_w"][:, half:half + 512]
            ph1 = B["ps_w"][:, half + 512:half + 1024]
            if gidx >= 2 and (it * NT + t - 2) in evac:
                p.wait("tensor", "scalar", evac[it * NT + t - 2])
            for c in range(FD):
                lhs = hidT[:, c * T + t * 128: c * T + (t + 1) * 128]
                p.do("tensor", lambda e, lhs=lhs, c=c, ph0=ph0: e.matmul(
                    ph0, lhs, dnW[:, c * D: c * D + 512], start=(c == 0), stop=False))
                last = p.do("tensor", lambda e, lhs=lhs, c=c, ph1=ph1: e.matmul(
                    ph1, lhs, dnW[:, c * D + 512: (c + 1) * D],
                    start=(c == 0), stop=(c == FD - 1)))
            mm = p.do("tensor", lambda e, ph0=ph0: e.matmul(
                ph0, B["ones_row"][0:1, 0:128], B["zero_row"][0:1, 0:512],
                start=False, stop=True))
            hid_consumed = mm
            p.wait("scalar", "tensor", mm)
            p.wait("scalar", "vector", sq_consumed)
            if it < E:
                sc = comb[:, t * E + it: t * E + it + 1]
                p.do("scalar", lambda e, ph0=ph0, sc=sc: e.activation(
                    sq[:, 0:512], ph0, AF.Identity, scale=sc))
                m = p.do("scalar", lambda e, ph1=ph1, sc=sc: e.activation(
                    sq[:, 512:1024], ph1, AF.Identity, scale=sc))
            else:
                p.do("scalar", lambda e, ph0=ph0: e.activation(
                    sq[:, 0:512], ph0, AF.Identity, scale=coef))
                m = p.do("scalar", lambda e, ph1=ph1: e.activation(
                    sq[:, 512:1024], ph1, AF.Identity, scale=coef))
            evac[it * NT + t] = m
            p.wait("vector", "scalar", m)
            sq_consumed = p.do("vector", lambda e, t=t: e.tensor_tensor(
                B["x_res"][:, t * D:(t + 1) * D], B["x_res"][:, t * D:(t + 1) * D],
                sq[:, 0:D], OP.add))
    return ("vector", sq_consumed)




def emit_l0_attn(p, B, W, coef):
    NH, DK, DC, DR = 8, 128, 32, 32
    SCALE = 1.0 / float(np.sqrt(DK + DR))
    ln_tiles = emit_ln(p, B)
    xnT = B["xnT"]

    wdq = p.sb("a0_wdq", [128, KD * DC], BF16)
    wdkv = p.sb("a0_wdkv", [128, KD * DC], BF16)
    wkr = p.sb("a0_wkr", [128, KD * DR], BF16)
    wuq = p.sb("a0_wuq", [32, NH * DK], BF16)
    wuk = p.sb("a0_wuk", [32, NH * DK], BF16)
    wuv = p.sb("a0_wuv", [32, NH * DK], BF16)
    wqr = p.sb("a0_wqr", [32, NH * DR], BF16)
    wo = p.sb("a0_wo", [128, D], BF16)
    bdq = p.sb("a0_bdq", [32, 1], F32)
    bdkv = p.sb("a0_bdkv", [32, 1], F32)
    bkr = p.sb("a0_bkr", [32, 1], F32)
    buq = p.sb("a0_buq", [128, NH], F32)
    buk = p.sb("a0_buk", [128, NH], F32)
    bqr = p.sb("a0_bqr", [32, NH], F32)
    buv_row = p.sb("a0_buvr", [1, NH * DK], F32)
    bo = p.sb("a0_bo", [1, D], F32)
    g = "a0w"
    p.dma("gpsimd", wdq[:, :].rearrange("p (c n) -> p c n", c=KD),
          W["W_DQ"].rearrange("(c p) n -> p c n", p=128), g)
    p.dma("gpsimd", wdkv[:, :].rearrange("p (c n) -> p c n", c=KD),
          W["W_DKV"].rearrange("(c p) n -> p c n", p=128), g)
    p.dma("gpsimd", wkr[:, :].rearrange("p (c n) -> p c n", c=KD),
          W["W_KR"].rearrange("(c p) n -> p c n", p=128), g)
    p.dma("gpsimd", wuq[:, :], W["W_UQ"][:, :], g)
    p.dma("gpsimd", wuk[:, :], W["W_UK"][:, :], g)
    p.dma("gpsimd", wuv[:, :], W["W_UV"][:, :], g)
    p.dma("gpsimd", wqr[:, :], W["W_QR"][:, :], g)
    gb_ = "a0b"
    p.dma("sync", bdq[:, 0:1], W["b_DQ"].rearrange("(p one) -> p one", one=1), gb_)
    p.dma("sync", bdkv[:, 0:1], W["b_DKV"].rearrange("(p one) -> p one", one=1), gb_)
    p.dma("sync", bkr[:, 0:1], W["b_KR"].rearrange("(p one) -> p one", one=1), gb_)
    p.dma("sync", buq[:, :], W["b_UQ"].rearrange("(h p) -> p h", p=128), gb_,
          allow_slow_non_contiguous=True)
    p.dma("sync", buk[:, :], W["b_UK"].rearrange("(h p) -> p h", p=128), gb_,
          allow_slow_non_contiguous=True)
    p.dma("sync", bqr[:, :], W["b_QR"].rearrange("(h p) -> p h", p=32), gb_,
          allow_slow_non_contiguous=True)
    p.dma("sync", buv_row[0:1, :], W["b_UV"][None, :], gb_)
    wmb = p.dma("sync", bo[0:1, :], W["b_O"][None, :], gb_)
    wm = p.gsem(g)[1]

    xnbf = B["xn"].bitcast(BF16)
    qcT = xnbf[:, 0:T]
    kcT = xnbf[:, T:2 * T]
    attn_n = xnbf[:, 2 * T:3 * T]
    attnT = xnbf[:, 3 * T:4 * T]
    vtok = xnbf[:, 4 * T:5 * T]
    oT = xnbf[:, 5 * T:6 * T]
    attn_sb = B["xn"][:, 6144:6144 + T]
    cqT = p.sb("a0_cq", [32, T], BF16)
    ckvT = p.sb("a0_ckv", [32, T], BF16)
    krT = p.sb("a0_kr", [32, T], BF16)
    qrT = xnbf[0:32, 16384:16384 + NH * T]

    p.waitg("tensor", g, wm)
    p.waitg("tensor", gb_, wmb)
    p.waitg("scalar", gb_, wmb)
    p.wait("tensor", "vector", ln_tiles[-1])

    rmax = B["stat"][:, 9:10]
    rsum = B["stat"][:, 10:11]
    rinv = B["stat"][:, 11:12]

    def proj32(dst, wT, bias_col, evm):
        for nb in range(4):
            ps = B["ps_c"][0:32, 0:512]
            if evm:
                p.wait("tensor", "scalar", evm)
            for c in range(KD):
                p.do("tensor", lambda e, c=c, nb=nb, wT=wT, ps=ps: e.matmul(
                    ps, wT[:, c * DC:(c + 1) * DC],
                    xnT[:, c * T + nb * 512:c * T + (nb + 1) * 512],
                    start=(c == 0), stop=(c == KD - 1)))
            p.wait("scalar", "tensor", p.mark("tensor"))
            evm = p.do("scalar", lambda e, nb=nb, dst=dst, bias_col=bias_col, ps=ps:
                       e.activation(dst[:, nb * 512:(nb + 1) * 512], ps,
                                    AF.Identity, bias=bias_col))
        return evm

    m1 = proj32(cqT, wdq, bdq[0:32, 0:1], 0)
    m2 = proj32(ckvT, wdkv, bdkv[0:32, 0:1], m1)
    m3 = proj32(krT, wkr, bkr[0:32, 0:1], m2)
    p.wait("tensor", "scalar", m1)
    evm = 0
    for h in range(NH):
        for nb in range(4):
            ps = B["ps_d"][0:32, 0:512]
            if evm:
                p.wait("tensor", "scalar", evm)
            p.do("tensor", lambda e, h=h, nb=nb, ps=ps: e.matmul(
                ps, wqr[:, h * DR:(h + 1) * DR], cqT[:, nb * 512:(nb + 1) * 512],
                start=True, stop=True))
            p.wait("scalar", "tensor", p.mark("tensor"))
            evm = p.do("scalar", lambda e, h=h, nb=nb, ps=ps: e.activation(
                qrT[:, h * T + nb * 512:h * T + (nb + 1) * 512], ps, AF.Identity,
                bias=bqr[0:32, h:h + 1]))
    qr_done = evm
    p.wait("tensor", "scalar", m2)
    p.wait("tensor", "scalar", m3)
    p.wait("tensor", "scalar", qr_done)

    psAb = B["ps_a"].bitcast(BF16)
    psBb = B["ps_b"].bitcast(BF16)
    evm_c = m3
    evm_d = qr_done
    psw_reader_s = 0
    psw_reader_v = 0
    mbf = 0
    tr_last = 0
    res_add = 0
    for h in range(NH):
        for dst, wmat, bcol, src_ in ((qcT, wuq, buq, cqT), (kcT, wuk, buk, ckvT)):
            for nb in range(4):
                ps = B["ps_c"][:, 0:512]
                if evm_c:
                    p.wait("tensor", "scalar", evm_c)
                p.do("tensor", lambda e, h=h, nb=nb, wmat=wmat, src_=src_, ps=ps:
                     e.matmul(ps, wmat[:, h * DK:(h + 1) * DK],
                              src_[:, nb * 512:(nb + 1) * 512], start=True, stop=True))
                p.wait("scalar", "tensor", p.mark("tensor"))
                evm_c = p.do("scalar", lambda e, nb=nb, dst=dst, bcol=bcol, h=h, ps=ps:
                             e.activation(dst[:, nb * 512:(nb + 1) * 512], ps,
                                          AF.Identity, bias=bcol[:, h:h + 1]))
        for t in range(NT):
            ps = B["ps_d"][:, 0:128]
            if evm_d:
                p.wait("tensor", "scalar", evm_d)
            p.do("tensor", lambda e, h=h, t=t, ps=ps: e.matmul(
                ps, ckvT[:, t * 128:(t + 1) * 128], wuv[:, h * DK:(h + 1) * DK],
                start=True, stop=False))
            p.do("tensor", lambda e, h=h, ps=ps: e.matmul(
                ps, B["ones_row"][0:1, 0:128], buv_row[0:1, h * DK:(h + 1) * DK],
                start=False, stop=True))
            p.wait("scalar", "tensor", p.mark("tensor"))
            evm_d = p.do("scalar", lambda e, t=t, ps=ps: e.activation(
                vtok[:, t * 128:(t + 1) * 128], ps, AF.Identity))
        p.wait("tensor", "scalar", evm_c)
        p.wait("tensor", "scalar", evm_d)
        for qt in range(NT):
            if psw_reader_s:
                p.wait("tensor", "scalar", psw_reader_s)
            if psw_reader_v:
                p.wait("tensor", "vector", psw_reader_v)
            for nb in range(4):
                p.do("tensor", lambda e, h=h, qt=qt, nb=nb: e.matmul(
                    B["ps_w"][:, nb * 512:(nb + 1) * 512],
                    qcT[:, qt * 128:(qt + 1) * 128],
                    kcT[:, nb * 512:(nb + 1) * 512], start=True, stop=False))
                p.do("tensor", lambda e, h=h, qt=qt, nb=nb: e.matmul(
                    B["ps_w"][:, nb * 512:(nb + 1) * 512],
                    qrT[0:32, h * T + qt * 128:h * T + (qt + 1) * 128],
                    krT[:, nb * 512:(nb + 1) * 512], start=False, stop=True))
            smm = p.mark("tensor")
            p.wait("vector", "tensor", smm)
            p.do("vector", lambda e: e.tensor_reduce(rmax, B["ps_w"][:, 0:2048],
                                                     AX.X, OP.max))
            mneg = p.do("vector", lambda e: e.tensor_scalar(
                rmax, rmax, -SCALE, None, OP.mult, OP.bypass))
            p.wait("scalar", "vector", mneg)
            psw_reader_s = mexp = p.do("scalar", lambda e: e.activation(
                attn_sb[:, :], B["ps_w"][:, 0:2048], AF.Exp, bias=rmax,
                scale=SCALE, accum_out=rsum))
            p.wait("vector", "scalar", mexp)
            p.do("vector", lambda e: e.reciprocal(rinv, rsum))
            if tr_last:
                p.wait("vector", "tensor", tr_last)
            mbf = p.do("vector", lambda e: e.tensor_scalar(
                attn_n[:, 0:T], attn_sb[:, :], rinv, None, OP.mult, OP.bypass))
            p.wait("tensor", "vector", mbf)
            mcp = {}
            for st in range(NT):
                slot = psAb if st % 2 == 0 else psBb
                if st >= 2:
                    p.wait("tensor", "vector", mcp[st - 2])
                p.do("tensor", lambda e, st=st, slot=slot: e.transpose(
                    slot[:, 0:128], attn_n[:, st * 128:(st + 1) * 128],
                    B["identbf"][:, :]))
                tr_last = p.mark("tensor")
                p.wait("vector", "tensor", tr_last)
                mcp[st] = p.do("vector", lambda e, st=st, slot=slot: e.tensor_copy(
                    attnT[:, st * 128:(st + 1) * 128], slot[:, 0:128]))
            p.wait("tensor", "vector", mcp[NT - 1])
            if evm_c:
                p.wait("tensor", "scalar", evm_c)
            pvm = 0
            for st in range(NT):
                pvm = p.do("tensor", lambda e, st=st: e.matmul(
                    B["ps_c"][:, 0:128], vtok[:, st * 128:(st + 1) * 128],
                    attnT[:, st * 128:(st + 1) * 128], start=(st == 0),
                    stop=(st == NT - 1)))
            p.wait("scalar", "tensor", pvm)
            evm_c = p.do("scalar", lambda e, qt=qt: e.activation(
                oT[:, qt * 128:(qt + 1) * 128], B["ps_c"][:, 0:128], AF.Identity))
        if res_add:
            p.wait("gpsimd", "vector", res_add)
        wom = p.dma("gpsimd", wo[:, :], W["W_O"][h * 128:(h + 1) * 128, :], "a0wo")
        p.waitg("tensor", "a0wo", wom)
        p.wait("tensor", "scalar", evm_c)
        if psw_reader_s:
            p.wait("tensor", "scalar", psw_reader_s)
        for t in range(NT):
            ph0 = B["ps_w"][:, 0:512]
            ph1 = B["ps_w"][:, 512:1024]
            if res_add:
                p.wait("tensor", "vector", res_add)
            p.do("tensor", lambda e, t=t, h=h: e.matmul(
                ph0, oT[:, t * 128:(t + 1) * 128], wo[:, 0:512],
                start=True, stop=(h != 0)))
            wmm = p.do("tensor", lambda e, t=t, h=h: e.matmul(
                ph1, oT[:, t * 128:(t + 1) * 128], wo[:, 512:1024],
                start=True, stop=(h != 0)))
            if h == 0:
                p.do("tensor", lambda e: e.matmul(
                    ph0, B["ones_row"][0:1, 0:128], bo[0:1, 0:512],
                    start=False, stop=True))
                wmm = p.do("tensor", lambda e: e.matmul(
                    ph1, B["ones_row"][0:1, 0:128], bo[0:1, 512:1024],
                    start=False, stop=True))
            p.wait("vector", "tensor", wmm)
            res_add = p.do("vector", lambda e, t=t: e.tensor_tensor(
                B["x_res"][:, t * D:(t + 1) * D], B["x_res"][:, t * D:(t + 1) * D],
                B["ps_w"][:, 0:1024], OP.add))
        psw_reader_v = res_add
    return ("vector", res_add)




def emit_l1_attn(p, B, W, coef):
    NH, HD = 16, 64
    ln_tiles = emit_ln(p, B)
    xnT = B["xnT"]
    gammas = [1.0 - 2.0 ** (-5.0 - h) for h in range(NH)]

    wqkv = p.sb("a1_w", [128, KD * 3 * HD], BF16)
    pkv = p.sb("a1_pkv", [64, NH * HD], BF16)
    pkv32 = p.sb("a1_pkv32", [64, NH * HD], F32)
    gnw = p.sb("a1_gnw", [1, D], F32)
    gnb = p.sb("a1_gnb", [1, D], F32)
    tokidx = p.sb("a1_tok", [128, NT], F32)
    gneg = p.sb("a1_gneg", [128, NT], F32)
    gpos = p.sb("a1_gpos", [128, NT], F32)
    iotaR = p.sb("a1_iR", [128, 128], F32)
    iotaP = p.sb("a1_iP", [128, 1], F32)
    trT = p.sb("a1_trT", [128, 128], F32)
    arow = p.sb("a1_arow", [1, D], F32)
    brow = p.sb("a1_brow", [1, D], F32)
    stats2 = p.sb("a1_st", [128, 8], F32)
    abc = p.sb("a1_abc", [128, 2 * D], F32)

    xnbf = B["xn"].bitcast(BF16)
    qkv_tok = xnbf[:, 0:NT * 192]
    qks = xnbf[:, 4096:4096 + NT * 128]
    qhT = xnbf[0:64, 8192:8192 + T]
    ktT = xnbf[0:64, 10240:10240 + T]
    qTr = xnbf[:, 12288:12288 + T]
    ret_bf = xnbf[:, 14336:14336 + 256]
    oTs = xnbf[:, 14848:14848 + 128]
    r_tok = xnbf[:, 16384:32768]

    gw = "a1w"
    m = p.dma("gpsimd", pkv32[:, :].rearrange("p (h e) -> p h e", h=NH),
              W["past_kv"].rearrange("h p e -> p h e"), "a1pkv")
    p.dma("sync", gnw[0:1, :], W["gn_weight_full"][None, :], "a1b")
    gm = p.dma("sync", gnb[0:1, :], W["gn_bias_full"][None, :], "a1b")
    p.waitg("vector", "a1pkv", m)
    pkm = p.do("vector", lambda e: e.tensor_scalar(
        pkv[:, :], pkv32[:, :], float(NH * HD), None, OP.mult, OP.bypass))
    gi1 = p.do("gpsimd", lambda e: e.iota(tokidx[:, :], pattern=[[128, NT]], base=0,
                                          channel_multiplier=1,
                                          allow_small_or_imprecise_dtypes=True))
    gi2 = p.do("gpsimd", lambda e: e.iota(iotaR[:, :], pattern=[[1, 128]], base=0,
                                          channel_multiplier=0,
                                          allow_small_or_imprecise_dtypes=True))
    gi3 = p.do("gpsimd", lambda e: e.iota(iotaP[:, :], pattern=[[0, 1]], base=0,
                                          channel_multiplier=1,
                                          allow_small_or_imprecise_dtypes=True))
    p.wait("vector", "gpsimd", gi3)
    trm = p.do("vector", lambda e: e.tensor_scalar(
        trT[:, :], iotaR[:, :], iotaP[:, 0:1], None, OP.is_ge, OP.bypass))

    psA, psB = B["ps_a"], B["ps_b"]
    evm_c = 0
    evm_d = 0
    ret_ev = {}
    rtok_w = 0
    tr_cp = {}
    p.wait("tensor", "vector", ln_tiles[-1])
    p.wait("tensor", "vector", trm)
    p.wait("tensor", "vector", pkm)
    p.wait("scalar", "gpsimd", gi3)
    p.wait("gpsimd", "vector", ln_tiles[-1])
    gz0 = p.do("gpsimd", lambda e: e.memset(oTs[:, :], 0.0))
    p.wait("scalar", "gpsimd", gz0)
    for h in range(NH):
        gam = gammas[h]
        lng = float(np.log(gam))
        if rtok_w:
            p.wait("scalar", "vector", rtok_w)
        mgp = p.do("scalar", lambda e, lng=lng: e.activation(gneg[:, :], tokidx[:, :],
                                                             AF.Exp, scale=-lng))
        mgp = p.do("scalar", lambda e, lng=lng: e.activation(gpos[:, :], tokidx[:, :],
                                                             AF.Exp, scale=lng))
        if h > 0:
            p.wait("gpsimd", "tensor", p.mark("tensor"))
        wq3 = wqkv[:, :].rearrange("p (c g) -> p c g", c=KD)
        for j, off in enumerate((0, D, 2 * D)):
            wm = p.dma("gpsimd", wq3[:, :, j * HD:(j + 1) * HD],
                       W["W_qkv"][:, off + h * HD: off + (h + 1) * HD]
                       .rearrange("(c p) f -> p c f", p=128), gw)
        p.waitg("tensor", gw, wm)
        for t in range(NT):
            ps = B["ps_c"][:, 0:192]
            if evm_c:
                p.wait("tensor", "scalar", evm_c)
            for c in range(KD):
                p.do("tensor", lambda e, c=c, t=t, ps=ps: e.matmul(
                    ps, xnT[:, c * T + t * 128:(c * T + (t + 1) * 128)],
                    wqkv[:, c * 192:(c + 1) * 192], start=(c == 0), stop=(c == KD - 1)))
            p.wait("scalar", "tensor", p.mark("tensor"))
            evm_c = p.do("scalar", lambda e, t=t, ps=ps: e.activation(
                qkv_tok[:, t * 192:(t + 1) * 192], ps, AF.Identity))
        p.wait("vector", "scalar", evm_c)
        mqs = 0
        for t in range(NT):
            mqs = p.do("vector", lambda e, t=t: e.tensor_scalar(
                qks[:, t * 128:(t + 1) * 128], qkv_tok[:, t * 192:t * 192 + 128],
                gneg[:, t:t + 1], None, OP.mult, OP.bypass))
        p.wait("tensor", "vector", mqs)
        for t in range(NT):
            slot = psA.bitcast(BF16) if t % 2 == 0 else psB.bitcast(BF16)
            if t >= 2:
                p.wait("tensor", "vector", tr_cp[t - 2])
            p.do("tensor", lambda e, t=t, slot=slot: e.transpose(
                slot[:, 0:128], qks[:, t * 128:(t + 1) * 128], B["identbf"][:, :]))
            p.do("tensor", lambda e, t=t, slot=slot: e.transpose(
                slot[:, 128:256], qkv_tok[:, t * 192:t * 192 + 128],
                B["identbf"][:, :]))
            mt = p.mark("tensor")
            p.wait("vector", "tensor", mt)
            p.do("vector", lambda e, t=t, slot=slot: e.tensor_copy(
                qhT[:, t * 128:(t + 1) * 128], slot[0:64, 0:128]))
            p.do("vector", lambda e, t=t, slot=slot: e.tensor_copy(
                ktT[:, t * 128:(t + 1) * 128], slot[64:128, 0:128]))
            tr_cp[t] = p.do("vector", lambda e, t=t, slot=slot: e.tensor_copy(
                qTr[0:64, t * 128:(t + 1) * 128], slot[0:64, 128:256]))
        p.wait("tensor", "vector", tr_cp[NT - 1])
        for qt in range(NT):
            if evm_c:
                p.wait("tensor", "scalar", evm_c)
            p.do("tensor", lambda e, h=h, qt=qt: e.matmul(
                B["ps_c"][0:64, 0:128], pkv[:, h * HD:(h + 1) * HD],
                qhT[:, qt * 128:(qt + 1) * 128], start=True, stop=False))
            for st in range(qt + 1):
                sidx = qt * (qt + 1) // 2 + st
                sps = psA if sidx % 2 == 0 else psB
                p.do("tensor", lambda e, st=st, qt=qt, sps=sps: e.matmul(
                    sps[:, 0:128], ktT[:, st * 128:(st + 1) * 128],
                    qTr[0:64, qt * 128:(qt + 1) * 128], start=True, stop=True))
                mt = p.mark("tensor")
                if st == qt:
                    p.wait("vector", "tensor", mt)
                    mm_ = p.do("vector", lambda e, sps=sps: e.tensor_tensor(
                        sps[:, 0:128], sps[:, 0:128], trT[:, :], OP.mult))
                    p.wait("scalar", "vector", mm_)
                else:
                    p.wait("scalar", "tensor", mt)
                slotr = (sidx % 2) * 128
                ret_ev[sidx] = p.do("scalar", lambda e, slotr=slotr, sps=sps: e.activation(
                    ret_bf[:, slotr:slotr + 128], sps[:, 0:128], AF.Identity))
                p.wait("tensor", "scalar", ret_ev[sidx])
                p.do("tensor", lambda e, st=st, slotr=slotr, qt=qt: e.matmul(
                    B["ps_c"][0:64, 0:128], qkv_tok[:, st * 192 + 128:(st + 1) * 192],
                    ret_bf[:, slotr:slotr + 128], start=False, stop=(st == qt)))
            p.wait("scalar", "tensor", p.mark("tensor"))
            evm_c = p.do("scalar", lambda e: e.activation(oTs[0:64, 0:128],
                                                          B["ps_c"][0:64, 0:128],
                                                          AF.Identity))
            p.wait("tensor", "scalar", evm_c)
            if evm_d:
                p.wait("tensor", "scalar", evm_d)
            if rtok_w:
                p.wait("tensor", "vector", rtok_w)
            p.do("tensor", lambda e: e.transpose(
                B["ps_d"].bitcast(BF16)[:, 0:128], oTs[:, 0:128],
                B["identbf"][:, :]))
            mt = p.mark("tensor")
            p.wait("vector", "tensor", mt)
            rtok_w = p.do("vector", lambda e, h=h, qt=qt: e.tensor_scalar(
                r_tok[:, qt * D + h * HD: qt * D + (h + 1) * HD],
                B["ps_d"].bitcast(BF16)[:, 0:64], gpos[:, qt:qt + 1], None,
                OP.mult, OP.bypass))
    s1c = stats2[:, 0:1]
    s2c = stats2[:, 1:2]
    p.wait("scalar", "vector", rtok_w)
    p.wait("vector", "vector", rtok_w) if False else None
    acc1 = stats2[:, 2:3]
    acc2 = stats2[:, 3:4]
    vdone = 0
    for t in range(NT):
        if vdone:
            p.wait("scalar", "vector", vdone)
        m_ = p.do("scalar", lambda e, t=t: e.activation(
            B["sq"][:, :], r_tok[:, t * D:(t + 1) * D], AF.Square,
            accum_out=s2c))
        p.wait("vector", "scalar", m_)
        p.do("vector", lambda e, t=t: e.tensor_reduce(
            s1c, r_tok[:, t * D:(t + 1) * D], AX.X, OP.add))
        if t == 0:
            p.do("vector", lambda e: e.tensor_copy(acc1, s1c))
            vdone = p.do("vector", lambda e: e.tensor_copy(acc2, s2c))
        else:
            p.do("vector", lambda e: e.tensor_tensor(acc1, acc1, s1c, OP.add))
            vdone = p.do("vector", lambda e: e.tensor_tensor(acc2, acc2, s2c, OP.add))
    mv = p.do("vector", lambda e: e.tensor_copy(stats2[:, 4:6], stats2[:, 2:4]))
    p.wait("tensor", "vector", mv)
    mm_ = p.do("tensor", lambda e: e.matmul(B["ps_d"][0:1, 0:2], B["ones_col"][:, 0:1],
                                            stats2[:, 4:6], start=True, stop=True))
    p.wait("vector", "tensor", mm_)
    NTOT = float(D * T)
    p.do("vector", lambda e: e.tensor_scalar(stats2[0:1, 6:8], B["ps_d"][0:1, 0:2],
                                             1.0 / NTOT, None, OP.mult, OP.bypass))
    p.do("vector", lambda e: e.tensor_tensor(stats2[0:1, 4:5], stats2[0:1, 6:7],
                                             stats2[0:1, 6:7], OP.mult))
    mv = p.do("vector", lambda e: e.tensor_tensor(stats2[0:1, 5:6], stats2[0:1, 7:8],
                                                  stats2[0:1, 4:5], OP.subtract))
    mv = p.do("vector", lambda e: e.tensor_scalar(stats2[0:1, 5:6], stats2[0:1, 5:6],
                                                   1.0, 1e-5, OP.mult, OP.add))
    p.wait("scalar", "vector", mv)
    p.do("scalar", lambda e: e.activation(stats2[0:1, 4:5], stats2[0:1, 5:6], AF.Ln))
    ms = p.do("scalar", lambda e: e.activation(stats2[0:1, 5:6], stats2[0:1, 4:5],
                                               AF.Exp, scale=-0.5))
    p.waitg("vector", "a1b", gm)
    p.wait("vector", "scalar", ms)
    p.do("vector", lambda e: e.tensor_scalar(arow[0:1, :], gnw[0:1, :],
                                             stats2[0:1, 5:6], None, OP.mult, OP.bypass))
    p.do("vector", lambda e: e.tensor_scalar(brow[0:1, :], arow[0:1, :],
                                             stats2[0:1, 6:7], -1.0, OP.mult, OP.mult))
    mv = p.do("vector", lambda e: e.tensor_tensor(brow[0:1, :], brow[0:1, :],
                                                  gnb[0:1, :], OP.add))
    p.wait("tensor", "vector", mv)
    mm_ = p.do("tensor", lambda e: e.matmul(B["ps_w"][:, 0:512], B["ones_row"][0:1, 0:128],
                                            arow[0:1, 0:512], start=True, stop=True))
    mm_ = p.do("tensor", lambda e: e.matmul(B["ps_w"][:, 512:1024], B["ones_row"][0:1, 0:128],
                                            arow[0:1, 512:1024], start=True, stop=True))
    mm_ = p.do("tensor", lambda e: e.matmul(B["ps_w"][:, 1024:1536], B["ones_row"][0:1, 0:128],
                                            brow[0:1, 0:512], start=True, stop=True))
    mm_ = p.do("tensor", lambda e: e.matmul(B["ps_w"][:, 1536:2048], B["ones_row"][0:1, 0:128],
                                            brow[0:1, 512:1024], start=True, stop=True))
    p.wait("vector", "tensor", mm_)
    p.do("vector", lambda e: e.tensor_copy(abc[:, 0:D], B["ps_w"][:, 0:1024]))
    mv = p.do("vector", lambda e: e.tensor_copy(abc[:, D:2 * D], B["ps_w"][:, 1024:2048]))
    last = 0
    for t in range(NT):
        p.do("vector", lambda e, t=t: e.tensor_tensor(
            B["sq"][:, :], r_tok[:, t * D:(t + 1) * D], abc[:, 0:D], OP.mult))
        p.do("vector", lambda e: e.tensor_tensor(
            B["sq"][:, :], B["sq"][:, :], abc[:, D:2 * D], OP.add))
        last = p.do("vector", lambda e, t=t: e.tensor_tensor(
            B["x_res"][:, t * D:(t + 1) * D], B["x_res"][:, t * D:(t + 1) * D],
            B["sq"][:, :], OP.add))
    return ("vector", last)




def colmax_to_scalar(p, B, maxcol, dst11, qdiv, src_mark=0):
    if src_mark:
        p.wait("tensor", "vector", src_mark)
    mm = p.do("tensor", lambda e: e.transpose(B["ps_d"][0:1, 0:128], maxcol,
                                              B["ident32"][:, :]))
    p.wait("vector", "tensor", mm)
    p.do("vector", lambda e: e.tensor_reduce(dst11, B["ps_d"][0:1, 0:128], AX.X, OP.max))
    return p.do("vector", lambda e: e.tensor_scalar(dst11, dst11, 1.0 / qdiv, 1e-12,
                                                    OP.mult, OP.max))


def emit_l3_ffn(p, B, W, coef, tag="q3"):
    I3 = 3072
    ln_tiles = emit_ln(p, B)
    xn = B["xn"]
    xnbf = xn.bitcast(BF16)
    xni = xn.bitcast(I32)
    xqT = B["xnT"][:, :]
    A = xn[:, 0:I3]
    Bf = xn[:, 3072:2 * 3072]
    Iv = xni[:, 2 * 3072:3 * 3072]
    Ivf = xn[:, 2 * 3072:3 * 3072]
    stage_bf = xnbf[:, 18432:18432 + I3]
    oqT = xnbf[:, 27648:27648 + I3]
    di = p.sb(tag + "_i", [128, D], I32)
    evb = p.sb(tag + "_e", [128, 1024], F32)
    d_i32 = di[:, :]
    d_f32 = B["sq"][:, :]
    ev0 = evb[:, 0:512]
    ev1 = evb[:, 512:1024]
    st3 = p.sb(tag + "_st", [128, 16], F32)
    row2 = p.sb(tag + "_r2", [1, 2], F32)
    bmx = p.sb(tag + "_bm", [128, 96], F32)
    ibm = p.sb(tag + "_ib", [128, 96], F32)
    wblk = p.sb(tag + "_wb", [128, 4 * D], F32)
    wT = p.sb(tag + "_wT", [128, KD * 512], BF16)
    uscr = p.dram(tag + "_u", [T, I3], F32)
    gscr = p.dram(tag + "_g", [T, I3], F32)
    oscr = p.dram(tag + "_o", [T, I3], F32)
    psAb = B["ps_a"].bitcast(BF16)
    psBb = B["ps_b"].bitcast(BF16)
    sq = B["sq"]

    gz = p.do("gpsimd", lambda e: e.memset(st3[:, 0:3], 0.0))
    mxu, mxg, mxp = st3[:, 0:1], st3[:, 1:2], st3[:, 2:3]

    mloc = st3[:, 3:4]
    p.wait("vector", "gpsimd", gz)
    for t in range(NT):
        p.do("vector", lambda e, t=t: e.tensor_reduce(
            st3[:, 4:5], xn[:, t * D:(t + 1) * D], AX.X, OP.max,
            apply_absolute_value=True))
        if t == 0:
            mv = p.do("vector", lambda e: e.tensor_copy(mloc, st3[:, 4:5]))
        else:
            mv = p.do("vector", lambda e: e.tensor_tensor(mloc, mloc, st3[:, 4:5], OP.max))
    p.wait("tensor", "vector", B["init_v"])
    mv = colmax_to_scalar(p, B, mloc, st3[0:1, 5:6], 32767.0, mv)
    arm, sx = emit_allreduce_row(p, B, st3[0:1, 5:6], 1, tag + "x", op=OP.max,
                                 src_ready=("vector", mv))
    p.waitg("vector", tag + "x", arm)
    p.waitg("tensor", tag + "x", arm)
    sxc, isxc = st3[:, 6:7], st3[:, 7:8]
    mm = bcast_row(p, B, sx[0:1, 0:1], 1, sxc, ("gpsimd", 0))
    p.wait("vector", "vector", mm)
    mv = p.do("vector", lambda e: e.reciprocal(isxc, sxc))
    tr_done = 0
    for t in range(NT):
        p.do("vector", lambda e, t=t: e.tensor_scalar(
            d_i32[:, :], xn[:, t * D:(t + 1) * D], isxc, None, OP.mult, OP.bypass))
        p.do("vector", lambda e: e.tensor_copy(d_f32[:, :], d_i32[:, :]))
        mv = p.do("vector", lambda e: e.tensor_scalar(
            B["xn_bf"][:, 0:D], d_f32[:, :], sxc, None, OP.mult, OP.bypass))
        p.wait("tensor", "vector", mv)
        for c in range(KD):
            g = t * KD + c
            slot = psAb if c % 2 == 0 else psBb
            if g >= 2:
                p.wait("tensor", "vector", tr_done - 1 if g >= 2 else 0)
            p.do("tensor", lambda e, c=c, slot=slot: e.transpose(
                slot[:, 0:128], B["xn_bf"][:, c * 128:(c + 1) * 128], B["identbf"][:, :]))
            mt = p.mark("tensor")
            p.wait("vector", "tensor", mt)
            tr_done = p.do("vector", lambda e, t=t, c=c, slot=slot: e.tensor_copy(
                xqT[:, c * T + t * 128:c * T + (t + 1) * 128], slot[:, 0:128]))
    evac_s = 0
    last_mv = 0
    dq_all = 0
    ldq = tag + "w"
    sdq = tag + "s"
    for mi, (wkey, scr, mxcol) in enumerate((("w_up", uscr, mxu), ("w_gate", gscr, mxg))):
        for blk in range(6):
            if dq_all:
                p.wait("gpsimd", "vector", dq_all)
            wm = p.dma("gpsimd", wblk[:, :].rearrange("p (r d) -> p r d", r=4),
                       W[wkey][blk * 512:(blk + 1) * 512, :]
                       .rearrange("(r p) d -> p r d", p=128), ldq)
            p.waitg("vector", ldq, wm)
            dq_v = 0
            for rt in range(4):
                wv = wblk[:, rt * D:(rt + 1) * D]
                w3 = wv.rearrange("p (b k) -> p b k", k=32)
                p.do("vector", lambda e, w3=w3: e.tensor_reduce(
                    bmx[:, 0:32], w3, AX.X, OP.max, apply_absolute_value=True))
                p.do("vector", lambda e: e.tensor_scalar(
                    bmx[:, 0:32], bmx[:, 0:32], 1.0 / 7.0, 1e-12, OP.mult, OP.max))
                p.do("vector", lambda e: e.reciprocal(ibm[:, 0:32], bmx[:, 0:32]))
                for bb in range(32):
                    p.do("vector", lambda e, wv=wv, bb=bb: e.tensor_scalar(
                        d_i32[:, bb * 32:(bb + 1) * 32], wv[:, bb * 32:(bb + 1) * 32],
                        ibm[:, bb:bb + 1], None, OP.mult, OP.bypass))
                p.do("vector", lambda e: e.tensor_copy(d_f32[:, :], d_i32[:, :]))
                for bb in range(32):
                    dq_v = p.do("vector", lambda e, bb=bb: e.tensor_scalar(
                        stage_bf[:, bb * 32:(bb + 1) * 32], d_f32[:, bb * 32:(bb + 1) * 32],
                        bmx[:, bb:bb + 1], None, OP.mult, OP.bypass))
                p.wait("tensor", "vector", dq_v)
                cpg = {}
                for c in range(KD):
                    slot = psAb if c % 2 == 0 else psBb
                    if c >= 2:
                        p.wait("tensor", "vector", cpg[c - 2])
                    p.do("tensor", lambda e, c=c, slot=slot: e.transpose(
                        slot[:, 0:128], stage_bf[:, c * 128:(c + 1) * 128],
                        B["identbf"][:, :]))
                    mt = p.mark("tensor")
                    p.wait("vector", "tensor", mt)
                    dq_v = cpg[c] = p.do("vector", lambda e, c=c, rt=rt, slot=slot: e.tensor_copy(
                        wT[:, c * 512 + rt * 128:c * 512 + (rt + 1) * 128],
                        slot[:, 0:128]))
            dq_all = dq_v
            p.wait("tensor", "vector", dq_v)
            for t in range(NT):
                half = (t % 2) * 512
                ph = B["ps_w"][:, half:half + 512]
                if evac_s:
                    p.wait("tensor", "scalar", evac_s)
                for c in range(KD):
                    p.do("tensor", lambda e, c=c, t=t, ph=ph: e.matmul(
                        ph, xqT[:, c * T + t * 128:c * T + (t + 1) * 128],
                        wT[:, c * 512:(c + 1) * 512], start=(c == 0), stop=(c == KD - 1)))
                mt = p.mark("tensor")
                p.wait("scalar", "tensor", mt)
                p.waitg("scalar", sdq, p.gsem(sdq)[1])
                if last_mv:
                    p.wait("scalar", "vector", last_mv)
                slot = ev0 if t % 2 == 0 else ev1
                evac_s = p.do("scalar", lambda e, ph=ph, slot=slot: e.activation(
                    slot, ph, AF.Identity))
                p.wait("vector", "scalar", evac_s)
                p.do("vector", lambda e, slot=slot: e.tensor_reduce(
                    st3[:, 4:5], slot, AX.X, OP.max, apply_absolute_value=True))
                last_mv = mv = p.do("vector", lambda e, mxcol=mxcol: e.tensor_tensor(
                    mxcol, mxcol, st3[:, 4:5], OP.max))
                p.wait("gpsimd", "scalar", evac_s)
                p.dma("gpsimd", scr[t * 128:(t + 1) * 128, blk * 512:(blk + 1) * 512],
                      slot, sdq)
    mv1 = colmax_to_scalar(p, B, mxu, row2[0:1, 0:1], 32767.0, last_mv)
    mv2 = colmax_to_scalar(p, B, mxg, row2[0:1, 1:2], 32767.0, mv1)
    arm, sug = emit_allreduce_row(p, B, row2[0:1, :], 2, tag + "ug", op=OP.max,
                                  src_ready=("vector", mv2))
    p.waitg("vector", tag + "ug", arm)
    p.waitg("tensor", tag + "ug", arm)
    su_c, sg_c, isu_c, isg_c = st3[:, 8:9], st3[:, 9:10], st3[:, 10:11], st3[:, 11:12]
    mm = bcast_row(p, B, sug[0:1, 0:1], 1, su_c, ("gpsimd", 0))
    mm = bcast_row(p, B, sug[0:1, 1:2], 1, sg_c, ("vector", mm))
    p.wait("vector", "vector", mm)
    p.do("vector", lambda e: e.reciprocal(isu_c, su_c))
    mv = p.do("vector", lambda e: e.reciprocal(isg_c, sg_c))

    def qdq_inplace(dstA, inv_c, s_c):
        Wd = dstA.shape[-1]
        p.do("vector", lambda e: e.tensor_scalar(Iv[:, 0:Wd], dstA, inv_c, None,
                                                 OP.mult, OP.bypass))
        p.do("vector", lambda e: e.tensor_copy(Ivf[:, 0:Wd], Iv[:, 0:Wd]))
        return p.do("vector", lambda e: e.tensor_scalar(dstA, Ivf[:, 0:Wd], s_c, None,
                                                        OP.mult, OP.bypass))

    lda = tag + "la"
    sta = tag + "sa"
    st_m = 0
    prev_v = 0
    for t in range(NT):
        p.waitg("gpsimd", sdq, p.gsem(sdq)[1])
        if st_m:
            p.waitg("gpsimd", sta, st_m)
        if prev_v:
            p.wait("gpsimd", "vector", prev_v)
        lm = p.dma("gpsimd", A, gscr[t * 128:(t + 1) * 128, :], lda)
        p.waitg("vector", lda, lm)
        qdq_inplace(A, isg_c, sg_c)
        mv = p.mark("vector")
        p.wait("scalar", "vector", mv)
        ms = p.do("scalar", lambda e: e.activation(Bf, A, AF.Sigmoid))
        p.wait("vector", "scalar", ms)
        p.do("vector", lambda e: e.tensor_scalar(Iv[:, 0:I3], Bf, 65536.0, None,
                                                 OP.mult, OP.bypass))
        p.do("vector", lambda e: e.tensor_copy(Ivf[:, 0:I3], Iv[:, 0:I3]))
        p.do("vector", lambda e: e.tensor_scalar(Bf, Ivf[:, 0:I3], 65535.0, 1.0 / 65536.0,
                                                 OP.min, OP.mult))
        p.do("vector", lambda e: e.tensor_tensor(A, A, Bf, OP.mult))
        p.do("vector", lambda e: e.tensor_reduce(st3[:, 4:5], A, AX.X, OP.max,
                                                 apply_absolute_value=True))
        mv = p.do("vector", lambda e: e.tensor_tensor(mxp, mxp, st3[:, 4:5], OP.max))
        prev_v = mv
        p.wait("gpsimd", "vector", mv)
        st_m = p.dma("gpsimd", gscr[t * 128:(t + 1) * 128, :], A, sta)
    mv = colmax_to_scalar(p, B, mxp, row2[0:1, 0:1], 32767.0, mv)
    arm, sp = emit_allreduce_row(p, B, row2[0:1, 0:1], 1, tag + "p", op=OP.max,
                                 src_ready=("vector", mv))
    p.waitg("vector", tag + "p", arm)
    p.waitg("tensor", tag + "p", arm)
    sp_c, isp_c = st3[:, 12:13], st3[:, 13:14]
    mm = bcast_row(p, B, sp[0:1, 0:1], 1, sp_c, ("gpsimd", 0))
    p.wait("vector", "vector", mm)
    mv = p.do("vector", lambda e: e.reciprocal(isp_c, sp_c))
    p.do("vector", lambda e: e.tensor_scalar(mxp, mxp, 0.0, None, OP.mult, OP.bypass))
    st_m2 = 0
    prev_v = 0
    for t in range(NT):
        p.waitg("gpsimd", sta, p.gsem(sta)[1])
        if st_m2:
            p.waitg("gpsimd", tag + "sb", st_m2)
        if prev_v:
            p.wait("gpsimd", "vector", prev_v)
        lm = p.dma("gpsimd", A, gscr[t * 128:(t + 1) * 128, :], lda)
        lm = p.dma("gpsimd", Bf, uscr[t * 128:(t + 1) * 128, :], lda)
        p.waitg("vector", lda, lm)
        qdq_inplace(A, isp_c, sp_c)
        qdq_inplace(Bf, isu_c, su_c)
        p.do("vector", lambda e: e.tensor_tensor(A, A, Bf, OP.mult))
        p.do("vector", lambda e: e.tensor_reduce(st3[:, 4:5], A, AX.X, OP.max,
                                                 apply_absolute_value=True))
        mv = p.do("vector", lambda e: e.tensor_tensor(mxp, mxp, st3[:, 4:5], OP.max))
        prev_v = mv
        p.wait("gpsimd", "vector", mv)
        st_m2 = p.dma("gpsimd", oscr[t * 128:(t + 1) * 128, :], A, tag + "sb")
    mv = colmax_to_scalar(p, B, mxp, row2[0:1, 0:1], 32767.0, mv)
    arm, so = emit_allreduce_row(p, B, row2[0:1, 0:1], 1, tag + "o", op=OP.max,
                                 src_ready=("vector", mv))
    p.waitg("vector", tag + "o", arm)
    p.waitg("tensor", tag + "o", arm)
    so_c, iso_c = st3[:, 14:15], st3[:, 15:16]
    mm = bcast_row(p, B, so[0:1, 0:1], 1, so_c, ("gpsimd", 0))
    p.wait("vector", "vector", mm)
    mv = p.do("vector", lambda e: e.reciprocal(iso_c, so_c))

    res_add = 0
    evac_s = 0
    wdT = B["xnT"]
    for half in range(2):
        dq_v = 0
        for rt in range(4):
            hr = half * 512 + rt * 128
            if dq_all or dq_v:
                p.wait("gpsimd", "vector", max(dq_all, dq_v))
            wm = p.dma("gpsimd", wblk[:, 0:I3], W["w_down"][hr:hr + 128, :], ldq)
            p.waitg("vector", ldq, wm)
            w3 = wblk[:, 0:I3].rearrange("p (b k) -> p b k", k=32)
            p.do("vector", lambda e, w3=w3: e.tensor_reduce(
                bmx[:, :], w3, AX.X, OP.max, apply_absolute_value=True))
            p.do("vector", lambda e: e.tensor_scalar(
                bmx[:, :], bmx[:, :], 1.0 / 7.0, 1e-12, OP.mult, OP.max))
            p.do("vector", lambda e: e.reciprocal(ibm[:, :], bmx[:, :]))
            for bb in range(96):
                p.do("vector", lambda e, bb=bb: e.tensor_scalar(
                    Iv[:, bb * 32:(bb + 1) * 32], wblk[:, bb * 32:(bb + 1) * 32],
                    ibm[:, bb:bb + 1], None, OP.mult, OP.bypass))
            p.do("vector", lambda e: e.tensor_copy(Ivf[:, 0:I3], Iv[:, 0:I3]))
            for bb in range(96):
                dq_v = p.do("vector", lambda e, bb=bb: e.tensor_scalar(
                    stage_bf[:, bb * 32:(bb + 1) * 32], Ivf[:, bb * 32:(bb + 1) * 32],
                    bmx[:, bb:bb + 1], None, OP.mult, OP.bypass))
            p.wait("tensor", "vector", dq_v)
            cpg = {}
            for c in range(24):
                slot = psAb if c % 2 == 0 else psBb
                if c >= 2:
                    p.wait("tensor", "vector", cpg[c - 2])
                p.do("tensor", lambda e, c=c, slot=slot: e.transpose(
                    slot[:, 0:128], stage_bf[:, c * 128:(c + 1) * 128],
                    B["identbf"][:, :]))
                mt = p.mark("tensor")
                p.wait("vector", "tensor", mt)
                dq_v = cpg[c] = p.do("vector", lambda e, c=c, rt=rt, slot=slot: e.tensor_copy(
                    wdT[:, c * 512 + rt * 128:c * 512 + (rt + 1) * 128],
                    slot[:, 0:128]))
        p.wait("tensor", "vector", dq_v)
        for t in range(NT):
            p.waitg("gpsimd", tag + "sb", p.gsem(tag + "sb")[1])
            if prev_v:
                p.wait("gpsimd", "vector", prev_v)
            lm = p.dma("gpsimd", A, oscr[t * 128:(t + 1) * 128, :], lda)
            p.waitg("vector", lda, lm)
            p.do("vector", lambda e: e.tensor_scalar(Iv[:, 0:I3], A, iso_c, None,
                                                     OP.mult, OP.bypass))
            p.do("vector", lambda e: e.tensor_copy(Ivf[:, 0:I3], Iv[:, 0:I3]))
            mv = p.do("vector", lambda e: e.tensor_scalar(stage_bf[:, 0:I3], Ivf[:, 0:I3],
                                                          so_c, None, OP.mult, OP.bypass))
            prev_v = mv
            p.wait("tensor", "vector", mv)
            cpg = {}
            for c in range(24):
                slot = psAb if c % 2 == 0 else psBb
                if c >= 2:
                    p.wait("tensor", "vector", cpg[c - 2])
                p.do("tensor", lambda e, c=c, slot=slot: e.transpose(
                    slot[:, 0:128], stage_bf[:, c * 128:(c + 1) * 128],
                    B["identbf"][:, :]))
                mt = p.mark("tensor")
                p.wait("vector", "tensor", mt)
                mv = cpg[c] = p.do("vector", lambda e, c=c, slot=slot: e.tensor_copy(
                    oqT[:, c * 128:(c + 1) * 128], slot[:, 0:128]))
            p.wait("tensor", "vector", mv)
            ph = B["ps_w"][:, (t % 2) * 512:(t % 2) * 512 + 512]
            if evac_s:
                p.wait("tensor", "scalar", evac_s)
            for c in range(24):
                p.do("tensor", lambda e, c=c, ph=ph: e.matmul(
                    ph, oqT[:, c * 128:(c + 1) * 128], wdT[:, c * 512:(c + 1) * 512],
                    start=(c == 0), stop=(c == 23)))
            mt = p.mark("tensor")
            p.wait("scalar", "tensor", mt)
            if res_add:
                p.wait("scalar", "vector", res_add)
            evac_s = p.do("scalar", lambda e, ph=ph: e.activation(ev0, ph, AF.Identity,
                                                                  scale=coef))
            p.wait("vector", "scalar", evac_s)
            res_add = p.do("vector", lambda e, t=t, half=half: e.tensor_tensor(
                B["x_res"][:, t * D + half * 512:t * D + half * 512 + 512],
                B["x_res"][:, t * D + half * 512:t * D + half * 512 + 512],
                ev0, OP.add))
    return ("vector", res_add)




def emit_l0_ffn(p, B, W, coef):
    E = 16
    CAP = 2560.0
    ln_tiles = emit_ln(p, B)
    idt = p.sb("l0_idt", [128, D], F32)
    for t in range(NT):
        p.do("vector", lambda e, t=t: e.tensor_scalar(
            idt[:, :], B["xn"][:, t * D:(t + 1) * D], coef, None,
            OP.mult, OP.bypass))
        p.do("vector", lambda e, t=t: e.tensor_tensor(
            B["x_res"][:, t * D:(t + 1) * D], B["x_res"][:, t * D:(t + 1) * D],
            idt[:, :], OP.add))
    gwT = p.sb("l0_gwT", [128, KD * E], F32)
    gb = p.sb("l0_gb", [1, E], F32)
    p.dma("sync", gwT[:, :].rearrange("p (c e) -> p c e", c=KD),
          W["router_w"].rearrange("(c p) e -> p c e", p=128), "l0w")
    dm = p.dma("sync", gb[0:1, :], W["router_b"][None, :], "l0w")
    p.waitg("tensor", "l0w", dm)
    gates = p.sb("l0_gates", [128, NT * E], F32)
    ev = emit_router(p, B, gwT, gb[0:1, :], E, gates, ln_tiles, "l0")

    oh1 = p.sb("l0_oh1", [128, NT * E], F32)
    oh2 = p.sb("l0_oh2", [128, NT * E], F32)
    ohs = p.sb("l0_ohs", [128, NT * E], F32)
    comb = p.sb("l0_comb", [128, NT * E], F32)
    vals = p.sb("l0_vals", [128, 8 * NT], F32)
    tmp = p.sb("l0_tmp", [128, 2 * E], F32)
    st = B["stat"]
    m1 = st[:, 9:10]
    zin = st[:, 10:11]
    prev = 0
    for t in range(NT):
        gt = gates[:, t * E:(t + 1) * E]
        o1 = oh1[:, t * E:(t + 1) * E]
        o2 = oh2[:, t * E:(t + 1) * E]
        p.do("vector", lambda e, gt=gt: e.tensor_reduce(m1, gt, AX.X, OP.max))
        p.do("vector", lambda e, gt=gt, o1=o1: e.tensor_scalar(
            o1, gt, m1, None, OP.is_ge, OP.bypass))
        p.do("vector", lambda e, o1=o1: e.tensor_scalar(
            tmp[:, 0:E], o1, -1e30, None, OP.mult, OP.bypass))
        p.do("vector", lambda e, gt=gt: e.tensor_tensor(
            tmp[:, 0:E], tmp[:, 0:E], gt, OP.add))
        p.do("vector", lambda e: e.tensor_reduce(st[:, 11:12], tmp[:, 0:E], AX.X, OP.max))
        p.do("vector", lambda e, o2=o2: e.tensor_scalar(
            o2, tmp[:, 0:E], st[:, 11:12], None, OP.is_ge, OP.bypass))
        p.do("vector", lambda e, o1=o1, o2=o2, t=t: e.tensor_tensor(
            ohs[:, t * E:(t + 1) * E], o1, o2, OP.add))
        mq = p.do("vector", lambda e: e.tensor_scalar(m1, m1, -1.0, None, OP.mult, OP.bypass))
        p.wait("scalar", "vector", mq)
        ms = p.do("scalar", lambda e, gt=gt: e.activation(tmp[:, 0:E], gt, AF.Exp,
                                                          bias=m1, accum_out=zin))
        p.wait("vector", "scalar", ms)
        p.do("vector", lambda e: e.reciprocal(st[:, 11:12], zin))
        p.do("vector", lambda e: e.tensor_scalar(tmp[:, 0:E], tmp[:, 0:E],
                                                 st[:, 11:12], None, OP.mult, OP.bypass))
        p.do("vector", lambda e, o1=o1: e.tensor_tensor(tmp[:, E:2 * E], tmp[:, 0:E],
                                                        o1, OP.mult))
        p.do("vector", lambda e, t=t: e.tensor_reduce(vals[:, t * 8:t * 8 + 1],
                                                      tmp[:, E:2 * E], AX.X, OP.add))
        p.do("vector", lambda e, o2=o2: e.tensor_tensor(tmp[:, E:2 * E], tmp[:, 0:E],
                                                        o2, OP.mult))
        prev = p.do("vector", lambda e, t=t: e.tensor_reduce(vals[:, t * 8 + 1:t * 8 + 2],
                                                             tmp[:, E:2 * E], AX.X, OP.add))

    ps_cnt = B["ps_d"][0:1, 0:E]
    p.wait("tensor", "vector", prev)
    mm = 0
    for t in range(NT):
        mm = p.do("tensor", lambda e, t=t: e.matmul(
            ps_cnt, B["ones_col"][:, 0:1], ohs[:, t * E:(t + 1) * E],
            start=(t == 0), stop=(t == NT - 1)))
    cnt_row = p.sb("l0_cnt", [1, E], F32)
    p.wait("vector", "tensor", mm)
    cm = p.do("vector", lambda e: e.tensor_copy(cnt_row[0:1, :], ps_cnt))
    agin = p.dram("l0ag_in", [1, E], F32)
    agout = p.dram("l0ag_out", [NCORE, E], F32)
    p.wait("gpsimd", "vector", cm)
    am = p.dma("gpsimd", agin[:, :], cnt_row[0:1, :], "l0ag")
    p.waitg("gpsimd", "l0ag", am)
    ccm = p.cc("AllGather", OP.bypass, [agin.ap()], [agout.ap()], grp="l0agc")
    p.waitg("gpsimd", "l0agc", ccm)
    agt = p.sb("l0_agt", [NCORE, E], F32)
    am = p.dma("gpsimd", agt[:, :], agout[:, :], "l0ag")
    iotac = p.sb("l0_iota", [NCORE, 1], I32)
    iotaf = p.sb("l0_iotaf", [NCORE, 1], F32)
    pidf = p.sb("l0_pid", [1, 1], I32)
    pidf32 = p.sb("l0_pidf32", [1, 1], F32)
    cmsk = p.sb("l0_cm", [NCORE, 1], F32)
    offrow = p.sb("l0_off", [1, E], F32)
    gp = p.do("gpsimd", lambda e: e.iota(iotac[:, :], pattern=[[0, 1]], base=0,
                                         channel_multiplier=1))

    def save_pid(e):
        pid = p.nc.partition_id([mybir.EngineType.Pool])
        r = e.to_reg(pid)
        return e.reg_save(pidf[0:1, 0:1], r)
    gp2 = p.do("gpsimd", save_pid)
    p.wait("vector", "gpsimd", gp2)
    p.do("vector", lambda e: e.tensor_copy(iotaf[:, :], iotac[:, :]))
    mv = p.do("vector", lambda e: e.tensor_copy(pidf32[0:1, :], pidf[0:1, :]))
    p.wait("tensor", "vector", mv)
    mmb = p.do("tensor", lambda e: e.matmul(B["ps_c"][0:NCORE, 0:1],
                                            B["ones_row"][0:1, 0:NCORE],
                                            pidf32[0:1, 0:1], start=True, stop=True))
    p.wait("vector", "tensor", mmb)
    p.do("vector", lambda e: e.tensor_copy(cmsk[:, :], B["ps_c"][0:NCORE, 0:1]))
    mv = p.do("vector", lambda e: e.tensor_tensor(cmsk[:, :], iotaf[:, :], cmsk[:, :],
                                                  OP.is_lt))
    p.wait("tensor", "vector", mv)
    p.waitg("tensor", "l0ag", am)
    mmb = p.do("tensor", lambda e: e.matmul(B["ps_c"][0:1, 0:E], cmsk[0:NCORE, 0:1],
                                            agt[0:NCORE, 0:E], start=True, stop=True))
    p.wait("vector", "tensor", mmb)
    roff_m = p.do("vector", lambda e: e.tensor_copy(offrow[0:1, :], B["ps_c"][0:1, 0:E]))

    sut = p.sb("l0_sut", [128, 128], F32)
    p.do("gpsimd", lambda e: e.iota(B["sq"][:, 0:128], pattern=[[1, 128]], base=0,
                                    channel_multiplier=0,
                                    allow_small_or_imprecise_dtypes=True))
    gp3 = p.do("gpsimd", lambda e: e.iota(sut[:, :], pattern=[[0, 128]], base=0,
                                          channel_multiplier=1,
                                          allow_small_or_imprecise_dtypes=True))
    p.wait("vector", "gpsimd", gp3)
    sut_m = p.do("vector", lambda e: e.tensor_tensor(sut[:, :], B["sq"][:, 0:128],
                                                     sut[:, :], OP.is_gt))
    Pg = p.sb("l0_Pg", [128, NT * E], F32)
    p.wait("tensor", "vector", sut_m)
    for t in range(NT):
        p.wait("tensor", "vector", roff_m)
        p.do("tensor", lambda e, t=t: e.matmul(
            B["ps_c"][:, 0:E], sut[:, :], ohs[:, t * E:(t + 1) * E],
            start=True, stop=False))
        mm = p.do("tensor", lambda e: e.matmul(
            B["ps_c"][:, 0:E], B["ones_row"][0:1, 0:128], offrow[0:1, :],
            start=False, stop=True))
        p.wait("vector", "tensor", mm)
        mv = p.do("vector", lambda e, t=t: e.tensor_copy(Pg[:, t * E:(t + 1) * E],
                                                         B["ps_c"][:, 0:E]))
        mm = p.do("tensor", lambda e, t=t: e.matmul(
            B["ps_d"][0:1, 0:E], B["ones_col"][:, 0:1], ohs[:, t * E:(t + 1) * E],
            start=True, stop=True))
        p.wait("vector", "tensor", mm)
        roff_m = p.do("vector", lambda e: e.tensor_tensor(offrow[0:1, :], offrow[0:1, :],
                                                          B["ps_d"][0:1, 0:E], OP.add))

    prev = 0
    for t in range(NT):
        o1 = oh1[:, t * E:(t + 1) * E]
        o2 = oh2[:, t * E:(t + 1) * E]
        Pt = Pg[:, t * E:(t + 1) * E]
        p.do("vector", lambda e, o1=o1, Pt=Pt: e.tensor_tensor(tmp[:, 0:E], o1, Pt, OP.mult))
        p.do("vector", lambda e, t=t: e.tensor_reduce(vals[:, t * 8 + 2:t * 8 + 3],
                                                      tmp[:, 0:E], AX.X, OP.add))
        p.do("vector", lambda e, o2=o2, Pt=Pt: e.tensor_tensor(tmp[:, 0:E], o2, Pt, OP.mult))
        p.do("vector", lambda e, t=t: e.tensor_reduce(vals[:, t * 8 + 3:t * 8 + 4],
                                                      tmp[:, 0:E], AX.X, OP.add))
        p.do("vector", lambda e, t=t: e.tensor_scalar(
            vals[:, t * 8 + 4:t * 8 + 5], vals[:, t * 8 + 2:t * 8 + 3], CAP, None,
            OP.is_lt, OP.bypass))
        p.do("vector", lambda e, t=t: e.tensor_scalar(
            vals[:, t * 8 + 5:t * 8 + 6], vals[:, t * 8 + 3:t * 8 + 4], CAP, None,
            OP.is_lt, OP.bypass))
        p.do("vector", lambda e, t=t: e.tensor_tensor(
            st[:, 11:12], vals[:, t * 8:t * 8 + 1], vals[:, t * 8 + 1:t * 8 + 2], OP.add))
        p.do("vector", lambda e: e.tensor_scalar(st[:, 11:12], st[:, 11:12], 1.0, 1e-9,
                                                 OP.mult, OP.add))
        p.do("vector", lambda e: e.reciprocal(st[:, 12:13], st[:, 11:12]))
        p.do("vector", lambda e, t=t: e.tensor_scalar(
            st[:, 13:14], vals[:, t * 8:t * 8 + 1], st[:, 12:13], None, OP.mult, OP.bypass))
        p.do("vector", lambda e, t=t: e.tensor_tensor(
            st[:, 13:14], st[:, 13:14], vals[:, t * 8 + 4:t * 8 + 5], OP.mult))
        p.do("vector", lambda e: e.tensor_scalar(
            st[:, 13:14], st[:, 13:14], coef, None, OP.mult, OP.bypass))
        p.do("vector", lambda e, t=t: e.tensor_scalar(
            st[:, 14:15], vals[:, t * 8 + 1:t * 8 + 2], st[:, 12:13], None, OP.mult, OP.bypass))
        p.do("vector", lambda e, t=t: e.tensor_tensor(
            st[:, 14:15], st[:, 14:15], vals[:, t * 8 + 5:t * 8 + 6], OP.mult))
        p.do("vector", lambda e: e.tensor_scalar(
            st[:, 14:15], st[:, 14:15], coef, None, OP.mult, OP.bypass))
        p.do("vector", lambda e, o1=o1: e.tensor_scalar(
            tmp[:, 0:E], o1, st[:, 13:14], None, OP.mult, OP.bypass))
        p.do("vector", lambda e, o2=o2: e.tensor_scalar(
            tmp[:, E:2 * E], o2, st[:, 14:15], None, OP.mult, OP.bypass))
        prev = p.do("vector", lambda e, t=t: e.tensor_tensor(
            comb[:, t * E:(t + 1) * E], tmp[:, 0:E], tmp[:, E:2 * E], OP.add))
    comb_done = prev

    xnbf = B["xn"].bitcast(BF16)
    wgA = xnbf[:, 0:6144]
    wuA = xnbf[:, 6144:12288]
    hidT = xnbf[:, 12288:24576]
    dnA = xnbf[:, 24576:30720]
    sq = B["sq"]
    p.wait("gpsimd", "vector", ev)
    subblocks = []
    for e_i in range(E):
        subblocks.append(("e", e_i, 0, 6))
        subblocks.append(("e", e_i, 6, 5))
    subblocks.append(("s", 0, 0, 6))
    subblocks.append(("s", 0, 6, 5))
    evac = {}
    hid_marks = {}
    g_iter = 0
    sq_consumed = 0
    last_dn = 0
    first = True
    gidx = 0
    for kind, e_i, f0, nf in subblocks:
        FW = nf * 128
        if kind == "e":
            gsrc = W["w_gate"][e_i, :, f0 * 128:f0 * 128 + FW]
            usrc = W["w_up"][e_i, :, f0 * 128:f0 * 128 + FW]
            dsrc = W["w_down"][e_i, f0 * 128:f0 * 128 + FW, :]
        else:
            gsrc = W["sw_gate"][:, f0 * 128:f0 * 128 + FW]
            usrc = W["sw_up"][:, f0 * 128:f0 * 128 + FW]
            dsrc = W["sw_down"][f0 * 128:f0 * 128 + FW, :]
        if not first:
            p.wait("gpsimd", "tensor", last_dn)
        p.dma("gpsimd", wgA[:, 0:KD * FW].rearrange("p (c f) -> p c f", c=KD),
              gsrc.rearrange("(c p) f -> p c f", p=128), "l0ew")
        p.dma("gpsimd", wuA[:, 0:KD * FW].rearrange("p (c f) -> p c f", c=KD),
              usrc.rearrange("(c p) f -> p c f", p=128), "l0ew")
        wm = p.dma("gpsimd", dnA[:, 0:nf * D].rearrange("p (c d) -> p c d", c=nf),
                   dsrc.rearrange("(c p) d -> p c d", p=128), "l0ew")
        p.waitg("tensor", "l0ew", wm)
        if first:
            p.wait("tensor", "vector", comb_done)
            first = False
        else:
            p.wait("tensor", "scalar", evac[gidx * NT - 1])
        for ht in range(nf):
            for nb in range(4):
                slot = (g_iter % 2) * 1024
                psG = B["ps_w"][:, slot:slot + 512]
                psU = B["ps_w"][:, slot + 512:slot + 1024]
                if g_iter >= 2:
                    p.wait("tensor", "vector", hid_marks[g_iter - 2])
                for c in range(KD):
                    p.do("tensor", lambda e, c=c, ht=ht, nb=nb, psG=psG, FW=FW: e.matmul(
                        psG, wgA[:, c * FW + ht * 128: c * FW + (ht + 1) * 128],
                        B["xnT"][:, c * T + nb * 512: c * T + (nb + 1) * 512],
                        start=(c == 0), stop=(c == KD - 1)))
                mmx = 0
                for c in range(KD):
                    mmx = p.do("tensor", lambda e, c=c, ht=ht, nb=nb, psU=psU, FW=FW: e.matmul(
                        psU, wuA[:, c * FW + ht * 128: c * FW + (ht + 1) * 128],
                        B["xnT"][:, c * T + nb * 512: c * T + (nb + 1) * 512],
                        start=(c == 0), stop=(c == KD - 1)))
                p.wait("scalar", "tensor", mmx)
                if g_iter >= 1:
                    p.wait("scalar", "vector", max(hid_marks[g_iter - 1], sq_consumed))
                msx = p.do("scalar", lambda e, psG=psG: e.activation(
                    sq[:, 0:512], psG, AF.Silu))
                p.wait("vector", "scalar", msx)
                hid_marks[g_iter] = p.do("vector", lambda e, ht=ht, nb=nb, psU=psU:
                                         e.tensor_tensor(
                    hidT[:, ht * T + nb * 512: ht * T + (nb + 1) * 512],
                    sq[:, 0:512], psU, OP.mult))
                g_iter += 1
        p.wait("tensor", "vector", hid_marks[g_iter - 1])
        for t in range(NT):
            half = (t % 2) * 1024
            ph0 = B["ps_w"][:, half:half + 512]
            ph1 = B["ps_w"][:, half + 512:half + 1024]
            if (gidx * NT + t - 2) in evac:
                p.wait("tensor", "scalar", evac[gidx * NT + t - 2])
            for c in range(nf):
                lhs = hidT[:, c * T + t * 128: c * T + (t + 1) * 128]
                p.do("tensor", lambda e, lhs=lhs, c=c, ph0=ph0: e.matmul(
                    ph0, lhs, dnA[:, c * D: c * D + 512], start=(c == 0), stop=False))
                p.do("tensor", lambda e, lhs=lhs, c=c, ph1=ph1, nf=nf: e.matmul(
                    ph1, lhs, dnA[:, c * D + 512: (c + 1) * D],
                    start=(c == 0), stop=(c == nf - 1)))
            mmx = p.do("tensor", lambda e, ph0=ph0: e.matmul(
                ph0, B["ones_row"][0:1, 0:128], B["zero_row"][0:1, 0:512],
                start=False, stop=True))
            last_dn = mmx
            p.wait("scalar", "tensor", mmx)
            p.wait("scalar", "vector", sq_consumed)
            if kind == "e":
                sc = comb[:, t * E + e_i: t * E + e_i + 1]
                p.do("scalar", lambda e, ph0=ph0, sc=sc: e.activation(
                    sq[:, 0:512], ph0, AF.Identity, scale=sc))
                msx = p.do("scalar", lambda e, ph1=ph1, sc=sc: e.activation(
                    sq[:, 512:1024], ph1, AF.Identity, scale=sc))
            else:
                p.do("scalar", lambda e, ph0=ph0: e.activation(
                    sq[:, 0:512], ph0, AF.Identity, scale=float(coef)))
                msx = p.do("scalar", lambda e, ph1=ph1: e.activation(
                    sq[:, 512:1024], ph1, AF.Identity, scale=float(coef)))
            evac[gidx * NT + t] = msx
            p.wait("vector", "scalar", msx)
            sq_consumed = p.do("vector", lambda e, t=t: e.tensor_tensor(
                B["x_res"][:, t * D:(t + 1) * D], B["x_res"][:, t * D:(t + 1) * D],
                sq[:, 0:D], OP.add))
        gidx += 1
    return ("vector", sq_consumed)




def _bar(p):
    mv, ms, mt = p.mark("vector"), p.mark("scalar"), p.mark("tensor")
    p.wait("tensor", "vector", mv)
    p.wait("tensor", "scalar", ms)
    p.wait("vector", "tensor", mt)
    p.wait("scalar", "tensor", mt)
    p.wait("vector", "scalar", ms)
    p.wait("scalar", "vector", mv)


def emit_l3_attn(p, B, W, coef, x_in, x_out):
    assert coef == 1.0
    NH, HD, DS, DI = 32, 64, 64, 2048
    NXT = 18
    CW = NXT * 128
    SCW = 256
    NSC = T // SCW
    stat, sq = B["stat"], B["sq"]
    psA = B["ps_a"].bitcast(BF16)
    psB = B["ps_b"].bitcast(BF16)

    xnT = p.sb("a3_xnT", [128, KD * T], BF16)
    Fb = p.sb("a3_F", [128, 16 * 1024], BF16)
    WZ = p.sb("a3_WZ", [128, KD * DI], BF16)
    WX = p.sb("a3_WX", [128, KD * CW], BF16)
    Wdt = p.sb("a3_Wdt", [128, KD * 32], BF16)
    xbct = p.sb("a3_xbct", [128, NXT * SCW], BF16)
    ybuf = p.sb("a3_y", [128, DI], F32)
    Xtok = p.sb("a3_Xtok", [128, DI], BF16)
    xw = p.sb("a3_xw", [128, DI], BF16)
    xtile = p.sb("a3_xtile", [128, D], F32)
    hT = p.sb("a3_hT", [64, DI], F32)
    xnb = p.sb("a3_xnb", [128, D], BF16)
    obuf = xnb
    catp = sq[:, 512:512 + SCW + 3]
    tails = [p.sb("a3_tailA", [128, NXT * 3], F32), p.sb("a3_tailB", [128, NXT * 3], F32)]
    UT1 = p.sb("a3_UT1", [128, 128], F32)
    LTS = p.sb("a3_LTS", [128, 128], F32)
    Rbf = p.sb("a3_Rbf", [128, 128], BF16)
    Ssb = p.sb("a3_S", [128, 128], F32)
    e1 = p.sb("a3_e1", [128, 128], F32)
    Mbf = p.sb("a3_Mbf", [128, 128], BF16)
    Btok = p.sb("a3_Btok", [128, 128], BF16)
    CTf = p.sb("a3_CTf", [64, 128], F32)
    cumA = p.sb("a3_cumA", [128, 32], F32)
    cumAT = p.sb("a3_cumAT", [32, 128], F32)
    ones32 = p.sb("a3_ones32", [32, 128], F32)
    amat = p.sb("a3_amat", [128, 32], F32)
    wmat = p.sb("a3_wmat", [128, 32], F32)
    gam = p.sb("a3_gam", [128, 32], F32)
    w32 = p.sb("a3_w32", [128, 128], F32)
    dtc = p.sb("a3_dtc", [128, 64], F32)
    glc = p.sb("a3_glc", [32, 1], F32)
    gLb = p.sb("a3_gLb", [64, 32], F32)
    Lh2 = [p.sb("a3_Lh0", [32, 128], F32), p.sb("a3_Lh1", [32, 128], F32)]
    e1b = p.sb("a3_e1b", [128, 128], F32)
    Mbf2 = p.sb("a3_Mbf2", [128, 128], BF16)
    Abc = p.sb("a3_Abc", [128, 32], F32)
    arow = p.sb("a3_arow", [1, 32], F32)
    cwr = p.sb("a3_cwr", [128, NXT * 4], F32)
    cbr = p.sb("a3_cbr", [128, NXT], F32)
    r32a = p.sb("a3_r32a", [1, 32], F32)
    rD = p.sb("a3_rD", [1, DI], BF16)
    rnw = p.sb("a3_rnw", [1, DI], BF16)
    rbo = p.sb("a3_rbo", [1, D], BF16)
    onesbf = p.sb("a3_onesbf", [1, 128], BF16)
    tA = p.sb("a3_tA", [128, 128], BF16)
    tB = p.sb("a3_tB", [128, 128], BF16)
    psl = [B["ps_a"], B["ps_b"], B["ps_c"], B["ps_d"]]

    p.wait("gpsimd", "vector", B["init_v"])
    p.do("gpsimd", lambda e: e.memset(ones32[:, :], 1.0))
    p.do("gpsimd", lambda e: e.memset(tails[0][:, :], 0.0))
    p.do("gpsimd", lambda e: e.memset(tails[1][:, :], 0.0))
    p.do("gpsimd", lambda e: e.iota(sq[:, 0:128], pattern=[[1, 128]], base=0,
                                    channel_multiplier=0, allow_small_or_imprecise_dtypes=True))
    p.do("gpsimd", lambda e: e.iota(UT1[:, :], pattern=[[0, 128]], base=0,
                                    channel_multiplier=1, allow_small_or_imprecise_dtypes=True))
    p.do("gpsimd", lambda e: e.memset(onesbf[:, :], 1.0))
    gm = p.mark("gpsimd")
    p.wait("vector", "gpsimd", gm)
    p.do("vector", lambda e: e.tensor_tensor(LTS[:, :], sq[:, 0:128], UT1[:, :], OP.is_lt))
    p.do("vector", lambda e: e.tensor_tensor(w32[:, :], sq[:, 0:128], UT1[:, :], OP.add))
    p.do("vector", lambda e: e.tensor_scalar(w32[:, :], w32[:, :], 127.0, None,
                                             OP.is_equal, OP.bypass))
    p.do("vector", lambda e: e.tensor_copy(Rbf[:, :], w32[:, :]))
    p.do("vector", lambda e: e.tensor_tensor(UT1[:, :], sq[:, 0:128], UT1[:, :], OP.is_ge))
    _bar(p)

    for t in range(NT):
        dm = p.dma("sync", xtile[:, :], x_in.rearrange("(t p) d -> p t d", p=128)[:, t, :],
                   "l3x")
        p.waitg("vector", "l3x", dm)
        p.waitg("scalar", "l3x", dm)
        xt = xtile[:, :]
        p.do("vector", lambda e, xt=xt: e.tensor_reduce(stat[:, 0:1], xt, AX.X, OP.add))
        m = p.do("vector", lambda e: e.tensor_scalar(stat[:, 1:2], stat[:, 0:1],
                                                     -1.0 / D, None, OP.mult, OP.bypass))
        p.wait("scalar", "vector", m)
        m = p.do("scalar", lambda e, xt=xt: e.activation(sq[:, 0:D], xt, AF.Square,
                                                         bias=stat[:, 1:2], scale=1.0,
                                                         accum_out=stat[:, 2:3]))
        p.wait("vector", "scalar", m)
        m = p.do("vector", lambda e: e.tensor_scalar(stat[:, 3:4], stat[:, 2:3],
                                                     1.0 / D, LN_EPS, OP.mult, OP.add))
        p.wait("scalar", "vector", m)
        p.do("scalar", lambda e: e.activation(stat[:, 4:5], stat[:, 3:4], AF.Ln))
        m = p.do("scalar", lambda e: e.activation(stat[:, 5:6], stat[:, 4:5], AF.Exp, scale=-0.5))
        p.wait("vector", "scalar", m)
        m = p.do("vector", lambda e: e.tensor_tensor(stat[:, 6:7], stat[:, 1:2],
                                                     stat[:, 5:6], OP.mult))
        p.wait("scalar", "vector", m)
        m = p.do("scalar", lambda e, xt=xt: e.activation(
            xnb[:, :], xt, AF.Identity, bias=stat[:, 6:7], scale=stat[:, 5:6]))
        p.wait("vector", "scalar", m)
        p.wait("tensor", "scalar", m)
        cps = []
        for c in range(KD):
            slot = psA if c % 2 == 0 else psB
            if c >= 2:
                p.wait("tensor", "vector", cps[c - 2])
            p.do("tensor", lambda e, c=c, slot=slot: e.transpose(
                slot[:, 0:128], xnb[:, c * 128:(c + 1) * 128], B["identbf"][:, :]))
            p.wait("vector", "tensor", p.mark("tensor"))
            m = p.do("vector", lambda e, t=t, c=c, slot=slot: e.tensor_copy(
                xnT[:, c * T + t * 128: c * T + (t + 1) * 128], slot[:, 0:128]))
            cps.append(m)
        _bar(p)
        p.wait("sync", "vector", cps[-1])
        p.wait("sync", "scalar", p.mark("scalar"))

    store_marks = []
    for di in range(2):
        pf = "f" if di == 0 else "b"
        _bar(p)
        if di == 1:
            p.wait("gpsimd", "vector", p.mark("vector"))
            gm = p.do("gpsimd", lambda e: e.memset(hT[:, :], 0.0))
            p.do("gpsimd", lambda e: e.memset(tails[0][:, :], 0.0), guard=False)
            p.do("gpsimd", lambda e: e.memset(tails[1][:, :], 0.0), guard=False)
            p.wait("vector", "gpsimd", gm + 2)
            for c in range(KD):
                for b in range(NT // 2):
                    s1 = xnT[:, c * T + b * 128: c * T + (b + 1) * 128]
                    s2 = xnT[:, c * T + (15 - b) * 128: c * T + (16 - b) * 128]
                    p.do("tensor", lambda e, s1=s1: e.transpose(psA[:, 0:128], s1,
                                                                B["identbf"][:, :]))
                    p.do("tensor", lambda e, s2=s2: e.transpose(psB[:, 0:128], s2,
                                                                B["identbf"][:, :]))
                    p.wait("vector", "tensor", p.mark("tensor"))
                    p.do("vector", lambda e: e.tensor_copy(tA[:, :], psA[:, 0:128]))
                    m = p.do("vector", lambda e: e.tensor_copy(tB[:, :], psB[:, 0:128]))
                    p.wait("tensor", "vector", m)
                    p.do("tensor", lambda e: e.matmul(B["ps_c"][:, 0:128], tA[:, :],
                                                      Rbf[:, :], start=True, stop=True))
                    mm = p.do("tensor", lambda e: e.matmul(B["ps_d"][:, 0:128], tB[:, :],
                                                           Rbf[:, :], start=True, stop=True))
                    p.wait("vector", "tensor", mm)
                    p.do("vector", lambda e, s2=s2: e.tensor_copy(s2, B["ps_c"][:, 0:128]))
                    m = p.do("vector", lambda e, s1=s1: e.tensor_copy(s1, B["ps_d"][:, 0:128]))
                    p.wait("tensor", "vector", m)
            _bar(p)
        else:
            gm = p.do("gpsimd", lambda e: e.memset(hT[:, :], 0.0))
            p.wait("vector", "gpsimd", gm)

        p.wait("gpsimd", "tensor", p.mark("tensor"))
        p.wait("gpsimd", "vector", p.mark("vector"))
        p.wait("sync", "tensor", p.mark("tensor"))
        p.wait("sync", "vector", p.mark("vector"))
        p.wait("sync", "scalar", p.mark("scalar"))
        p.dma("gpsimd", WZ[:, :].rearrange("p (c n) -> p c n", c=KD),
              W["woutT" + pf].rearrange("(c p) n -> p c n", p=128), "l3fw")
        dm = p.dma("gpsimd", WX[:, 0:KD * 1024].rearrange("p (c n) -> p c n", c=KD),
                   W["wo" + ("A" if di == 0 else "B")].rearrange("(c p) n -> p c n", p=128),
                   "l3fw")
        p.waitg("tensor", "l3fw", dm)
        fcp = []
        for i in range(16):
            for hf in range(2):
                g = i * 2 + hf
                slot = B["ps_c"] if g % 2 == 0 else B["ps_d"]
                if g >= 2:
                    p.wait("tensor", "scalar", fcp[g - 2])
                mm = 0
                for c in range(KD):
                    mm = p.do("tensor", lambda e, c=c, i=i, hf=hf, slot=slot: e.matmul(
                        slot[:, 0:512], WZ[:, c * DI + i * 128: c * DI + (i + 1) * 128],
                        WX[:, c * 1024 + hf * 512: c * 1024 + hf * 512 + 512],
                        start=(c == 0), stop=(c == KD - 1)))
                p.wait("scalar", "tensor", mm)
                m = p.do("scalar", lambda e, i=i, hf=hf, slot=slot: e.activation(
                    Fb[:, i * 1024 + hf * 512: i * 1024 + hf * 512 + 512],
                    slot[:, 0:512], AF.Identity))
                fcp.append(m)
        _bar(p)
        p.wait("gpsimd", "scalar", fcp[-1])

        p.dma("gpsimd", WZ[:, :].rearrange("p (c n) -> p c n", c=KD),
              W["winz" + pf].rearrange("(c p) n -> p c n", p=128), "l3rw")
        p.dma("gpsimd", WX[:, :].rearrange("p (c n) -> p c n", c=KD),
              W["winx" + pf].rearrange("(c p) n -> p c n", p=128), "l3rw")
        p.dma("gpsimd", Wdt[:, :].rearrange("p (c n) -> p c n", c=KD),
              W["windt" + pf].rearrange("(c p) n -> p c n", p=128), "l3rw")
        p.dma("sync", cwr[:, :], W["cw" + pf][:, :], "l3rs")
        p.dma("sync", cbr[:, :], W["cb" + pf][:, :], "l3rs")
        p.dma("sync", r32a[0:1, :], W["dtb" + pf][:, :], "l3rs")
        dms = p.dma("sync", arow[0:1, :], W["alog" + pf][:, :], "l3rs")
        p.dma("gpsimd", rD[0:1, :], W["D" + pf][:, :], "l3rw")
        dm = p.dma("gpsimd", rnw[0:1, :], W["nw" + pf][:, :], "l3rw")
        if di == 0:
            dm = p.dma("gpsimd", rbo[0:1, :], W["bo"][:, :], "l3rw")
        p.waitg("scalar", "l3rw", dm)
        p.waitg("vector", "l3rw", dm)
        p.waitg("tensor", "l3rw", dm)
        p.waitg("scalar", "l3rs", dms)
        p.waitg("vector", "l3rs", dms)
        p.waitg("tensor", "l3rs", dms)
        m = p.do("scalar", lambda e: e.activation(arow[0:1, :], arow[0:1, :], AF.Exp,
                                                  scale=1.0))
        p.wait("vector", "scalar", m)
        m = p.do("vector", lambda e: e.tensor_scalar(arow[0:1, :], arow[0:1, :], -1.0,
                                                     None, OP.mult, OP.bypass))
        p.wait("tensor", "vector", m)
        mm = p.do("tensor", lambda e: e.matmul(B["ps_c"][:, 0:32], B["ones_row"][0:1, 0:128],
                                               arow[0:1, :], start=True, stop=True))
        p.wait("vector", "tensor", mm)
        p.do("vector", lambda e: e.tensor_copy(Abc[:, :], B["ps_c"][:, 0:32]))
        _bar(p)

        for sc in range(NSC):
            prc = []
            for i in range(NXT):
                slot = B["ps_w"][:, (i % 4) * 512: (i % 4) * 512 + SCW]
                if i >= 4:
                    p.wait("tensor", "scalar", prc[i - 4])
                mm = 0
                for c in range(KD):
                    mm = p.do("tensor", lambda e, c=c, i=i, sc=sc, slot=slot: e.matmul(
                        slot, WX[:, c * CW + i * 128: c * CW + (i + 1) * 128],
                        xnT[:, c * T + sc * SCW: c * T + (sc + 1) * SCW],
                        start=(c == 0), stop=(c == KD - 1)))
                p.wait("scalar", "tensor", mm)
                m = p.do("scalar", lambda e, i=i, slot=slot: e.activation(
                    xbct[:, i * SCW:(i + 1) * SCW], slot, AF.Identity))
                prc.append(m)
            _bar(p)
            tailO = tails[sc % 2]
            tailN = tails[1 - sc % 2]
            p.do("vector", lambda e, tailN=tailN: e.tensor_copy(
                tailN[:, :].rearrange("p (i k) -> p i k", i=NXT),
                xbct[:, :].rearrange("p (i w) -> p i w", i=NXT)[:, :, SCW - 3:SCW]))
            for i in range(NXT):
                p.do("vector", lambda e, i=i, tailO=tailO: e.tensor_copy(
                    catp[:, 0:3], tailO[:, i * 3:(i + 1) * 3]))
                p.do("vector", lambda e, i=i: e.tensor_copy(
                    catp[:, 3:3 + SCW], xbct[:, i * SCW:(i + 1) * SCW]))
                p.do("vector", lambda e, i=i: e.tensor_scalar(
                    sq[:, 0:SCW], catp[:, 0:SCW], cwr[:, i * 4: i * 4 + 1], None,
                    OP.mult, OP.bypass))
                for k in range(1, 4):
                    p.do("vector", lambda e, i=i, k=k: e.tensor_scalar(
                        sq[:, SCW:2 * SCW], catp[:, k:k + SCW], cwr[:, i * 4 + k: i * 4 + k + 1],
                        None, OP.mult, OP.bypass))
                    p.do("vector", lambda e: e.tensor_tensor(
                        sq[:, 0:SCW], sq[:, 0:SCW], sq[:, SCW:2 * SCW], OP.add))
                m = p.do("vector", lambda e, i=i: e.tensor_scalar(
                    sq[:, 0:SCW], sq[:, 0:SCW], cbr[:, i:i + 1], None, OP.add, OP.bypass))
                p.wait("scalar", "vector", m)
                m = p.do("scalar", lambda e, i=i: e.activation(
                    xbct[:, i * SCW:(i + 1) * SCW], sq[:, 0:SCW], AF.Silu))
                p.wait("vector", "scalar", m)
            _bar(p)
            for qq in range(2):
                mm = 0
                for c in range(KD):
                    p.do("tensor", lambda e, c=c, qq=qq, sc=sc: e.matmul(
                        B["ps_c"][:, 0:32],
                        xnT[:, c * T + sc * SCW + qq * 128: c * T + sc * SCW + (qq + 1) * 128],
                        Wdt[:, c * 32:(c + 1) * 32], start=(c == 0), stop=False))
                mm = p.do("tensor", lambda e: e.matmul(
                    B["ps_c"][:, 0:32], B["ones_row"][0:1, 0:128], r32a[0:1, :],
                    start=False, stop=True))
                p.wait("vector", "tensor", mm)
                p.do("vector", lambda e: e.tensor_copy(w32[:, 0:32], B["ps_c"][:, 0:32]))
                m = p.do("vector", lambda e: e.tensor_scalar(w32[:, 32:64], w32[:, 0:32],
                                                             0.0, None, OP.max, OP.bypass))
                p.wait("scalar", "vector", m)
                m = p.do("scalar", lambda e: e.activation(w32[:, 64:96], w32[:, 0:32], AF.Abs))
                m = p.do("scalar", lambda e: e.activation(w32[:, 64:96], w32[:, 64:96],
                                                          AF.Exp, scale=-1.0))
                p.wait("vector", "scalar", m)
                m = p.do("vector", lambda e: e.tensor_scalar(w32[:, 64:96], w32[:, 64:96],
                                                             1.0, None, OP.add, OP.bypass))
                p.wait("scalar", "vector", m)
                m = p.do("scalar", lambda e: e.activation(w32[:, 64:96], w32[:, 64:96], AF.Ln))
                p.wait("vector", "scalar", m)
                m = p.do("vector", lambda e, qq=qq: e.tensor_tensor(
                    dtc[:, qq * 32:(qq + 1) * 32], w32[:, 32:64], w32[:, 64:96], OP.add))
                p.wait("tensor", "vector", m)

            for qi in range(2):
                q = sc * 2 + qi
                dtq = dtc[:, qi * 32:(qi + 1) * 32]
                if store_marks:
                    p.waitg("sync", "l3o", store_marks[-1])
                if di == 0:
                    xdm = p.dma("sync", xtile[:, :],
                                x_in.rearrange("(t p) d -> p t d", p=128)[:, q, :], "l3x")
                else:
                    xdm = p.dma("sync", xtile[:, :],
                                x_out.rearrange("(t p) d -> p t d", p=128)[:, 15 - q, :],
                                "l3x")
                p.do("vector", lambda e, dtq=dtq: e.tensor_tensor(amat[:, :], dtq,
                                                                  Abc[:, :], OP.mult))
                _bar(p)
                p.do("tensor", lambda e: e.matmul(B["ps_b"][:, 0:32], UT1[:, :], amat[:, :],
                                                  start=True, stop=True))
                p.do("tensor", lambda e: e.matmul(B["ps_b"][:, 32:64], LTS[:, :], amat[:, :],
                                                  start=True, stop=True))
                p.do("tensor", lambda e: e.matmul(B["ps_b"][0:32, 64:65], amat[:, :],
                                                  B["ones_col"][:, 0:1], start=True, stop=True))
                _bar(p)
                p.do("vector", lambda e: e.tensor_copy(cumA[:, :], B["ps_b"][:, 0:32]))
                m = p.do("vector", lambda e: e.tensor_copy(wmat[:, :], B["ps_b"][:, 32:64]))
                p.wait("scalar", "vector", m)
                p.do("scalar", lambda e: e.activation(gam[:, :], cumA[:, :], AF.Exp))
                p.do("scalar", lambda e: e.activation(wmat[:, :], wmat[:, :], AF.Exp))
                m = p.do("scalar", lambda e: e.activation(glc[:, 0:1], B["ps_b"][0:32, 64:65],
                                                          AF.Exp))
                p.wait("vector", "scalar", m)
                m = p.do("vector", lambda e, dtq=dtq: e.tensor_tensor(wmat[:, :], wmat[:, :],
                                                                      dtq, OP.mult))
                _bar(p)
                p.do("tensor", lambda e: e.transpose(B["ps_b"][0:32, 128:256], cumA[:, :],
                                                     B["ident32"][:, :]))
                _bar(p)
                p.do("vector", lambda e: e.tensor_copy(cumAT[:, :], B["ps_b"][0:32, 128:256]))
                p.do("tensor", lambda e, qi=qi: e.transpose(
                    psA[:, 0:128], xbct[:, 16 * SCW + qi * 128: 16 * SCW + (qi + 1) * 128],
                    B["identbf"][:, :]))
                mm = p.do("tensor", lambda e, qi=qi: e.matmul(
                    B["ps_c"][:, 0:128],
                    xbct[0:64, 16 * SCW + qi * 128: 16 * SCW + (qi + 1) * 128],
                    xbct[0:64, 17 * SCW + qi * 128: 17 * SCW + (qi + 1) * 128],
                    start=True, stop=True))
                p.wait("vector", "tensor", mm)
                p.do("vector", lambda e: e.tensor_copy(Btok[:, :], psA[:, 0:128]))
                p.do("vector", lambda e: e.tensor_copy(Ssb[:, :], B["ps_c"][:, 0:128]))
                p.do("vector", lambda e, qi=qi: e.tensor_copy(
                    CTf[:, :], xbct[0:64, 17 * SCW + qi * 128: 17 * SCW + (qi + 1) * 128]))
                cps = []
                _bar(p)
                for i in range(16):
                    slot = psA if i % 2 == 0 else psB
                    if i >= 2:
                        p.wait("tensor", "vector", cps[i - 2])
                    p.do("tensor", lambda e, i=i, qi=qi, slot=slot: e.transpose(
                        slot[:, 0:128], xbct[:, i * SCW + qi * 128: i * SCW + (qi + 1) * 128],
                        B["identbf"][:, :]))
                    p.wait("vector", "tensor", p.mark("tensor"))
                    m = p.do("vector", lambda e, i=i, slot=slot: e.tensor_copy(
                        Xtok[:, i * 128:(i + 1) * 128], slot[:, 0:128]))
                    cps.append(m)
                _bar(p)
                for j in range(4):
                    p.do("tensor", lambda e, j=j: e.matmul(
                        psl[j][:, 0:512], CTf[:, :], hT[:, j * 512:(j + 1) * 512],
                        start=True, stop=True))
                _bar(p)
                for h in range(NH):
                    j, jj = h // 8, h % 8
                    p.do("vector", lambda e, h=h, j=j, jj=jj: e.tensor_scalar(
                        ybuf[:, h * 64:(h + 1) * 64], psl[j][:, jj * 64:(jj + 1) * 64],
                        gam[:, h:h + 1], None, OP.mult, OP.bypass))
                _bar(p)
                e1s = [e1, e1b]
                Mbfs = [Mbf, Mbf2]
                post_mark = {}
                m = p.do("vector", lambda e: e.tensor_scalar(
                    Lh2[0][:, :], ones32[:, :], B["ident32"][0:32, 0:1], None,
                    OP.mult, OP.bypass))
                p.wait("tensor", "vector", m)
                rb_mark = p.do("tensor", lambda e: e.matmul(
                    B["ps_a"][:, 0:128], Lh2[0][:, :], cumAT[:, :],
                    start=True, stop=True))
                for h in range(NH):
                    par = h % 2
                    eh = e1s[par]
                    Mh = Mbfs[par]
                    p.wait("vector", "tensor", rb_mark)
                    dm_ = p.do("vector", lambda e, h=h, eh=eh: e.tensor_scalar(
                        eh[:, :], B["ps_a"][:, 0:128], cumA[:, h:h + 1], None,
                        OP.subtract, OP.bypass))
                    m = p.do("vector", lambda e, eh=eh: e.tensor_tensor(
                        eh[:, :], eh[:, :], UT1[:, :], OP.mult))
                    if h + 1 < NH:
                        lm_ = p.do("vector", lambda e, h=h: e.tensor_scalar(
                            Lh2[1][:, :], ones32[:, :], B["ident32"][0:32, h + 1:h + 2],
                            None, OP.mult, OP.bypass))
                        p.wait("tensor", "vector", lm_)
                        rb_mark = p.do("tensor", lambda e: e.matmul(
                            B["ps_a"][:, 0:128], Lh2[1][:, :], cumAT[:, :],
                            start=True, stop=True))
                        Lh2 = [Lh2[1], Lh2[0]]
                    p.wait("scalar", "vector", m)
                    m = p.do("scalar", lambda e, eh=eh: e.activation(eh[:, :], eh[:, :],
                                                                    AF.Exp))
                    p.wait("vector", "scalar", m)
                    p.do("vector", lambda e, eh=eh: e.tensor_tensor(
                        eh[:, :], eh[:, :], UT1[:, :], OP.mult))
                    p.do("vector", lambda e, h=h, dtq=dtq, eh=eh: e.tensor_scalar(
                        eh[:, :], eh[:, :], dtq[:, h:h + 1], None, OP.mult, OP.bypass))
                    if h - 2 in post_mark:
                        p.wait("vector", "tensor", post_mark[h - 2])
                    m = p.do("vector", lambda e, eh=eh, Mh=Mh: e.tensor_tensor(
                        Mh[:, :], eh[:, :], Ssb[:, :], OP.mult))
                    p.wait("tensor", "vector", m)
                    post_mark[h] = p.do("tensor", lambda e, h=h, Mh=Mh: e.matmul(
                        B["ps_w"][:, h * 64:(h + 1) * 64], Mh[:, :],
                        Xtok[:, h * 64:(h + 1) * 64], start=True, stop=True))
                _bar(p)
                for j in range(4):
                    p.do("vector", lambda e, j=j: e.tensor_tensor(
                        ybuf[:, j * 512:(j + 1) * 512], ybuf[:, j * 512:(j + 1) * 512],
                        B["ps_w"][:, j * 512:(j + 1) * 512], OP.add))
                _bar(p)
                for j in range(4):
                    p.do("tensor", lambda e, j=j: e.matmul(
                        psl[j][:, 0:512], onesbf[0:1, :],
                        rD[0:1, j * 512:(j + 1) * 512], start=True, stop=True))
                _bar(p)
                for j in range(4):
                    p.do("vector", lambda e, j=j: e.tensor_tensor(
                        sq[:, 0:512], psl[j][:, 0:512], Xtok[:, j * 512:(j + 1) * 512],
                        OP.mult))
                    p.do("vector", lambda e, j=j: e.tensor_tensor(
                        ybuf[:, j * 512:(j + 1) * 512], ybuf[:, j * 512:(j + 1) * 512],
                        sq[:, 0:512], OP.add))
                _bar(p)
                p.do("vector", lambda e: e.tensor_scalar(
                    w32[0:32, 96:128], B["ident32"][0:32, 0:32], glc[:, 0:1], None,
                    OP.mult, OP.bypass))
                for h in range(NH):
                    p.do("vector", lambda e, h=h: e.tensor_scalar(
                        xw[:, h * 64:(h + 1) * 64], Xtok[:, h * 64:(h + 1) * 64],
                        wmat[:, h:h + 1], None, OP.mult, OP.bypass))
                _bar(p)
                p.do("tensor", lambda e: e.matmul(
                    B["ps_a"][0:64, 0:32], ones32[:, 0:64], w32[0:32, 96:128],
                    start=True, stop=True))
                _bar(p)
                p.do("vector", lambda e: e.tensor_copy(gLb[:, :], B["ps_a"][0:64, 0:32]))
                _bar(p)
                for h in range(NH):
                    p.do("vector", lambda e, h=h: e.tensor_scalar(
                        hT[:, h * 64:(h + 1) * 64], hT[:, h * 64:(h + 1) * 64],
                        gLb[:, h:h + 1], None, OP.mult, OP.bypass))
                _bar(p)
                for j in range(4):
                    p.do("tensor", lambda e, j=j: e.matmul(
                        psl[j][0:64, 0:512], Btok[:, 0:64], xw[:, j * 512:(j + 1) * 512],
                        start=True, stop=True))
                _bar(p)
                for j in range(4):
                    p.do("vector", lambda e, j=j: e.tensor_tensor(
                        hT[:, j * 512:(j + 1) * 512], hT[:, j * 512:(j + 1) * 512],
                        psl[j][0:64, 0:512], OP.add))
                _bar(p)
                tok0 = sc * SCW + qi * 128
                for j in range(4):
                    mm = 0
                    for c in range(KD):
                        mm = p.do("tensor", lambda e, c=c, j=j, tok0=tok0: e.matmul(
                            psl[j][:, 0:512], xnT[:, c * T + tok0: c * T + tok0 + 128],
                            WZ[:, c * DI + j * 512: c * DI + (j + 1) * 512],
                            start=(c == 0), stop=(c == KD - 1)))
                _bar(p)
                for j in range(4):
                    m = p.do("scalar", lambda e, j=j: e.activation(sq[:, 0:512],
                                                                   psl[j][:, 0:512], AF.Silu))
                    p.wait("vector", "scalar", m)
                    m = p.do("vector", lambda e, j=j: e.tensor_tensor(
                        ybuf[:, j * 512:(j + 1) * 512], ybuf[:, j * 512:(j + 1) * 512],
                        sq[:, 0:512], OP.mult))
                    p.wait("scalar", "vector", m)
                p.do("scalar", lambda e: e.activation(sq[:, 0:D], ybuf[:, 0:D], AF.Square,
                                                      accum_out=stat[:, 0:1]))
                m = p.do("scalar", lambda e: e.activation(sq[:, 0:D], ybuf[:, D:2 * D],
                                                          AF.Square, accum_out=stat[:, 1:2]))
                p.wait("vector", "scalar", m)
                p.do("vector", lambda e: e.tensor_tensor(stat[:, 2:3], stat[:, 0:1],
                                                         stat[:, 1:2], OP.add))
                m = p.do("vector", lambda e: e.tensor_scalar(stat[:, 3:4], stat[:, 2:3],
                                                             1.0 / DI, 1e-5, OP.mult, OP.add))
                p.wait("scalar", "vector", m)
                p.do("scalar", lambda e: e.activation(stat[:, 4:5], stat[:, 3:4], AF.Ln))
                m = p.do("scalar", lambda e: e.activation(stat[:, 5:6], stat[:, 4:5],
                                                          AF.Exp, scale=-0.5))
                p.wait("vector", "scalar", m)
                p.do("vector", lambda e: e.tensor_scalar(ybuf[:, :], ybuf[:, :],
                                                         stat[:, 5:6], None, OP.mult,
                                                         OP.bypass))
                _bar(p)
                for j in range(4):
                    p.do("tensor", lambda e, j=j: e.matmul(
                        psl[j][:, 0:512], onesbf[0:1, :],
                        rnw[0:1, j * 512:(j + 1) * 512], start=True, stop=True))
                _bar(p)
                for j in range(4):
                    p.do("vector", lambda e, j=j: e.tensor_tensor(
                        ybuf[:, j * 512:(j + 1) * 512], ybuf[:, j * 512:(j + 1) * 512],
                        psl[j][:, 0:512], OP.mult))
                _bar(p)
                cps = []
                for i in range(16):
                    slot = B["ps_a"] if i % 2 == 0 else B["ps_b"]
                    if i >= 2:
                        p.wait("tensor", "vector", cps[i - 2])
                    p.do("tensor", lambda e, i=i, slot=slot: e.transpose(
                        slot[:, 0:128], ybuf[:, i * 128:(i + 1) * 128], B["ident32"][:, :]))
                    p.wait("vector", "tensor", p.mark("tensor"))
                    m = p.do("vector", lambda e, i=i, slot=slot: e.tensor_copy(
                        xw[:, i * 128:(i + 1) * 128], slot[:, 0:128]))
                    cps.append(m)
                _bar(p)
                for hf in range(2):
                    for i in range(16):
                        p.do("tensor", lambda e, i=i, hf=hf, di=di: e.matmul(
                            B["ps_w"][:, hf * 512:(hf + 1) * 512], xw[:, i * 128:(i + 1) * 128],
                            Fb[:, i * 1024 + hf * 512: i * 1024 + (hf + 1) * 512],
                            start=(i == 0), stop=(i == 15 and di == 1)))
                    if di == 0:
                        p.do("tensor", lambda e, hf=hf: e.matmul(
                            B["ps_w"][:, hf * 512:(hf + 1) * 512], onesbf[0:1, :],
                            rbo[0:1, hf * 512:(hf + 1) * 512], start=False, stop=True))
                _bar(p)
                p.waitg("vector", "l3x", xdm)
                if di == 0:
                    for hf in range(2):
                        p.do("vector", lambda e, hf=hf: e.tensor_tensor(
                            xtile[:, hf * 512:(hf + 1) * 512], xtile[:, hf * 512:(hf + 1) * 512],
                            B["ps_w"][:, hf * 512:(hf + 1) * 512], OP.add))
                    m = p.mark("vector")
                    p.wait("sync", "vector", m)
                    sm = p.dma("sync", x_out.rearrange("(t p) d -> p t d", p=128)[:, q, :],
                               xtile[:, :], "l3o")
                    store_marks.append(sm)
                else:
                    m = 0
                    for hf in range(2):
                        m = p.do("scalar", lambda e, hf=hf: e.activation(
                            obuf[:, hf * 512:(hf + 1) * 512], B["ps_w"][:, hf * 512:(hf + 1) * 512],
                            AF.Identity))
                    p.wait("tensor", "scalar", m)
                    for hf in range(2):
                        p.do("tensor", lambda e, hf=hf: e.matmul(
                            psl[hf][:, 0:512], Rbf[:, :].bitcast(BF16),
                            obuf[:, hf * 512:(hf + 1) * 512], start=True, stop=True))
                    _bar(p)
                    for hf in range(2):
                        p.do("vector", lambda e, hf=hf: e.tensor_tensor(
                            xtile[:, hf * 512:(hf + 1) * 512], xtile[:, hf * 512:(hf + 1) * 512],
                            psl[hf][:, 0:512], OP.add))
                    m = p.mark("vector")
                    p.wait("sync", "vector", m)
                    sm = p.dma("sync",
                               x_out.rearrange("(t p) d -> p t d", p=128)[:, 15 - q, :],
                               xtile[:, :], "l3o")
                    store_marks.append(sm)
                _bar(p)
        if di == 0:
            p.waitg("sync", "l3o", store_marks[-1])
            p.waitg("vector", "l3o", store_marks[-1])
    return ("sync", store_marks[-1])




def emit_l2_attn(p, B, W, coef, x_in, x_out):
    assert coef == 1.0
    NH, DK, DV = 12, 64, 128
    KDIM, VDIM = NH * DK, NH * DV
    NQT = 24
    CW3 = NQT * 128
    SCW = 256
    NSC = T // SCW
    SCALE = DK ** (-0.5)
    stat, sq = B["stat"], B["sq"]
    psA = B["ps_a"].bitcast(BF16)
    psB = B["ps_b"].bitcast(BF16)
    psl = [B["ps_a"], B["ps_b"], B["ps_c"], B["ps_d"]]

    xnT = p.sb("a2_xnT", [128, KD * T], BF16)
    WQKV = p.sb("a2_WQKV", [128, KD * CW3], BF16)
    WG = p.sb("a2_WG", [128, KD * VDIM], BF16)
    WO = p.sb("a2_WO", [128, NH * 1024], BF16)
    Wgkb = p.sb("a2_Wgkb", [128, KD * 24], BF16)
    xqkv = p.sb("a2_xqkv", [128, NQT * SCW], BF16)
    qtok = p.sb("a2_qtok", [128, KDIM], BF16)
    ktok = p.sb("a2_ktok", [128, KDIM], BF16)
    vtok = p.sb("a2_vtok", [128, VDIM], BF16)
    qT = p.sb("a2_qT", [64, NH * 128], BF16)
    kT = p.sb("a2_kT", [64, NH * 128], BF16)
    hS = p.sb("a2_hS", [64, NH * 128], F32)
    hSb = p.sb("a2_hSb", [64, NH * 128], BF16)
    obuf = p.sb("a2_obuf", [128, VDIM], F32)
    oT = p.sb("a2_oT", [128, NH * 128], BF16)
    xtile = p.sb("a2_xtile", [128, D], F32)
    xnb = p.sb("a2_xnb", [128, D], BF16)
    Eincl = p.sb("a2_Ei", [128, 128], F32)
    Estr = p.sb("a2_Es", [128, 128], F32)
    bbc = p.sb("a2_bbc", [128, 128], F32)
    wtile = p.sb("a2_wt", [128, 128], F32)
    Df = p.sb("a2_Df", [128, 128], F32)
    dlb = p.sb("a2_dlb", [128, 128], BF16)
    Nb = [p.sb("a2_Nb0", [128, 128], F32), p.sb("a2_Nb1", [128, 128], F32)]
    NTb = [p.sb("a2_NTb0", [128, 128], F32), p.sb("a2_NTb1", [128, 128], F32)]
    Mqb = p.sb("a2_Mqb", [128, 128], F32)
    UT1 = p.sb("a2_UT1", [128, 128], F32)
    UTs = p.sb("a2_UTs", [128, 128], F32)
    LTS = p.sb("a2_LTS", [128, 128], F32)
    Lh = p.sb("a2_Lh", [12, 128], F32)
    ones32 = p.sb("a2_ones32", [32, 128], F32)
    GT2 = p.sb("a2_GT2", [12, 256], F32)
    gmat = p.sb("a2_gmat", [128, 2 * 12], F32)
    bmat = p.sb("a2_bmat", [128, 2 * 12], F32)
    Gm = p.sb("a2_Gm", [128, 12], F32)
    lam = p.sb("a2_lam", [128, 12], F32)
    wmat = p.sb("a2_wmat", [128, 12], F32)
    glc = p.sb("a2_glc", [12, 1], F32)
    gLb = p.sb("a2_gLb", [64, 12], F32)
    nrm = p.sb("a2_nrm", [128, 2 * 12], F32)
    Abc = p.sb("a2_Abc", [128, 12], F32)
    arow = p.sb("a2_arow", [1, 12], F32)
    r24 = p.sb("a2_r24", [1, 24], F32)
    gnb = p.sb("a2_gnb", [128, 128], F32)
    cwr = p.sb("a2_cwr", [128, NQT * 4], F32)
    tails = [p.sb("a2_tailA", [128, NQT * 3], F32), p.sb("a2_tailB", [128, NQT * 3], F32)]
    catp = sq[:, 512:512 + SCW + 3]
    w32 = p.sb("a2_w32", [128, 128], F32)

    p.wait("gpsimd", "vector", B["init_v"])
    p.do("gpsimd", lambda e: e.memset(ones32[:, :], 1.0))
    p.do("gpsimd", lambda e: e.memset(tails[0][:, :], 0.0))
    p.do("gpsimd", lambda e: e.memset(tails[1][:, :], 0.0))
    p.do("gpsimd", lambda e: e.memset(hS[:, :], 0.0))
    p.do("gpsimd", lambda e: e.memset(hSb[:, :], 0.0))
    p.do("gpsimd", lambda e: e.iota(sq[:, 0:128], pattern=[[1, 128]], base=0,
                                    channel_multiplier=0, allow_small_or_imprecise_dtypes=True))
    gm = p.do("gpsimd", lambda e: e.iota(UT1[:, :], pattern=[[0, 128]], base=0,
                                         channel_multiplier=1,
                                         allow_small_or_imprecise_dtypes=True))
    p.wait("vector", "gpsimd", gm)
    p.do("vector", lambda e: e.tensor_tensor(LTS[:, :], sq[:, 0:128], UT1[:, :], OP.is_lt))
    p.do("vector", lambda e: e.tensor_tensor(UTs[:, :], sq[:, 0:128], UT1[:, :], OP.is_gt))
    p.do("vector", lambda e: e.tensor_tensor(UT1[:, :], sq[:, 0:128], UT1[:, :], OP.is_ge))
    _bar(p)

    for t in range(NT):
        dm = p.dma("sync", xtile[:, :], x_in.rearrange("(t p) d -> p t d", p=128)[:, t, :],
                   "l2x")
        p.waitg("vector", "l2x", dm)
        p.waitg("scalar", "l2x", dm)
        xt = xtile[:, :]
        p.do("vector", lambda e, xt=xt: e.tensor_reduce(stat[:, 0:1], xt, AX.X, OP.add))
        m = p.do("vector", lambda e: e.tensor_scalar(stat[:, 1:2], stat[:, 0:1],
                                                     -1.0 / D, None, OP.mult, OP.bypass))
        p.wait("scalar", "vector", m)
        m = p.do("scalar", lambda e, xt=xt: e.activation(sq[:, 0:D], xt, AF.Square,
                                                         bias=stat[:, 1:2], scale=1.0,
                                                         accum_out=stat[:, 2:3]))
        p.wait("vector", "scalar", m)
        m = p.do("vector", lambda e: e.tensor_scalar(stat[:, 3:4], stat[:, 2:3],
                                                     1.0 / D, LN_EPS, OP.mult, OP.add))
        p.wait("scalar", "vector", m)
        p.do("scalar", lambda e: e.activation(stat[:, 4:5], stat[:, 3:4], AF.Ln))
        m = p.do("scalar", lambda e: e.activation(stat[:, 5:6], stat[:, 4:5], AF.Exp, scale=-0.5))
        p.wait("vector", "scalar", m)
        m = p.do("vector", lambda e: e.tensor_tensor(stat[:, 6:7], stat[:, 1:2],
                                                     stat[:, 5:6], OP.mult))
        p.wait("scalar", "vector", m)
        m = p.do("scalar", lambda e, xt=xt: e.activation(
            xnb[:, :], xt, AF.Identity, bias=stat[:, 6:7], scale=stat[:, 5:6]))
        p.wait("vector", "scalar", m)
        p.wait("tensor", "scalar", m)
        cps = []
        for c in range(KD):
            slot = psA if c % 2 == 0 else psB
            if c >= 2:
                p.wait("tensor", "vector", cps[c - 2])
            p.do("tensor", lambda e, c=c, slot=slot: e.transpose(
                slot[:, 0:128], xnb[:, c * 128:(c + 1) * 128], B["identbf"][:, :]))
            p.wait("vector", "tensor", p.mark("tensor"))
            m = p.do("vector", lambda e, t=t, c=c, slot=slot: e.tensor_copy(
                xnT[:, c * T + t * 128: c * T + (t + 1) * 128], slot[:, 0:128]))
            cps.append(m)
        _bar(p)
        p.wait("sync", "vector", cps[-1])
        p.wait("sync", "scalar", p.mark("scalar"))

    p.wait("gpsimd", "tensor", p.mark("tensor"))
    p.dma("gpsimd", WQKV[:, :].rearrange("p (c n) -> p c n", c=KD)[:, :, 0:KDIM],
          W["wq"].rearrange("(c p) n -> p c n", p=128), "l2w")
    p.dma("gpsimd", WQKV[:, :].rearrange("p (c n) -> p c n", c=KD)[:, :, KDIM:2 * KDIM],
          W["wk"].rearrange("(c p) n -> p c n", p=128), "l2w")
    p.dma("gpsimd", WQKV[:, :].rearrange("p (c n) -> p c n", c=KD)[:, :, 2 * KDIM:CW3],
          W["wv"].rearrange("(c p) n -> p c n", p=128), "l2w")
    p.dma("gpsimd", WG[:, :].rearrange("p (c n) -> p c n", c=KD),
          W["wg"].rearrange("(c p) n -> p c n", p=128), "l2w")
    p.dma("gpsimd", WO[:, :].rearrange("p (c n) -> p c n", c=NH),
          W["wo"].rearrange("(c p) n -> p c n", p=128), "l2w")
    p.dma("gpsimd", Wgkb[:, :].rearrange("p (c n) -> p c n", c=KD)[:, :, 0:12],
          W["wgk"].rearrange("(c p) n -> p c n", p=128), "l2w")
    dm = p.dma("gpsimd", Wgkb[:, :].rearrange("p (c n) -> p c n", c=KD)[:, :, 12:24],
               W["wb"].rearrange("(c p) n -> p c n", p=128), "l2w")
    p.dma("sync", cwr[:, 0:24], W["cwq"][:, :], "l2ws")
    p.dma("sync", cwr[:, 24:48], W["cwk"][:, :], "l2ws")
    p.dma("sync", cwr[:, 48:96], W["cwv"][:, :], "l2ws")
    p.dma("sync", arow[0:1, :], W["alog"][:, :], "l2ws")
    p.dma("sync", r24[0:1, 0:12], W["dtb"][:, :], "l2ws")
    p.dma("sync", r24[0:1, 12:24], W["bb"][:, :], "l2ws")
    dms = p.dma("sync", w32[0:1, 0:128], W["gnw"][:, :], "l2ws")
    p.waitg("scalar", "l2w", dm)
    p.waitg("vector", "l2w", dm)
    p.waitg("tensor", "l2w", dm)
    p.waitg("scalar", "l2ws", dms)
    p.waitg("vector", "l2ws", dms)
    p.waitg("tensor", "l2ws", dms)
    m = p.do("scalar", lambda e: e.activation(arow[0:1, :], arow[0:1, :], AF.Exp, scale=1.0))
    p.wait("vector", "scalar", m)
    m = p.do("vector", lambda e: e.tensor_scalar(arow[0:1, :], arow[0:1, :], -1.0,
                                                 None, OP.mult, OP.bypass))
    p.wait("tensor", "vector", m)
    p.do("tensor", lambda e: e.matmul(B["ps_c"][:, 0:12], B["ones_row"][0:1, 0:128],
                                      arow[0:1, :], start=True, stop=True))
    mm = p.do("tensor", lambda e: e.matmul(B["ps_c"][:, 128:256], B["ones_row"][0:1, 0:128],
                                           w32[0:1, 0:128], start=True, stop=True))
    p.wait("vector", "tensor", mm)
    p.do("vector", lambda e: e.tensor_copy(Abc[:, :], B["ps_c"][:, 0:12]))
    p.do("vector", lambda e: e.tensor_copy(gnb[:, :], B["ps_c"][:, 128:256]))
    _bar(p)

    store_marks = []
    for sc in range(NSC):
        prc = []
        for i in range(NQT):
            slot = B["ps_w"][:, (i % 4) * 512: (i % 4) * 512 + SCW]
            if i >= 4:
                p.wait("tensor", "scalar", prc[i - 4])
            mm = 0
            for c in range(KD):
                mm = p.do("tensor", lambda e, c=c, i=i, sc=sc, slot=slot: e.matmul(
                    slot, WQKV[:, c * CW3 + i * 128: c * CW3 + (i + 1) * 128],
                    xnT[:, c * T + sc * SCW: c * T + (sc + 1) * SCW],
                    start=(c == 0), stop=(c == KD - 1)))
            p.wait("scalar", "tensor", mm)
            m = p.do("scalar", lambda e, i=i, slot=slot: e.activation(
                xqkv[:, i * SCW:(i + 1) * SCW], slot, AF.Identity))
            prc.append(m)
        _bar(p)
        tailO = tails[sc % 2]
        tailN = tails[1 - sc % 2]
        p.do("vector", lambda e, tailN=tailN: e.tensor_copy(
            tailN[:, :].rearrange("p (i k) -> p i k", i=NQT),
            xqkv[:, :].rearrange("p (i w) -> p i w", i=NQT)[:, :, SCW - 3:SCW]))
        for i in range(NQT):
            p.do("vector", lambda e, i=i, tailO=tailO: e.tensor_copy(
                catp[:, 0:3], tailO[:, i * 3:(i + 1) * 3]))
            p.do("vector", lambda e, i=i: e.tensor_copy(
                catp[:, 3:3 + SCW], xqkv[:, i * SCW:(i + 1) * SCW]))
            p.do("vector", lambda e, i=i: e.tensor_scalar(
                sq[:, 0:SCW], catp[:, 0:SCW], cwr[:, i * 4: i * 4 + 1], None,
                OP.mult, OP.bypass))
            for k in range(1, 4):
                p.do("vector", lambda e, i=i, k=k: e.tensor_scalar(
                    sq[:, SCW:2 * SCW], catp[:, k:k + SCW], cwr[:, i * 4 + k: i * 4 + k + 1],
                    None, OP.mult, OP.bypass))
                m = p.do("vector", lambda e: e.tensor_tensor(
                    sq[:, 0:SCW], sq[:, 0:SCW], sq[:, SCW:2 * SCW], OP.add))
            p.wait("scalar", "vector", m)
            m = p.do("scalar", lambda e, i=i: e.activation(
                xqkv[:, i * SCW:(i + 1) * SCW], sq[:, 0:SCW], AF.Silu))
            p.wait("vector", "scalar", m)
        _bar(p)
        for qq in range(2):
            mm = 0
            for c in range(KD):
                p.do("tensor", lambda e, c=c, qq=qq, sc=sc: e.matmul(
                    B["ps_c"][:, 0:24],
                    xnT[:, c * T + sc * SCW + qq * 128: c * T + sc * SCW + (qq + 1) * 128],
                    Wgkb[:, c * 24:(c + 1) * 24], start=(c == 0), stop=False))
            mm = p.do("tensor", lambda e: e.matmul(
                B["ps_c"][:, 0:24], B["ones_row"][0:1, 0:128], r24[0:1, :],
                start=False, stop=True))
            p.wait("vector", "tensor", mm)
            p.do("vector", lambda e: e.tensor_copy(w32[:, 0:24], B["ps_c"][:, 0:24]))
            m = p.do("vector", lambda e: e.tensor_scalar(w32[:, 24:36], w32[:, 0:12],
                                                         0.0, None, OP.max, OP.bypass))
            p.wait("scalar", "vector", m)
            m = p.do("scalar", lambda e: e.activation(w32[:, 36:48], w32[:, 0:12], AF.Abs))
            m = p.do("scalar", lambda e: e.activation(w32[:, 36:48], w32[:, 36:48],
                                                      AF.Exp, scale=-1.0))
            p.wait("vector", "scalar", m)
            m = p.do("vector", lambda e: e.tensor_scalar(w32[:, 36:48], w32[:, 36:48],
                                                         1.0, None, OP.add, OP.bypass))
            p.wait("scalar", "vector", m)
            m = p.do("scalar", lambda e: e.activation(w32[:, 36:48], w32[:, 36:48], AF.Ln))
            ms = p.do("scalar", lambda e, qq=qq: e.activation(bmat[:, qq * 12:(qq + 1) * 12],
                                                              w32[:, 12:24], AF.Sigmoid))
            p.wait("vector", "scalar", ms)
            p.do("vector", lambda e: e.tensor_tensor(w32[:, 24:36], w32[:, 24:36],
                                                     w32[:, 36:48], OP.add))
            m = p.do("vector", lambda e, qq=qq: e.tensor_tensor(
                gmat[:, qq * 12:(qq + 1) * 12], w32[:, 24:36], Abc[:, :], OP.mult))
            p.wait("tensor", "vector", m)

        for qi in range(2):
            q = sc * 2 + qi
            gkq = gmat[:, qi * 12:(qi + 1) * 12]
            btq = bmat[:, qi * 12:(qi + 1) * 12]
            if store_marks:
                p.waitg("sync", "l2o", store_marks[-1])
            xdm = p.dma("sync", xtile[:, :],
                        x_in.rearrange("(t p) d -> p t d", p=128)[:, q, :], "l2x")
            cps = []
            g = 0
            for i in range(NQT):
                slot = psA if g % 2 == 0 else psB
                if g >= 2:
                    p.wait("tensor", "vector", cps[g - 2])
                p.do("tensor", lambda e, i=i, qi=qi, slot=slot: e.transpose(
                    slot[:, 0:128], xqkv[:, i * SCW + qi * 128: i * SCW + (qi + 1) * 128],
                    B["identbf"][:, :]))
                p.wait("vector", "tensor", p.mark("tensor"))
                if i < 6:
                    dst = qtok[:, i * 128:(i + 1) * 128]
                elif i < 12:
                    dst = ktok[:, (i - 6) * 128:(i - 5) * 128]
                else:
                    dst = vtok[:, (i - 12) * 128:(i - 11) * 128]
                m = p.do("vector", lambda e, dst=dst, slot=slot: e.tensor_copy(
                    dst, slot[:, 0:128]))
                cps.append(m)
                g += 1
            _bar(p)
            p.do("vector", lambda e: e.tensor_tensor(sq[:, 0:KDIM], qtok[:, :],
                                                     qtok[:, :], OP.mult))
            for h in range(NH):
                p.do("vector", lambda e, h=h: e.tensor_reduce(
                    nrm[:, h:h + 1], sq[:, h * 64:(h + 1) * 64], AX.X, OP.add))
            p.do("vector", lambda e: e.tensor_tensor(sq[:, 0:KDIM], ktok[:, :],
                                                     ktok[:, :], OP.mult))
            for h in range(NH):
                p.do("vector", lambda e, h=h: e.tensor_reduce(
                    nrm[:, 12 + h:13 + h], sq[:, h * 64:(h + 1) * 64], AX.X, OP.add))
            m = p.do("vector", lambda e: e.tensor_scalar(nrm[:, :], nrm[:, :], 1.0, 1e-6,
                                                         OP.mult, OP.add))
            p.wait("scalar", "vector", m)
            p.do("scalar", lambda e: e.activation(nrm[:, :], nrm[:, :], AF.Ln))
            m = p.do("scalar", lambda e: e.activation(nrm[:, :], nrm[:, :], AF.Exp,
                                                      scale=-0.5))
            p.wait("vector", "scalar", m)
            m = p.do("vector", lambda e: e.tensor_scalar(nrm[:, 0:12], nrm[:, 0:12], SCALE,
                                                         None, OP.mult, OP.bypass))
            for h in range(NH):
                p.do("vector", lambda e, h=h: e.tensor_scalar(
                    qtok[:, h * 64:(h + 1) * 64], qtok[:, h * 64:(h + 1) * 64],
                    nrm[:, h:h + 1], None, OP.mult, OP.bypass))
                m = p.do("vector", lambda e, h=h: e.tensor_scalar(
                    ktok[:, h * 64:(h + 1) * 64], ktok[:, h * 64:(h + 1) * 64],
                    nrm[:, 12 + h:13 + h], None, OP.mult, OP.bypass))
            _bar(p)
            p.do("tensor", lambda e, gkq=gkq: e.matmul(B["ps_b"][:, 0:12], UT1[:, :], gkq,
                                                       start=True, stop=True))
            p.do("tensor", lambda e, gkq=gkq: e.matmul(B["ps_b"][:, 12:24], LTS[:, :], gkq,
                                                       start=True, stop=True))
            p.do("tensor", lambda e, gkq=gkq: e.matmul(B["ps_b"][0:12, 24:25], gkq,
                                                       B["ones_col"][:, 0:1],
                                                       start=True, stop=True))
            _bar(p)
            p.do("vector", lambda e: e.tensor_copy(Gm[:, :], B["ps_b"][:, 0:12]))
            m = p.do("vector", lambda e: e.tensor_copy(wmat[:, :], B["ps_b"][:, 12:24]))
            p.wait("scalar", "vector", m)
            p.do("scalar", lambda e: e.activation(lam[:, :], Gm[:, :], AF.Exp))
            p.do("scalar", lambda e: e.activation(wmat[:, :], wmat[:, :], AF.Exp))
            m = p.do("scalar", lambda e: e.activation(glc[:, 0:1], B["ps_b"][0:12, 24:25],
                                                      AF.Exp))
            p.wait("vector", "scalar", m)
            m = p.do("vector", lambda e: e.tensor_scalar(
                w32[0:12, 64:76], B["ident32"][0:12, 0:12], glc[:, 0:1], None,
                OP.mult, OP.bypass))
            p.wait("tensor", "vector", m)
            mm = p.do("tensor", lambda e: e.matmul(
                B["ps_c"][0:64, 0:12], ones32[0:12, 0:64], w32[0:12, 64:76],
                start=True, stop=True))
            p.do("tensor", lambda e: e.transpose(B["ps_c"][0:12, 128:256], Gm[:, :],
                                                 B["ident32"][:, :]))
            mm = p.do("tensor", lambda e, btq=btq: e.transpose(
                B["ps_c"][0:12, 256:384], btq, B["ident32"][:, :]))
            _bar(p)
            p.do("vector", lambda e: e.tensor_copy(gLb[:, :], B["ps_c"][0:64, 0:12]))
            p.do("vector", lambda e: e.tensor_copy(GT2[:, 0:128], B["ps_c"][0:12, 128:256]))
            m = p.do("vector", lambda e: e.tensor_copy(GT2[:, 128:256],
                                                       B["ps_c"][0:12, 256:384]))
            p.wait("tensor", "vector", m)
            cps = []
            g = 0
            for h in range(NH):
                for which, tok, dstb in (("q", qtok, qT), ("k", ktok, kT)):
                    slot = B["ps_d"] if g % 2 == 0 else B["ps_w"]
                    if g >= 2:
                        p.wait("tensor", "vector", cps[g - 2])
                    p.do("tensor", lambda e, h=h, tok=tok, slot=slot: e.transpose(
                        slot.bitcast(BF16)[0:64, 0:128], tok[:, h * 64:(h + 1) * 64],
                        B["identbf"][:, :]))
                    p.wait("vector", "tensor", p.mark("tensor"))
                    m = p.do("vector", lambda e, h=h, dstb=dstb, slot=slot: e.tensor_copy(
                        dstb[:, h * 128:(h + 1) * 128], slot.bitcast(BF16)[0:64, 0:128]))
                    cps.append(m)
                    g += 1
            _bar(p)
            if DEBUG_TAPS and sc == 0 and qi == 0:
                p.wait("sync", "vector", p.mark("vector"))
                p.wait("sync", "scalar", p.mark("scalar"))
                for nm, b_ in (("qtok", qtok), ("ktok", ktok), ("vtok", vtok),
                               ("gk0", gmat), ("bt0", bmat), ("Gm0", Gm),
                               ("lam0", lam), ("wm0", wmat)):
                    db = p.dram("dbg_" + nm, list(b_.shape), b_.ap().dtype,
                                kind="ExternalOutput")
                    dmt = p.dma("sync", db[:, :], b_[:, :], "l2o")
                p.waitg("vector", "l2o", dmt)
                p.waitg("scalar", "l2o", dmt)
                _bar(p)
            for h in range(NH):
                m = p.do("vector", lambda e, h=h: e.tensor_scalar(
                    Lh[:, :], ones32[0:12, :], B["ident32"][0:12, h:h + 1], None,
                    OP.mult, OP.bypass))
                p.wait("tensor", "vector", m)
                mm = p.do("tensor", lambda e: e.matmul(
                    B["ps_a"][:, 0:256], Lh[:, :], GT2[:, :], start=True, stop=True))
                p.do("tensor", lambda e, h=h: e.matmul(
                    B["ps_b"][:, 0:128], kT[:, h * 128:(h + 1) * 128],
                    kT[:, h * 128:(h + 1) * 128], start=True, stop=True))
                p.do("tensor", lambda e, h=h: e.matmul(
                    B["ps_b"][:, 128:256], kT[:, h * 128:(h + 1) * 128],
                    qT[:, h * 128:(h + 1) * 128], start=True, stop=True))
                p.do("tensor", lambda e, h=h: e.matmul(
                    B["ps_c"][:, 0:128], kT[:, h * 128:(h + 1) * 128],
                    hSb[:, h * 128:(h + 1) * 128], start=True, stop=True))
                mm = p.do("tensor", lambda e, h=h: e.matmul(
                    B["ps_c"][:, 128:256], qT[:, h * 128:(h + 1) * 128],
                    hSb[:, h * 128:(h + 1) * 128], start=True, stop=True))
                _bar(p)
                p.do("vector", lambda e: e.tensor_copy(bbc[:, :], B["ps_a"][:, 128:256]))
                p.do("vector", lambda e, h=h: e.tensor_scalar(
                    Eincl[:, :], B["ps_a"][:, 0:128], Gm[:, h:h + 1], None,
                    OP.subtract, OP.bypass))
                m = p.do("vector", lambda e: e.tensor_tensor(Eincl[:, :], Eincl[:, :],
                                                             UT1[:, :], OP.mult))
                p.wait("scalar", "vector", m)
                m = p.do("scalar", lambda e: e.activation(Eincl[:, :], Eincl[:, :], AF.Exp))
                p.wait("vector", "scalar", m)
                p.do("vector", lambda e: e.tensor_tensor(Eincl[:, :], Eincl[:, :],
                                                         UT1[:, :], OP.mult))
                p.do("vector", lambda e: e.tensor_tensor(Estr[:, :], Eincl[:, :],
                                                         UTs[:, :], OP.mult))
                p.do("vector", lambda e: e.tensor_tensor(Mqb[:, :], Eincl[:, :],
                                                         B["ps_b"][:, 128:256], OP.mult))
                p.do("vector", lambda e: e.tensor_tensor(Estr[:, :], Estr[:, :],
                                                         B["ps_b"][:, 0:128], OP.mult))
                p.do("vector", lambda e: e.tensor_tensor(Estr[:, :], Estr[:, :],
                                                         bbc[:, :], OP.mult))
                p.do("vector", lambda e: e.tensor_scalar(NTb[0][:, :], Estr[:, :], -1.0,
                                                         None, OP.mult, OP.bypass))
                p.do("vector", lambda e, h=h: e.tensor_scalar(
                    Estr[:, :], B["ps_a"][:, 0:128], Gm[:, h:h + 1], None,
                    OP.subtract, OP.bypass))
                p.do("vector", lambda e: e.tensor_scalar(Estr[:, :], Estr[:, :], -1.0,
                                                         None, OP.mult, OP.bypass))
                m = p.do("vector", lambda e: e.tensor_tensor(Estr[:, :], Estr[:, :],
                                                             LTS[:, :], OP.mult))
                p.wait("scalar", "vector", m)
                m = p.do("scalar", lambda e: e.activation(Estr[:, :], Estr[:, :], AF.Exp))
                p.wait("vector", "scalar", m)
                p.do("vector", lambda e: e.tensor_tensor(Estr[:, :], Estr[:, :],
                                                         LTS[:, :], OP.mult))
                p.do("vector", lambda e: e.tensor_tensor(Estr[:, :], Estr[:, :],
                                                         B["ps_b"][:, 0:128], OP.mult))
                p.do("vector", lambda e, h=h, btq=btq: e.tensor_scalar(
                    Estr[:, :], Estr[:, :], btq[:, h:h + 1], None, OP.mult, OP.bypass))
                p.do("vector", lambda e: e.tensor_scalar(Nb[0][:, :], Estr[:, :], -1.0,
                                                         None, OP.mult, OP.bypass))
                p.do("vector", lambda e, h=h: e.tensor_scalar(
                    wtile[:, :], B["ps_c"][:, 0:128], lam[:, h:h + 1], None,
                    OP.mult, OP.bypass))
                p.do("vector", lambda e, h=h: e.tensor_tensor(
                    Df[:, :], vtok[:, h * 128:(h + 1) * 128], wtile[:, :], OP.subtract))
                m = p.do("vector", lambda e, h=h, btq=btq: e.tensor_scalar(
                    Df[:, :], Df[:, :], btq[:, h:h + 1], None, OP.mult, OP.bypass))
                p.wait("tensor", "vector", m)
                for j in range(7):
                    cur = j % 2
                    nxt = 1 - cur
                    mm = p.do("tensor", lambda e, cur=cur: e.matmul(
                        B["ps_w"][:, 0:128], NTb[cur][:, :], Df[:, :],
                        start=True, stop=True))
                    if j < 6:
                        p.do("tensor", lambda e, cur=cur: e.matmul(
                            B["ps_w"][:, 128:256], NTb[cur][:, :], Nb[cur][:, :],
                            start=True, stop=True))
                        mm = p.do("tensor", lambda e, cur=cur: e.matmul(
                            B["ps_w"][:, 256:384], Nb[cur][:, :], NTb[cur][:, :],
                            start=True, stop=True))
                    p.wait("vector", "tensor", mm)
                    m = p.do("vector", lambda e: e.tensor_tensor(Df[:, :], Df[:, :],
                                                                B["ps_w"][:, 0:128], OP.add))
                    if j < 6:
                        p.do("vector", lambda e, nxt=nxt: e.tensor_copy(
                            Nb[nxt][:, :], B["ps_w"][:, 128:256]))
                        m = p.do("vector", lambda e, nxt=nxt: e.tensor_copy(
                            NTb[nxt][:, :], B["ps_w"][:, 256:384]))
                    p.wait("tensor", "vector", m)
                mm = p.do("tensor", lambda e: e.matmul(
                    B["ps_d"][:, 0:128], Mqb[:, :], Df[:, :], start=True, stop=True))
                p.wait("vector", "tensor", mm)
                p.do("vector", lambda e, h=h: e.tensor_scalar(
                    obuf[:, h * 128:(h + 1) * 128], B["ps_c"][:, 128:256],
                    lam[:, h:h + 1], None, OP.mult, OP.bypass))
                p.do("vector", lambda e, h=h: e.tensor_tensor(
                    obuf[:, h * 128:(h + 1) * 128], obuf[:, h * 128:(h + 1) * 128],
                    B["ps_d"][:, 0:128], OP.add))
                m = p.do("vector", lambda e, h=h: e.tensor_scalar(
                    dlb[:, :], Df[:, :], wmat[:, h:h + 1], None, OP.mult, OP.bypass))
                p.wait("tensor", "vector", m)
                mm = p.do("tensor", lambda e, h=h: e.matmul(
                    B["ps_d"][0:64, 128:256], ktok[:, h * 64:(h + 1) * 64], dlb[:, :],
                    start=True, stop=True))
                p.wait("vector", "tensor", mm)
                p.do("vector", lambda e, h=h: e.tensor_scalar(
                    hS[:, h * 128:(h + 1) * 128], hS[:, h * 128:(h + 1) * 128],
                    gLb[:, h:h + 1], None, OP.mult, OP.bypass))
                p.do("vector", lambda e, h=h: e.tensor_tensor(
                    hS[:, h * 128:(h + 1) * 128], hS[:, h * 128:(h + 1) * 128],
                    B["ps_d"][0:64, 128:256], OP.add))
                m = p.do("vector", lambda e, h=h: e.tensor_copy(
                    hSb[:, h * 128:(h + 1) * 128], hS[:, h * 128:(h + 1) * 128]))
                p.wait("tensor", "vector", m)
            _bar(p)
            if DEBUG_TAPS and sc == 0 and qi == 0:
                p.wait("sync", "vector", p.mark("vector"))
                p.wait("sync", "scalar", p.mark("scalar"))
                for nm, b_ in (("obuf", obuf), ("hS0", hS)):
                    db = p.dram("dbg_" + nm, list(b_.shape), b_.ap().dtype,
                                kind="ExternalOutput")
                    dmt = p.dma("sync", db[:, :], b_[:, :], "l2o")
                p.waitg("vector", "l2o", dmt)
                p.waitg("scalar", "l2o", dmt)
                _bar(p)
            p.do("vector", lambda e: e.tensor_tensor(sq[:, 0:D], obuf[:, 0:D],
                                                     obuf[:, 0:D], OP.mult))
            for h in range(8):
                p.do("vector", lambda e, h=h: e.tensor_reduce(
                    nrm[:, h:h + 1], sq[:, h * 128:(h + 1) * 128], AX.X, OP.add))
            p.do("vector", lambda e: e.tensor_tensor(sq[:, 0:512], obuf[:, D:VDIM],
                                                     obuf[:, D:VDIM], OP.mult))
            for h in range(4):
                p.do("vector", lambda e, h=h: e.tensor_reduce(
                    nrm[:, 8 + h:9 + h], sq[:, h * 128:(h + 1) * 128], AX.X, OP.add))
            m = p.do("vector", lambda e: e.tensor_scalar(nrm[:, 0:12], nrm[:, 0:12],
                                                         1.0 / DV, 1e-5, OP.mult, OP.add))
            p.wait("scalar", "vector", m)
            p.do("scalar", lambda e: e.activation(nrm[:, 0:12], nrm[:, 0:12], AF.Ln))
            m = p.do("scalar", lambda e: e.activation(nrm[:, 0:12], nrm[:, 0:12],
                                                      AF.Exp, scale=-0.5))
            p.wait("vector", "scalar", m)
            for h in range(NH):
                p.do("vector", lambda e, h=h: e.tensor_scalar(
                    obuf[:, h * 128:(h + 1) * 128], obuf[:, h * 128:(h + 1) * 128],
                    nrm[:, h:h + 1], None, OP.mult, OP.bypass))
                p.do("vector", lambda e, h=h: e.tensor_tensor(
                    obuf[:, h * 128:(h + 1) * 128], obuf[:, h * 128:(h + 1) * 128],
                    gnb[:, :], OP.mult))
            _bar(p)
            tok0 = sc * SCW + qi * 128
            for j in range(3):
                mm = 0
                for c in range(KD):
                    mm = p.do("tensor", lambda e, c=c, j=j, tok0=tok0: e.matmul(
                        psl[j][:, 0:512], xnT[:, c * T + tok0: c * T + tok0 + 128],
                        WG[:, c * VDIM + j * 512: c * VDIM + (j + 1) * 512],
                        start=(c == 0), stop=(c == KD - 1)))
            _bar(p)
            for j in range(3):
                m = p.do("scalar", lambda e, j=j: e.activation(sq[:, 0:512],
                                                               psl[j][:, 0:512], AF.Silu))
                p.wait("vector", "scalar", m)
                m = p.do("vector", lambda e, j=j: e.tensor_tensor(
                    obuf[:, j * 512:(j + 1) * 512], obuf[:, j * 512:(j + 1) * 512],
                    sq[:, 0:512], OP.mult))
                p.wait("scalar", "vector", m)
            _bar(p)
            cps = []
            for i in range(NH):
                slot = B["ps_a"] if i % 2 == 0 else B["ps_b"]
                if i >= 2:
                    p.wait("tensor", "vector", cps[i - 2])
                p.do("tensor", lambda e, i=i, slot=slot: e.transpose(
                    slot[:, 0:128], obuf[:, i * 128:(i + 1) * 128], B["ident32"][:, :]))
                p.wait("vector", "tensor", p.mark("tensor"))
                m = p.do("vector", lambda e, i=i, slot=slot: e.tensor_copy(
                    oT[:, i * 128:(i + 1) * 128], slot[:, 0:128]))
                cps.append(m)
            _bar(p)
            for hf in range(2):
                for i in range(NH):
                    p.do("tensor", lambda e, i=i, hf=hf: e.matmul(
                        B["ps_w"][:, hf * 512:(hf + 1) * 512], oT[:, i * 128:(i + 1) * 128],
                        WO[:, i * 1024 + hf * 512: i * 1024 + (hf + 1) * 512],
                        start=(i == 0), stop=(i == NH - 1)))
            _bar(p)
            p.waitg("vector", "l2x", xdm)
            for hf in range(2):
                p.do("vector", lambda e, hf=hf: e.tensor_tensor(
                    xtile[:, hf * 512:(hf + 1) * 512], xtile[:, hf * 512:(hf + 1) * 512],
                    B["ps_w"][:, hf * 512:(hf + 1) * 512], OP.add))
            m = p.mark("vector")
            p.wait("sync", "vector", m)
            sm = p.dma("sync", x_out.rearrange("(t p) d -> p t d", p=128)[:, q, :],
                       xtile[:, :], "l2o")
            store_marks.append(sm)
            _bar(p)
    return ("sync", store_marks[-1])


def _l2a_cw(cw):
    C = cw.shape[0]
    nt = C // 128
    return np.ascontiguousarray(cw.reshape(nt, 128, 4).transpose(1, 0, 2).reshape(128, nt * 4))


def _l2a_wspec():
    return [("wq", "L2_attn_Wq", None), ("wk", "L2_attn_Wk", None),
            ("wv", "L2_attn_Wv", None), ("wg", "L2_attn_Wg", None),
            ("wo", "L2_attn_Wo", None), ("wgk", "L2_attn_Wgk", None),
            ("wb", "L2_attn_Wb", None),
            ("cwq", "L2_attn_conv_q", _l2a_cw), ("cwk", "L2_attn_conv_k", _l2a_cw),
            ("cwv", "L2_attn_conv_v", _l2a_cw),
            ("alog", "L2_attn_A_log", _row), ("dtb", "L2_attn_dt_bias", _row),
            ("bb", "L2_attn_b_b", _row), ("gnw", "L2_attn_g_norm_weight", _row)]



def _tr_last2(a):
    return np.ascontiguousarray(np.swapaxes(a, -1, -2))


WSPECS = {
    "l1f": [("gate_w", "L1_ffn_gate_w", None), ("gate_b", "L1_ffn_gate_b", None),
            ("expert_w", "L1_ffn_expert_w", _tr_last2), ("expert_b", "L1_ffn_expert_b", None)],
    "l0a": [("W_DQ", "L0_attn_W_DQ", None), ("b_DQ", "L0_attn_b_DQ", None),
            ("W_UQ", "L0_attn_W_UQ", None), ("b_UQ", "L0_attn_b_UQ", None),
            ("W_DKV", "L0_attn_W_DKV", None), ("b_DKV", "L0_attn_b_DKV", None),
            ("W_UK", "L0_attn_W_UK", None), ("b_UK", "L0_attn_b_UK", None),
            ("W_UV", "L0_attn_W_UV", None), ("b_UV", "L0_attn_b_UV", None),
            ("W_QR", "L0_attn_W_QR", None), ("b_QR", "L0_attn_b_QR", None),
            ("W_KR", "L0_attn_W_KR", None), ("b_KR", "L0_attn_b_KR", None),
            ("W_O", "L0_attn_W_O", None), ("b_O", "L0_attn_b_O", None)],
    "l1a": [("past_kv", "L1_attn_past_kv", None), ("W_qkv", "L1_attn_W_qkv", None),
            ("gn_weight_full", "L1_attn_gn_weight", lambda a: np.ascontiguousarray(np.repeat(a, 64))),
            ("gn_bias_full", "L1_attn_gn_bias", lambda a: np.ascontiguousarray(np.repeat(a, 64)))],
    "l0f": [("router_w", "L0_ffn_router_w", None), ("router_b", "L0_ffn_router_b", None),
            ("w_gate", "L0_ffn_w_gate", None), ("w_up", "L0_ffn_w_up", None),
            ("w_down", "L0_ffn_w_down", None), ("sw_gate", "L0_ffn_sw_gate", None),
            ("sw_up", "L0_ffn_sw_up", None), ("sw_down", "L0_ffn_sw_down", None)],
    "l3a": None,
    "l2a": None,
    "l3f": [("w_gate", "L3_ffn_w_gate", None), ("w_up", "L3_ffn_w_up", None),
            ("w_down", "L3_ffn_w_down", None)],
    "l2f": [("router_w", "L2_ffn_router_w", _tr_last2), ("router_bias", "L2_ffn_router_bias", None),
            ("up_proj", "L2_ffn_up_proj", None), ("gate_proj", "L2_ffn_gate_proj", None),
            ("down_proj", "L2_ffn_down_proj", None),
            ("sh_gate", "L2_ffn_sh_gate", _tr_last2), ("sh_up", "L2_ffn_sh_up", _tr_last2),
            ("sh_down", "L2_ffn_sh_down", _tr_last2)],
}

EMITTERS = {}



def _l3a_winx(win):
    out = np.zeros((1024, 2304), np.float32)
    out[:, 0:2048] = win[:, 2048:4096]
    out[:, 2048:2112] = win[:, 4096:4160]
    out[:, 2176:2240] = win[:, 4160:4224]
    return out


def _l3a_cw(cw):
    pad = np.zeros((2304, 4), np.float32)
    pad[0:2048] = cw[:, 0, 0:2048].T
    pad[2048:2112] = cw[:, 0, 2048:2112].T
    pad[2176:2240] = cw[:, 0, 2112:2176].T
    return np.ascontiguousarray(pad.reshape(18, 128, 4).transpose(1, 0, 2).reshape(128, 72))


def _l3a_cb(cb):
    pad = np.zeros((2304,), np.float32)
    pad[0:2048] = cb[0:2048]
    pad[2048:2112] = cb[2048:2112]
    pad[2176:2240] = cb[2112:2176]
    return np.ascontiguousarray(pad.reshape(18, 128).T)


def _row(a):
    return np.ascontiguousarray(np.asarray(a)[None, :])


def _l3a_wspec():
    sp = [("woA", "L3_attn_Wo", lambda a: np.ascontiguousarray(a[0:1024])),
          ("woB", "L3_attn_Wo", lambda a: np.ascontiguousarray(a[1024:2048])),
          ("bo", "L3_attn_bo", _row)]
    for pf, ik in (("f", "f"), ("b", "b")):
        sp += [("winx" + pf, f"L3_attn_{ik}Win", _l3a_winx),
               ("winz" + pf, f"L3_attn_{ik}Win", lambda a: np.ascontiguousarray(a[:, 0:2048])),
               ("windt" + pf, f"L3_attn_{ik}Win", lambda a: np.ascontiguousarray(a[:, 4224:4256])),
               ("cw" + pf, f"L3_attn_{ik}convw", _l3a_cw),
               ("cb" + pf, f"L3_attn_{ik}convb", _l3a_cb),
               ("dtb" + pf, f"L3_attn_{ik}dtb", _row),
               ("alog" + pf, f"L3_attn_{ik}Alog", _row),
               ("D" + pf, f"L3_attn_{ik}D", lambda a: _row(np.repeat(a, 64))),
               ("nw" + pf, f"L3_attn_{ik}normw", _row),
               ("woutT" + pf, f"L3_attn_{ik}Wout", _tr_last2)]
    return sp


def get_wspec(name):
    sp = WSPECS.get(name)
    if sp is None:
        sp = globals()[WSPECS_LAZY[name]]()
        WSPECS[name] = sp
    return sp


def build_sublayer_prog(name, coef, inputs):
    p = Prog()
    x_in = p.dram("x_in", [T, D], F32, kind="ExternalInput").ap()
    x_out = p.dram("x_out", [T, D], F32, kind="ExternalOutput").ap()
    W = {}
    for wname, key, tf in get_wspec(name):
        arr = np.asarray(inputs[key], dtype=np.float32)
        shape = list(tf(arr).shape) if tf else list(arr.shape)
        W[wname] = p.dram(wname, shape, F32, kind="ExternalInput").ap()
    if name in RAW_EMITTERS:
        B = make_common(p, small=True)
        fin_eng, fin_mark = RAW_EMITTERS[name](p, B, W, coef, x_in, x_out)
        p.wait("sync", fin_eng, fin_mark) if fin_eng != "sync" else None
        grp = "l3o" if name == "l3a" else "l2o"
        if DEBUG_TAPS:
            for nm, buf in p.dbg.items():
                dt_out = p.dram("dbg_" + nm, list(buf.shape), buf.ap().dtype,
                                kind="ExternalOutput")
                fin_mark = p.dma("sync", dt_out[:, :], buf[:, :], grp)
        p.waitg("sync", grp, fin_mark)
        return p.finalize()
    B = make_common(p)
    lm = emit_load_x(p, B, x_in)
    p.waitg("vector", "xio", lm)
    p.waitg("scalar", "xio", lm)
    p.waitg("tensor", "xio", lm)
    fin_eng, fin_mark = EMITTERS[name](p, B, W, coef)
    p.wait("sync", fin_eng, fin_mark)
    sm = emit_store_x(p, B, x_out)
    if DEBUG_TAPS:
        for nm, buf in p.dbg.items():
            dt_out = p.dram("dbg_" + nm, list(buf.shape), buf.ap().dtype, kind="ExternalOutput")
            sm = p.dma("sync", dt_out[:, :], buf[:, :], "xio")
    p.waitg("sync", "xio", sm)
    return p.finalize()


_PROG_CACHE = {}


def run_sublayer(name, coef, xs, inputs):
    key = (name, coef, DEBUG_TAPS)
    if key not in _PROG_CACHE:
        _PROG_CACHE[key] = build_sublayer_prog(name, coef, inputs)
    nc = _PROG_CACHE[key]
    wmap = {}
    for wname, ikey, tf in get_wspec(name):
        a = np.asarray(inputs[ikey], dtype=np.float32)
        wmap[wname] = tf(a) if tf else np.ascontiguousarray(a)
    in_maps = [{"x_in": xs[b], **wmap} for b in range(NCORE)]
    res = run_bass_kernel_spmd(nc, in_maps, core_ids=list(range(NCORE)))
    if DEBUG_TAPS:
        return res
    return [np.asarray(res.results[b]["x_out"]) for b in range(NCORE)]


def _split_batch(x):
    x = np.asarray(x, dtype=np.float32)
    return [np.ascontiguousarray(x[b]) for b in range(NCORE)]




FUSE_SEGMENTS = [
    ("b", [("l0f", 0.5), ("l0a", 1.0), ("l0f", 0.5), ("l1f", 0.5), ("l1a", 1.0),
           ("l1f", 0.5), ("l2f", 0.5)]),
    ("r", [("l2a", 1.0)]),
    ("b", [("l2f", 0.5), ("l3f", 0.5)]),
    ("r", [("l3a", 1.0)]),
    ("b", [("l3f", 0.5)]),
]


def build_fused_prog(inputs):
    p = Prog()
    x_in = p.dram("x_in", [T, D], F32, kind="ExternalInput").ap()
    x_out = p.dram("x_out", [T, D], F32, kind="ExternalOutput").ap()
    Wall = {}
    for name in set(n for _, subs in FUSE_SEGMENTS for n, _ in subs):
        W = {}
        for wname, key, tf in get_wspec(name):
            arr = np.asarray(inputs[key], dtype=np.float32)
            shape = list(tf(arr).shape) if tf else list(arr.shape)
            W[wname] = p.dram(f"{name}_{wname}", shape, F32, kind="ExternalInput").ap()
        Wall[name] = W
    xbufs = [p.dram(f"xh{i}", [T, D], F32).ap() for i in range(len(FUSE_SEGMENTS) - 1)]
    srcs = [x_in] + xbufs
    dsts = xbufs + [x_out]
    Bbase = make_common(p, small=True)
    si = 0
    last_store = 0
    for gi, (kind, subs) in enumerate(FUSE_SEGMENTS):
        xsrc, xdst = srcs[gi], dsts[gi]
        if kind == "b":
            with p.scope(f"g{gi}_"):
                B = dict(Bbase)
                make_seg_bufs(p, B)
                lm = emit_load_x(p, B, xsrc)
                p.waitg("vector", "xio", lm)
                p.waitg("scalar", "xio", lm)
                p.waitg("tensor", "xio", lm)
                for name, coef in subs:
                    with p.scope(f"s{si}_"):
                        fin_eng, fin_mark = EMITTERS[name](p, B, Wall[name], coef)
                        p.join()
                    si += 1
                p.wait("sync", "vector", p.mark("vector"))
                last_store = emit_store_x(p, B, xdst)
                for e in Prog.ENGS:
                    p.waitg(e, "xio", last_store)
                p.join()
        else:
            name, coef = subs[0]
            with p.scope(f"s{si}_"):
                fin_eng, fin_mark = RAW_EMITTERS[name](p, Bbase, Wall[name], coef,
                                                       xsrc, xdst)
                last_store = fin_mark
                p.join()
            si += 1
    p.waitg("sync", "xio", last_store) if False else None
    return p.finalize()


_FUSED_CACHE = {}


def kernel_fused(inputs):
    if "prog" not in _FUSED_CACHE:
        _FUSED_CACHE["prog"] = build_fused_prog(inputs)
    nc = _FUSED_CACHE["prog"]
    wmap = {}
    for name in set(n for _, subs in FUSE_SEGMENTS for n, _ in subs):
        for wname, ikey, tf in get_wspec(name):
            a = np.asarray(inputs[ikey], dtype=np.float32)
            wmap[f"{name}_{wname}"] = tf(a) if tf else np.ascontiguousarray(a)
    xs = _split_batch(inputs["L0_attn_h_t"])
    in_maps = [{"x_in": xs[b], **wmap} for b in range(NCORE)]
    res = run_bass_kernel_spmd(nc, in_maps, core_ids=list(range(NCORE)))
    return np.stack([np.asarray(res.results[b]["x_out"]) for b in range(NCORE)], 0)


SUBLAYERS = [("l0f", 0.5), ("l0a", 1.0), ("l0f", 0.5),
             ("l1f", 0.5), ("l1a", 1.0), ("l1f", 0.5),
             ("l2f", 0.5), ("l2a", 1.0), ("l2f", 0.5),
             ("l3f", 0.5), ("l3a", 1.0), ("l3f", 0.5)]


FUSED = True


def kernel(**inputs):
    if FUSED:
        return kernel_fused(inputs)
    xs = _split_batch(inputs["L0_attn_h_t"])
    for name, coef in SUBLAYERS:
        xs = run_sublayer(name, coef, xs, inputs)
    return np.stack(xs, 0)


EMITTERS["l1f"] = emit_l1_ffn
EMITTERS["l2f"] = emit_l2_ffn
EMITTERS["l0a"] = emit_l0_attn
EMITTERS["l1a"] = emit_l1_attn
EMITTERS["l3f"] = emit_l3_ffn
EMITTERS["l0f"] = emit_l0_ffn
RAW_EMITTERS = {"l3a": emit_l3_attn, "l2a": emit_l2_attn}
WSPECS_LAZY = {"l3a": "_l3a_wspec", "l2a": "_l2a_wspec"}
